# version 03
# baseline (speedup 1.0000x reference)
import sys
import numpy as np
import ml_dtypes
from contextlib import ExitStack

import concourse.bass as bass
import concourse.mybir as mybir
from concourse.bass_utils import run_bass_kernel_spmd

F32 = mybir.dt.float32
BF16 = mybir.dt.bfloat16
FP16 = mybir.dt.float16
AF = mybir.ActivationFunctionType
ALU = mybir.AluOpType
AXX = mybir.AxisListType.X

D = 1024
DI = 2048
SEQ = 2048
NSEQ = 2
NST = 16
DTR = 64
LC = 512
NCH = SEQ // LC
NSUB = LC // 128
FC = D // 128
DC = DI // 128
TS = SEQ // 128
EPS = 1e-5
UOF = 8
UW = UOF + 512

ENGS = ("tensor", "vector", "scalar", "gpsimd", "sync")
RELAXED = False


class Planner:

    def __init__(self):
        self.ops = []
        self.bufs = {}
        self.groups = {}
        self.nbar = 0
        self.last_eng = {}

    def _b(self, k):
        b = self.bufs.get(k)
        if b is None:
            b = {"w": None, "r": []}
            self.bufs[k] = b
        return b

    def _g(self, name):
        g = self.groups.get(name)
        if g is None:
            g = {"gen": 0, "count": 0, "bounds": [], "pending": set(), "last": None}
            self.groups[name] = g
        return g

    def add(self, eng, fn, r=(), w=(), grp=None, extra=(), bar_inc=False,
            bar_wait=None, async_out=False):
        i = len(self.ops)
        deps = set(extra)
        sdeps = set()
        for k in r:
            b = self._b(k)
            if b["w"] is not None:
                deps.add(b["w"])
        for k in w:
            b = self._b(k)
            if b["w"] is not None:
                sdeps.add(b["w"])
            sdeps.update(b["r"])
        deps.discard(i)
        sdeps.discard(i)
        deps -= sdeps
        gw = {}
        eng_deps = set()
        for j in deps | sdeps:
            oj = self.ops[j]
            if oj["grp"] is not None:
                gname, gen = oj["grp"]
                gw[gname] = max(gw.get(gname, -1), gen)
                gg = self._g(gname)
                if gen == gg["gen"]:
                    gg["pending"].add(i)
            else:
                eng_deps.add(j)
        gtag = None
        if grp is not None:
            g = self._g(grp)
            pend = {j for j in g["pending"] if j != i}
            if pend:
                g["bounds"].append(g["count"])
                g["gen"] += 1
                eng_deps.update(j for j in pend if self.ops[j]["grp"] is None)
                for j in pend:
                    oj = self.ops[j]
                    if oj["grp"] is not None:
                        gn, gg = oj["grp"]
                        gw[gn] = max(gw.get(gn, -1), gg)
                g["pending"] = set()
            g["count"] += 1
            g["last"] = i
            gtag = (grp, g["gen"])
        self.ops.append({"eng": eng, "fn": fn, "grp": gtag, "deps": eng_deps,
                         "gw": gw, "async_out": async_out,
                         "bar_inc": bar_inc, "bar_wait": bar_wait})
        for k in r:
            self._b(k)["r"].append(i)
        for k in w:
            b = self._b(k)
            b["w"] = i
            b["r"] = []
        if gtag is None and not bar_inc and bar_wait is None:
            self.last_eng[eng] = i
        return i

    def barrier(self):
        lasts = [i for i in self.last_eng.values()]
        last_dmas = [g["last"] for g in self.groups.values() if g["last"] is not None]
        for g in self.groups.values():
            done = g["bounds"][-1] if g["bounds"] else 0
            if g["count"] > done:
                g["bounds"].append(g["count"])
                g["gen"] += 1
        self.add("sync", lambda eng: eng.nop(), extra=lasts + last_dmas,
                 bar_inc=True)
        self.nbar += 1
        for e in ENGS:
            if e != "sync":
                self.add(e, lambda eng: eng.nop(), bar_wait=self.nbar)

    def finalize(self):
        n = len(self.ops)
        for g in self.groups.values():
            g["bounds"].append(g["count"])
        if RELAXED:
            for o in self.ops:
                if o["grp"] is None:
                    o["deps"] = {j for j in o["deps"]
                                 if self.ops[j]["eng"] != o["eng"]
                                 or self.ops[j]["grp"] is not None
                                 or self.ops[j]["async_out"]}
            need = [False] * n
            for o in self.ops:
                for j in o["deps"]:
                    need[j] = True
        else:
            need = [o["grp"] is None and not o["bar_inc"] and o["bar_wait"] is None
                    for o in self.ops]
        cum = {}
        val = [0] * n
        for i, o in enumerate(self.ops):
            if o["grp"] is not None:
                continue
            if need[i]:
                s = "e_" + o["eng"]
                cum[s] = cum.get(s, 0) + 1
                val[i] = cum[s]

        def gen_threshold(gname, gen):
            b = self.groups[gname]["bounds"]
            k = min(gen, len(b) - 1)
            return 16 * b[k]

        sched = {k: [] for k in ENGS}
        floor = {e: {} for e in ENGS}
        for i, o in enumerate(self.ops):
            wmap = {}
            for j in o["deps"]:
                if val[j] > 0:
                    s = "e_" + self.ops[j]["eng"]
                    wmap[s] = max(wmap.get(s, 0), val[j])
            for gname, gen in o["gw"].items():
                s = "g_" + gname
                wmap[s] = max(wmap.get(s, 0), gen_threshold(gname, gen))
            if o["bar_wait"] is not None:
                wmap["bar"] = o["bar_wait"]
            if o["bar_inc"]:
                inc = ("bar", 1)
                nbc = self.bar_cnt = getattr(self, "bar_cnt", 0) + 1
                if nbc > 1:
                    wmap["bar"] = max(wmap.get("bar", 0), nbc - 1)
            elif o["grp"] is not None:
                inc = ("g_" + o["grp"][0], 16)
            elif need[i]:
                inc = ("e_" + o["eng"], 1)
                if not RELAXED:
                    s = inc[0]
                    if val[i] > 1:
                        wmap[s] = max(wmap.get(s, 0), val[i] - 1)
            else:
                inc = None
            fl = floor[o["eng"]]
            wl = []
            for s, v in sorted(wmap.items()):
                if fl.get(s, -1) < v:
                    fl[s] = v
                    wl.append((s, v))
            sched[o["eng"]].append((o["fn"], wl, inc))
        return sched, cum


class Builder:
    def __init__(self):
        self.nc = bass.Bass("TRN2", target_bir_lowering=False,
                            detect_race_conditions=not RELAXED)
        self.pl = Planner()
        self.ctx = ExitStack()
        self.arena_off = 0

    def sb(self, name, shape, dt):
        return self.ctx.enter_context(self.nc.sbuf_tensor(name, shape, dt))

    def view(self, dt, cols):
        esz = 2 if dt in (BF16, FP16) else 4
        off = (self.arena_off + 63) & ~63
        self.arena_off = off + cols * esz
        assert self.arena_off <= self.ARENA_BYTES, \
            f"arena overflow: {self.arena_off} > {self.ARENA_BYTES}"
        base = self.arena.ap() if dt == BF16 else self.arena.ap().bitcast(dt)
        return base[:, off // esz: off // esz + cols]

    def reset_arena(self):
        self.arena_off = 0

    def dma(self, q, out, in_, grp, r=(), w=()):
        self.pl.add(q, lambda e, o=out, i=in_: e.dma_start(out=o, in_=i),
                    r=r, w=w, grp=grp)

    def dmat(self, out, in_, r=(), w=()):
        self.pl.add("scalar",
                    lambda e, o=out, i=in_: e.dma_start_transpose(out=o, in_=i),
                    r=r, w=w, grp="tp")

    def mm(self, out, lhsT, rhs, start, stop, r=(), w=()):
        self.pl.add("tensor",
                    lambda e, o=out, l=lhsT, rr=rhs, s1=start, s2=stop:
                    e.matmul(o, l, rr, start=s1, stop=s2), r=r, w=w)

    def tpose(self, out, in_, r=(), w=()):
        self.pl.add("tensor",
                    lambda e, o=out, i=in_, d=self.IDEN[:]: e.transpose(o, i, d),
                    r=r, w=w)

    def act(self, out, in_, func, bias=0.0, scale=1.0, accum=None, r=(), w=()):
        self.pl.add("scalar",
                    lambda e, o=out, i=in_, f=func, b=bias, s=scale, a=accum:
                    e.activation(out=o, in_=i, func=f, bias=b, scale=s, accum_out=a),
                    r=r, w=w, async_out=accum is not None)

    def tt(self, out, a, b, op, r=(), w=()):
        self.pl.add("vector",
                    lambda e, o=out, x=a, y=b, p=op:
                    e.tensor_tensor(out=o, in0=x, in1=y, op=p), r=r, w=w)

    def ts(self, out, a, s1, s2, op0, op1, accum=None, r=(), w=()):
        if s2 is None and accum is None:
            self.pl.add("vector",
                        lambda e, o=out, x=a, u=s1, p0=op0:
                        e.tensor_scalar(out=o, in0=x, scalar1=u, scalar2=None, op0=p0),
                        r=r, w=w)
        else:
            self.pl.add("vector",
                        lambda e, o=out, x=a, u=s1, v=s2, p0=op0, p1=op1, ac=accum:
                        e.tensor_scalar(out=o, in0=x, scalar1=u, scalar2=v,
                                        op0=p0, op1=p1, accum_out=ac), r=r, w=w,
                        async_out=accum is not None)

    def stt(self, out, in0, scal, in1, op0, op1, r=(), w=()):
        self.pl.add("vector",
                    lambda e, o=out, x=in0, u=scal, y=in1, p0=op0, p1=op1:
                    e.scalar_tensor_tensor(out=o, in0=x, scalar=u, in1=y,
                                           op0=p0, op1=p1), r=r, w=w)

    def scan(self, out, a, b, init, r=(), w=()):
        self.pl.add("vector",
                    lambda e, o=out, x=a, y=b, h0=init:
                    e.tensor_tensor_scan(out=o, data0=x, data1=y, initial=h0,
                                         op0=ALU.mult, op1=ALU.add), r=r, w=w)

    def vcopy(self, out, in_, r=(), w=()):
        self.pl.add("vector", lambda e, o=out, i=in_: e.tensor_copy(out=o, in_=i),
                    r=r, w=w)

    def gcopy(self, out, in_, r=(), w=()):
        self.pl.add("gpsimd", lambda e, o=out, i=in_: e.tensor_copy(out=o, in_=i),
                    r=r, w=w)

    def vmemset(self, ap, val, r=(), w=()):
        self.pl.add("vector", lambda e, a=ap, v=val: e.memset(a, v), r=r, w=w)

    def rmax(self, out, in_, negate, r=(), w=()):
        self.pl.add("vector",
                    lambda e, o=out, i=in_, ng=negate:
                    e.reduce_max(out=o, in_=i, axis=AXX, negate=ng), r=r, w=w)

    def rsumf(self, out, in_, r=(), w=()):
        self.pl.add("vector",
                    lambda e, o=out, i=in_: e.reduce_sum(out=o, in_=i, axis=AXX),
                    r=r, w=w)

    def recip(self, out, in_, r=(), w=()):
        self.pl.add("vector", lambda e, o=out, i=in_: e.reciprocal(out=o, in_=i),
                    r=r, w=w)

    def build(self):
        nc = self.nc
        self.x_in = nc.dram_tensor("x", [NSEQ, SEQ, D], F32, kind="ExternalInput").ap()
        self.out = nc.dram_tensor("out", [NSEQ, SEQ, D], F32, kind="ExternalOutput").ap()
        f32in = lambda n, sh: nc.dram_tensor(n, sh, F32, kind="ExternalInput").ap()
        bf16in = lambda n, sh: nc.dram_tensor(n, sh, BF16, kind="ExternalInput").ap()
        self.w_in = f32in("w_in", [2 * DC, 128, 8 * 128])
        self.conv_w_r = f32in("conv_w_r", [128, 64])
        self.conv_b_r = f32in("conv_b_r", [128, 16])
        self.b_dt_r = f32in("b_dt_r", [128, 16])
        self.d_r = f32in("d_r", [128, 16])
        self.a_log_r = f32in("a_log_r", [128, 256])
        self.w_x = f32in("w_x", [128, DC * 96])
        self.w_dt = f32in("w_dt", [DTR, DI])
        self.w_out = f32in("w_out", [DI, D])
        self.w0t = f32in("w0t", [D, D])
        self.gatet = f32in("gatet", [128, FC * 8])
        self.ut = f32in("ut", [128, FC * 128])
        self.v_r = f32in("v_r", [128, D])
        self.wrt = f32in("wrt", [D, D])
        self.wkt = f32in("wkt", [FC, 128, FC * 128])
        self.wvt = f32in("wvt", [D, D])
        self.mur_r = f32in("mur_r", [128, 8])
        self.muk_r = f32in("muk_r", [128, 8])
        self.selb = bf16in("selb", [32, NST * 128])
        self.selc = bf16in("selc", [32, NST * 128])
        self.rep8 = bf16in("rep8", [8, 128])
        self.ones1 = bf16in("ones1", [128, 1])
        self.iden = bf16in("iden", [128, 128])

        self.win_p = nc.dram_tensor("win_p", [2 * DC, 128, 8 * 128], BF16).ap()
        self.wout_p = nc.dram_tensor("wout_p", [DI, D], BF16).ap()
        self.w0t_p = nc.dram_tensor("w0t_p", [D, D], BF16).ap()
        self.wrt_p = nc.dram_tensor("wrt_p", [D, D], BF16).ap()
        self.wkt_p = nc.dram_tensor("wkt_p", [FC, 128, FC * 128], BF16).ap()
        self.wvt_p = nc.dram_tensor("wvt_p", [D, D], BF16).ap()
        self.xres_h = nc.dram_tensor("xres_h", [NSEQ, SEQ, D], F32).ap()

        self.ARENA_BYTES = 164 * 1024
        self.arena = self.sb("arena", [128, self.ARENA_BYTES // 2], BF16)
        self.WSTR = self.sb("wstr", [128, 4 * 1024], BF16)
        self.WX = self.sb("wxr", [128, DC * 96], BF16)
        self.WDT = self.sb("wdtr", [DTR, DI], BF16)
        self.UTS = self.sb("uts", [128, FC * 128], BF16)
        self.VSB = self.sb("vsb", [128, D], BF16)
        self.GATET = self.sb("gatetr", [128, FC * 8], BF16)
        self.SELB = self.sb("selbr", [128, NST * 128], BF16)
        self.SELC = self.sb("selcr", [128, NST * 128], BF16)
        self.REP8 = self.sb("rep8r", [8, 128], BF16)
        self.ONES1 = self.sb("ones1r", [128, 1], BF16)
        self.IDEN = self.sb("idenr", [128, 128], BF16)
        self.CONVW = self.sb("convw", [128, 64], F32)
        self.CONVB = self.sb("convb", [128, 16], F32)
        self.BDT = self.sb("bdt", [128, 16], F32)
        self.DCOL = self.sb("dcol", [128, 16], F32)
        self.ANEG = self.sb("aneg", [128, 256], F32)
        self.MUR = self.sb("mur", [128, 8], F32)
        self.MUK = self.sb("muk", [128, 8], F32)
        self.HST = self.sb("hst", [128, 512], F32)
        def stats_set(tag):
            t = lambda nm, c: self.sb(tag + nm, [128, c], F32)
            return {"pf": tag, "s1": t("s1", 16), "s2": t("s2", 16),
                    "negm": t("nm", 16), "rstd": t("rs", 16),
                    "sa": t("sa", 16), "sb": t("sb", 16),
                    "sv": t("sv", 16), "sw": t("sw", 16),
                    "aux": t("ax", 16), "aux2": t("a2", 16),
                    "dv": self.sb(tag + "dv", [128, 1024], BF16),
                    "da": self.sb(tag + "da", [128, 1024], BF16)}
        self.STA = stats_set("LA")
        self.STB = stats_set("LB")
        self.RINV = self.sb("rinv", [128, 4], F32)
        self.NM = self.sb("nm", [128, 1], F32)
        self.GE = self.sb("ge", [128, 8], F32)
        self.GS = self.sb("gs", [128, 2], F32)
        self.GSUB = self.sb("gsub", [128, 8], BF16)
        self.GFM = self.sb("gfm", [8, LC], BF16)
        self.FCONST = self.sb("fconst", [128, 2], F32)
        self.nc.const_aps.register(F32, EPS, self.FCONST[:, 0:1])
        self.nc.const_aps.register(F32, 1.0, self.FCONST[:, 1:2])
        self.PS = [self.ctx.enter_context(self.nc.psum_tensor(f"ps{i}", [128, 512], F32))
                   for i in range(8)]

        self.prep()
        self.pl.barrier()
        self.reset_arena()
        self.phaseA_merged()
        self.pl.barrier()
        self.reset_arena()
        self.phaseB_merged()
        for s in range(NSEQ):
            self.pl.barrier()
            self.reset_arena()
            self.phaseC(s)
            self.pl.barrier()
            self.reset_arena()
            self.phaseD(s)

        self.pl.add("sync", lambda e: e.nop(),
                    r=[("out", s, g) for s in range(NSEQ) for g in range(TS)])

        sched, cum = self.pl.finalize()
        sems = {}
        names = ["e_tensor", "e_vector", "e_scalar", "e_gpsimd", "e_sync", "bar"]
        names += ["g_" + g for g in self.pl.groups]
        for name in names:
            sems[name] = self.ctx.enter_context(nc.semaphore(name))

        def body(eng_obj, items):
            for fn, wl, inc in items:
                for sname, v in wl:
                    eng_obj.wait_ge(sems[sname], v)
                inst = fn(eng_obj)
                if inc is not None:
                    inst.then_inc(sems[inc[0]], inc[1])

        with nc.Block() as blk:
            blk.tensor(lambda e: body(e, sched["tensor"]))
            blk.vector(lambda e: body(e, sched["vector"]))
            blk.scalar(lambda e: body(e, sched["scalar"]))
            blk.gpsimd(lambda e: body(e, sched["gpsimd"]))
            blk.sync(lambda e: body(e, sched["sync"]))
        self.ctx.close()
        return nc

    def prep(self):
        self.vmemset(self.FCONST[:, 0:1], EPS, w=[("fconst",)])
        self.vmemset(self.FCONST[:, 1:2], 1.0, w=[("fconst",)])
        for dst, src, key in [(self.win_p, self.w_in, "win_p"),
                              (self.wout_p, self.w_out, "wout_p"),
                              (self.w0t_p, self.w0t, "w0t_p"),
                              (self.wrt_p, self.wrt, "wrt_p"),
                              (self.wkt_p, self.wkt, "wkt_p"),
                              (self.wvt_p, self.wvt, "wvt_p")]:
            rows = src.shape[0]
            q = rows // 4
            for j in range(4):
                self.dma("gpsimd", dst[j * q:(j + 1) * q], src[j * q:(j + 1) * q],
                         "p_" + key, w=[(key, j)])
        self.dma("gpsimd", self.WX[:], self.w_x[:], "pres", w=[("wx",)])
        self.dma("gpsimd", self.WDT[:], self.w_dt[:], "pres", w=[("wdt",)])
        self.dma("gpsimd", self.UTS[:], self.ut[:], "pres", w=[("uts",)])
        self.dma("gpsimd", self.VSB[:], self.v_r[:], "pres", w=[("vsb",)])
        self.dma("gpsimd", self.GATET[:], self.gatet[:], "pres", w=[("gatet",)])
        for dst, src, key in [(self.CONVW, self.conv_w_r, "convw"),
                              (self.CONVB, self.conv_b_r, "convb"),
                              (self.BDT, self.b_dt_r, "bdt"),
                              (self.DCOL, self.d_r, "dcol"),
                              (self.MUR, self.mur_r, "mur"),
                              (self.MUK, self.muk_r, "muk")]:
            self.dma("sync", dst[:], src[:], "psml", w=[(key,)])
        self.dma("sync", self.SELB[64:96, :], self.selb[:], "pcon", w=[("selb",)])
        self.dma("sync", self.SELC[64:96, :], self.selc[:], "pcon", w=[("selc",)])
        self.dma("sync", self.REP8[:], self.rep8[:], "pcon", w=[("rep8",)])
        self.dma("sync", self.ONES1[:], self.ones1[:], "pcon", w=[("ones1",)])
        self.dma("sync", self.IDEN[:], self.iden[:], "pcon", w=[("iden",)])
        self.dma("sync", self.ANEG[:], self.a_log_r[:], "pan", w=[("aneg",)])
        self.act(self.ANEG[:], self.ANEG[:], AF.Exp, r=[("aneg",)], w=[("aneg",)])
        self.ts(self.ANEG[:], self.ANEG[:], -1.0, None, ALU.mult, None,
                r=[("aneg",)], w=[("aneg",)])

    def ln_stats(self, st, src_ap, col, key_r):
        pf = st["pf"]
        self.ts(st["dv"][:, 0:src_ap.shape[1]], src_ap, 1.0, 0.0, ALU.mult, ALU.add,
                accum=st["s1"][:, col:col + 1],
                r=key_r, w=[(pf, "s1", col), (pf, "dumpv")])
        self.act(st["da"][:, 0:src_ap.shape[1]], src_ap, AF.Square,
                 accum=st["s2"][:, col:col + 1],
                 r=key_r, w=[(pf, "s2", col), (pf, "dumpa")])

    def ln_math(self, st, lo, n, want_r=False):
        pf = st["pf"]
        sl = slice(lo, lo + n)
        rs1 = [(pf, "s1", c) for c in range(lo, lo + n)]
        rs2 = [(pf, "s2", c) for c in range(lo, lo + n)]
        self.ts(st["negm"][:, sl], st["s1"][:, sl], -1.0 / D, None, ALU.mult, None,
                r=rs1, w=[(pf, "negm", lo)])
        self.ts(st["sa"][:, sl], st["s1"][:, sl], 1.0 / D, None, ALU.mult, None,
                r=rs1, w=[(pf, "sa", lo)])
        self.tt(st["sb"][:, sl], st["sa"][:, sl], st["sa"][:, sl], ALU.mult,
                r=[(pf, "sa", lo)], w=[(pf, "sb", lo)])
        self.stt(st["sv"][:, sl], st["s2"][:, sl], 1.0 / D, st["sb"][:, sl],
                 ALU.mult, ALU.subtract, r=rs2 + [(pf, "sb", lo)], w=[(pf, "sv", lo)])
        self.act(st["sw"][:, sl], st["sv"][:, sl], AF.Ln, bias=EPS,
                 r=[(pf, "sv", lo)], w=[(pf, "sw", lo)])
        self.act(st["rstd"][:, sl], st["sw"][:, sl], AF.Exp, scale=-0.5,
                 r=[(pf, "sw", lo)], w=[(pf, "rstd", lo)])
        if want_r:
            self.tt(st["aux"][:, sl], st["rstd"][:, sl], st["rstd"][:, sl], ALU.mult,
                    r=[(pf, "rstd", lo)], w=[(pf, "aux", lo)])
            self.stt(st["aux"][:, sl], st["sv"][:, sl], float(D), st["aux"][:, sl],
                     ALU.mult, ALU.mult,
                     r=[(pf, "sv", lo), (pf, "aux", lo)], w=[(pf, "aux", lo)])
            self.ts(st["aux"][:, sl], st["aux"][:, sl], 1e-20, None, ALU.max, None,
                    r=[(pf, "aux", lo)], w=[(pf, "aux", lo)])
            self.act(st["aux"][:, sl], st["aux"][:, sl], AF.Ln,
                     r=[(pf, "aux", lo)], w=[(pf, "aux", lo)])
            self.act(st["aux"][:, sl], st["aux"][:, sl], AF.Exp, scale=-0.5,
                     r=[(pf, "aux", lo)], w=[(pf, "aux", lo)])
            self.tt(st["aux2"][:, sl], st["rstd"][:, sl], st["aux"][:, sl], ALU.mult,
                    r=[(pf, "rstd", lo), (pf, "aux", lo)], w=[(pf, "aux2", lo)])

    def phaseA_merged(self):
        v = self.view
        self.AU = [v(BF16, DC * UW) for _ in range(2)]
        self.ADEL = v(BF16, DC * 512)
        self.AW = v(BF16, DC * 512)
        self.AY = v(FP16, DC * 512)
        self.AXFA = [v(BF16, FC * 512) for _ in range(2)]
        self.AX0 = [v(F32, 1024) for _ in range(3)]
        self.AXH = [v(BF16, 1024) for _ in range(2)]
        self.AXO = [v(F32, 1024) for _ in range(4)]
        self.ADR = v(BF16, 512)
        self.AXDB = [v(BF16, 512) for _ in range(2)]
        self.AE1 = v(F32, 512)
        self.ABBC = [v(BF16, 512) for _ in range(2)]
        self.ACBC = [v(BF16, 512) for _ in range(2)]
        self.AA2 = [v(FP16, 512) for _ in range(2)]
        self.AB2 = [v(BF16, 512) for _ in range(2)]
        self.AH2 = [v(BF16, 512) for _ in range(2)]
        self.AYC = [v(BF16, 512) for _ in range(2)]
        self.AACC = v(BF16, 512)
        self.AG = [v(BF16, 512) for _ in range(2)]
        self.ATMP = [v(F32, 512) for _ in range(2)]
        self.x0cnt = 0
        self.xhcnt = 0
        NSTEP = 2 * NCH

        self.vmemset(self.HST[:], 0.0,
                     w=[("hst", s, n, dc) for s in range(2)
                        for n in range(NST) for dc in range(DC)])
        for q in range(2):
            u3 = self.AU[q].rearrange("p (dc r) -> p dc r", dc=DC)
            self.vmemset(u3[:, :, UOF - 3:UOF], 0.0,
                         w=[("tl", q, dc) for dc in range(DC)])

        self.frontA(0)
        self.frontA(1)
        self.frontB(0)
        for j in range(NSTEP):
            self.scanA(j)
            if j + 1 < NSTEP:
                self.frontB(j + 1)
            self.backA(j)

    def autile(self, q, dc):
        return self.AU[q][:, dc * UW: dc * UW + UW]

    def adtile(self, dc):
        return self.ADEL[:, dc * 512:(dc + 1) * 512]

    def awtile(self, dc):
        return self.AW[:, dc * 512:(dc + 1) * 512]

    def aytile(self, dc):
        return self.AY[:, dc * 512:(dc + 1) * 512]

    def axfa(self, q, fc):
        return self.AXFA[q][:, fc * 512:(fc + 1) * 512]

    def frontA(self, j):
        s, c = j % 2, j // 2
        q = j % 2
        lo = 4 * q
        st = self.STA
        pf = st["pf"]
        for sub in range(NSUB):
            g = c * NSUB + sub
            b = self.x0cnt % 3
            self.x0cnt += 1
            self.dma("sync", self.AX0[b][:], self.x_in[s, g * 128:(g + 1) * 128, :],
                     f"x0{b}", w=[("x0", b)])
            self.ln_stats(st, self.AX0[b][:], lo + sub, [("x0", b)])
        self.ln_math(st, lo, NSUB)
        for sub in range(NSUB):
            g = c * NSUB + sub
            b = self.x0cnt % 3
            self.x0cnt += 1
            rx = self.xhcnt % 2
            self.xhcnt += 1
            self.dma("sync", self.AX0[b][:], self.x_in[s, g * 128:(g + 1) * 128, :],
                     f"x0{b}", w=[("x0", b)])
            self.ts(self.AXH[rx], self.AX0[b][:],
                    st["negm"][:, lo + sub:lo + sub + 1],
                    st["rstd"][:, lo + sub:lo + sub + 1], ALU.add, ALU.mult,
                    r=[("x0", b), (pf, "negm", lo), (pf, "rstd", lo)],
                    w=[("xh", rx)])
            for fc in range(FC):
                self.dmat(self.axfa(q, fc)[:, sub * 128:(sub + 1) * 128],
                          self.AXH[rx][:, fc * 128:(fc + 1) * 128],
                          r=[("xh", rx)], w=[("xfa", q, fc, sub)])
        xfar = [("xfa", q, fc, sub) for fc in range(FC) for sub in range(NSUB)]
        for oc in range(DC):
            p = oc % 2
            wb = self.WSTR[:, p * 1024:p * 1024 + 1024]
            self.dma("sync", wb, self.win_p[oc], f"wb{p}",
                     r=[("win_p", jj) for jj in range(4)], w=[("wbd", p)])
            pb = self.PS[p]
            for fc in range(FC):
                self.mm(pb[:], wb[:, fc * 128:(fc + 1) * 128], self.axfa(q, fc),
                        start=(fc == 0), stop=(fc == FC - 1),
                        r=[("wbd", p)] + xfar, w=[("ps", p)])
            self.act(self.autile(q, oc)[:, UOF:UOF + 512], pb[:], AF.Copy,
                     r=[("ps", p)], w=[("u", q, oc), ("ps", p)])
        for dc in range(DC):
            ut = self.autile(q, dc)
            self.ts(self.AACC, ut[:, UOF - 3:UOF + 509],
                    self.CONVW[:, dc * 4:dc * 4 + 1], None, ALU.mult, None,
                    r=[("u", q, dc), ("tl", q, dc), ("convw",)], w=[("acc",)])
            for k in range(1, 4):
                self.stt(self.AACC, ut[:, UOF - 3 + k:UOF + 509 + k],
                         self.CONVW[:, dc * 4 + k:dc * 4 + k + 1], self.AACC,
                         ALU.mult, ALU.add,
                         r=[("u", q, dc), ("tl", q, dc), ("acc",), ("convw",)],
                         w=[("acc",)])
            self.gcopy(ut[:, UOF - 3:UOF], ut[:, UOF + 509:UOF + 512],
                       r=[("u", q, dc)], w=[("tl", q, dc)])
            self.act(ut[:, UOF:UOF + 512], self.AACC, AF.Silu,
                     bias=self.CONVB[:, dc:dc + 1],
                     r=[("acc",), ("convb",)], w=[("u", q, dc)])

    def frontB(self, j):
        q = j % 2
        px = self.PS[2]
        for kc in range(DC):
            self.mm(px[0:96, :], self.WX[:, kc * 96:(kc + 1) * 96],
                    self.autile(q, kc)[:, UOF:UOF + 512],
                    start=(kc == 0), stop=(kc == DC - 1),
                    r=[("u", q, kc), ("wx",)], w=[("ps", 2)])
        self.act(self.ADR[0:64, :], px[0:64, :], AF.Copy,
                 r=[("ps", 2)], w=[("dr",)])
        self.act(self.AXDB[q][64:96, :], px[64:96, :], AF.Copy,
                 r=[("ps", 2)], w=[("xdb", q), ("ps", 2)])
        for dc in range(DC):
            pi = 3 if dc % 2 == 0 else 2
            pd = self.PS[pi]
            self.mm(pd[:], self.WDT[:, dc * 128:(dc + 1) * 128], self.ADR[0:64, :],
                    start=True, stop=True,
                    r=[("dr",), ("wdt",)], w=[("ps", pi)])
            self.act(self.AE1, pd[:], AF.Exp, bias=self.BDT[:, dc:dc + 1],
                     r=[("ps", pi), ("bdt",)], w=[("e1",), ("ps", pi)])
            self.act(self.adtile(dc), self.AE1, AF.Ln, bias=1.0,
                     r=[("e1",)], w=[("del", dc)])
            self.tt(self.awtile(dc), self.adtile(dc),
                    self.autile(q, dc)[:, UOF:UOF + 512], ALU.mult,
                    r=[("del", dc), ("u", q, dc)], w=[("w", dc)])

    def scanA(self, j):
        s = j % 2
        q = j % 2
        for n in range(NST):
            if n == 4 and j >= 1 and j + 1 < 2 * NCH:
                self.frontA(j + 1)
            nb = n % 2
            self.mm(self.PS[4][:], self.SELB[64:96, n * 128:(n + 1) * 128],
                    self.AXDB[q][64:96, :], start=True, stop=True,
                    r=[("xdb", q), ("selb",)], w=[("ps", 4)])
            self.act(self.ABBC[nb], self.PS[4][:], AF.Copy,
                     r=[("ps", 4)], w=[("bbc", nb), ("ps", 4)])
            self.mm(self.PS[5][:], self.SELC[64:96, n * 128:(n + 1) * 128],
                    self.AXDB[q][64:96, :], start=True, stop=True,
                    r=[("xdb", q), ("selc",)], w=[("ps", 5)])
            self.vcopy(self.ACBC[nb], self.PS[5][:],
                       r=[("ps", 5)], w=[("cbc", nb), ("ps", 5)])
            for dc in range(DC):
                p = dc % 2
                col = s * 256 + dc * 16 + n
                acol = dc * 16 + n
                self.act(self.AA2[p], self.adtile(dc), AF.Exp,
                         scale=self.ANEG[:, acol:acol + 1],
                         r=[("del", dc), ("aneg",)], w=[("a2", p)])
                self.tt(self.AB2[p], self.awtile(dc), self.ABBC[nb], ALU.mult,
                        r=[("w", dc), ("bbc", nb)], w=[("b2", p)])
                self.scan(self.AH2[p], self.AA2[p], self.AB2[p],
                          self.HST[:, col:col + 1],
                          r=[("a2", p), ("b2", p), ("hst", s, n, dc)],
                          w=[("h2", p)])
                self.gcopy(self.HST[:, col:col + 1], self.AH2[p][:, 511:512],
                           r=[("h2", p)], w=[("hst", s, n, dc)])
                if n == 0:
                    self.tt(self.aytile(dc), self.AH2[p], self.ACBC[nb], ALU.mult,
                            r=[("h2", p), ("cbc", nb)], w=[("y", dc)])
                else:
                    self.tt(self.AYC[p], self.AH2[p], self.ACBC[nb], ALU.mult,
                            r=[("h2", p), ("cbc", nb)], w=[("yc", p)])
                    self.tt(self.aytile(dc), self.aytile(dc), self.AYC[p], ALU.add,
                            r=[("yc", p), ("y", dc)], w=[("y", dc)])

    def backA(self, j):
        s, c = j % 2, j // 2
        q = j % 2
        st = self.STA
        pf = st["pf"]
        xfar = [("xfa", q, fc, sub) for fc in range(FC) for sub in range(NSUB)]
        for oc in range(DC, 2 * DC):
            p = oc % 2
            dc = oc - DC
            wb = self.WSTR[:, p * 1024:p * 1024 + 1024]
            self.dma("sync", wb, self.win_p[oc], f"wb{p}",
                     r=[("win_p", jj) for jj in range(4)], w=[("wbd", p)])
            pb = self.PS[p]
            for fc in range(FC):
                self.mm(pb[:], wb[:, fc * 128:(fc + 1) * 128], self.axfa(q, fc),
                        start=(fc == 0), stop=(fc == FC - 1),
                        r=[("wbd", p)] + xfar, w=[("ps", p)])
            self.act(self.AG[p], pb[:], AF.Silu,
                     r=[("ps", p)], w=[("g", p), ("ps", p)])
            self.stt(self.ATMP[p], self.autile(q, dc)[:, UOF:UOF + 512],
                     self.DCOL[:, dc:dc + 1], self.aytile(dc), ALU.mult, ALU.add,
                     r=[("u", q, dc), ("dcol",), ("y", dc)], w=[("tmp", p)])
            self.tt(self.autile(q, dc)[:, UOF:UOF + 512], self.ATMP[p], self.AG[p],
                    ALU.mult,
                    r=[("tmp", p), ("g", p), ("u", q, dc)], w=[("u", q, dc)])
        for pp in range(2):
            for half in range(2):
                for dc in range(DC):
                    wsl = dc % 2
                    wb = self.WSTR[:, 2048 + wsl * 512:2048 + wsl * 512 + 512]
                    self.dma("sync", wb,
                             self.wout_p[dc * 128:(dc + 1) * 128,
                                         half * 512:(half + 1) * 512],
                             f"wo{wsl}",
                             r=[("wout_p", dc // 4)], w=[("wod", wsl)])
                    for k in range(2):
                        sub = 2 * pp + k
                        self.mm(self.PS[6 + k][:],
                                self.autile(q, dc)[:, UOF + sub * 128:
                                                   UOF + (sub + 1) * 128],
                                wb[:, 0:512], start=(dc == 0), stop=(dc == DC - 1),
                                r=[("wod", wsl), ("u", q, dc)], w=[("ps", 6 + k)])
                for k in range(2):
                    sub = 2 * pp + k
                    self.act(self.AXO[sub][:, half * 512:(half + 1) * 512],
                             self.PS[6 + k][:], AF.Copy,
                             r=[("ps", 6 + k)], w=[("xo", sub), ("ps", 6 + k)])
        lo = 8 + 4 * q
        for sub in range(NSUB):
            self.ln_stats(st, self.AXO[sub][:], lo + sub, [("xo", sub)])
        self.ln_math(st, lo, NSUB)
        for sub in range(NSUB):
            g = c * NSUB + sub
            b = self.x0cnt % 3
            self.x0cnt += 1
            self.dma("sync", self.AX0[b][:], self.x_in[s, g * 128:(g + 1) * 128, :],
                     f"x0{b}", w=[("x0", b)])
            for half in range(2):
                hp = half
                self.ts(self.ATMP[hp],
                        self.AXO[sub][:, half * 512:(half + 1) * 512],
                        st["negm"][:, lo + sub:lo + sub + 1],
                        st["rstd"][:, lo + sub:lo + sub + 1], ALU.add, ALU.mult,
                        r=[("xo", sub), (pf, "negm", lo), (pf, "rstd", lo)],
                        w=[("tmp", hp)])
                self.tt(self.AXO[sub][:, half * 512:(half + 1) * 512],
                        self.AX0[b][:, half * 512:(half + 1) * 512],
                        self.ATMP[hp], ALU.add,
                        r=[("x0", b), ("tmp", hp)], w=[("xo", sub)])
            self.dma("gpsimd", self.xres_h[s, g * 128:(g + 1) * 128, :],
                     self.AXO[sub][:], "xw",
                     r=[("xo", sub)], w=[("xrh", s, g)])

    def phaseB_merged(self):
        v = self.view
        BXIN = [v(F32, 1024) for _ in range(3)]
        BXFB = [v(BF16, FC * 512) for _ in range(2)]
        BZ = [v(F32, NSUB * 1024) for _ in range(2)]
        BXH = [v(BF16, 1024) for _ in range(2)]
        BT = [v(BF16, 512) for _ in range(2)]
        BG = [v(BF16, 512) for _ in range(2)]
        BTMP = [v(F32, 512) for _ in range(2)]
        st = self.STB
        pf = st["pf"]
        bx = [0]
        xfb = lambda qb, fc: BXFB[qb][:, fc * 512:(fc + 1) * 512]

        for i in range(2 * NCH):
            s, c = i % 2, i // 2
            qb = i % 2
            lo = 4 * qb
            loz = 8 + 4 * qb
            for sub in range(NSUB):
                g = c * NSUB + sub
                b = bx[0] % 3
                bx[0] += 1
                self.dma("sync", BXIN[b][:], self.xres_h[s, g * 128:(g + 1) * 128, :],
                         f"bx{b}", r=[("xrh", s, g)], w=[("bxin", b)])
                self.ln_stats(st, BXIN[b][:], lo + sub, [("bxin", b)])
            self.ln_math(st, lo, NSUB)
            for sub in range(NSUB):
                g = c * NSUB + sub
                b = bx[0] % 3
                bx[0] += 1
                rx = sub % 2
                self.dma("sync", BXIN[b][:], self.xres_h[s, g * 128:(g + 1) * 128, :],
                         f"bx{b}", r=[("xrh", s, g)], w=[("bxin", b)])
                self.ts(BXH[rx], BXIN[b][:],
                        st["negm"][:, lo + sub:lo + sub + 1],
                        st["rstd"][:, lo + sub:lo + sub + 1], ALU.add, ALU.mult,
                        r=[("bxin", b), (pf, "negm", lo), (pf, "rstd", lo)],
                        w=[("bxh", rx)])
                for fc in range(FC):
                    self.dmat(xfb(qb, fc)[:, sub * 128:(sub + 1) * 128],
                              BXH[rx][:, fc * 128:(fc + 1) * 128],
                              r=[("bxh", rx)], w=[("xfb", qb, fc, sub)])
            xfbr = [("xfb", qb, fc, sub) for fc in range(FC) for sub in range(NSUB)]

            for fc in range(FC):
                self.mm(self.PS[0][:], self.UTS[:, fc * 128:(fc + 1) * 128],
                        xfb(qb, fc), start=(fc == 0), stop=(fc == FC - 1),
                        r=[("uts",)] + xfbr, w=[("ps", 0)])
            self.vcopy(BT[qb], self.PS[0][:],
                       r=[("ps", 0)], w=[("texp", qb), ("ps", 0)])
            for sub in range(NSUB):
                for fc in range(FC):
                    self.mm(self.PS[1][:, 0:8],
                            xfb(qb, fc)[:, sub * 128:(sub + 1) * 128],
                            self.GATET[:, fc * 8:(fc + 1) * 8],
                            start=(fc == 0), stop=(fc == FC - 1),
                            r=[("gatet",)] + xfbr, w=[("ps", 1)])
                self.rmax(self.NM[:], self.PS[1][:, 0:8], True,
                          r=[("ps", 1)], w=[("nm",)])
                self.act(self.GE[:], self.PS[1][:, 0:8], AF.Exp,
                         bias=self.NM[:, 0:1],
                         r=[("ps", 1), ("nm",)], w=[("ge",), ("ps", 1)])
                self.rsumf(self.GS[:, 0:1], self.GE[:], r=[("ge",)], w=[("gs",)])
                self.ts(self.GS[:, 1:2], self.GS[:, 0:1], 16.0, None, ALU.mult,
                        None, r=[("gs",)], w=[("gs16",)])
                self.recip(self.GS[:, 0:1], self.GS[:, 1:2],
                           r=[("gs16",)], w=[("gs",)])
                self.ts(self.GSUB[:], self.GE[:], self.GS[:, 0:1], None, ALU.mult,
                        None, r=[("ge",), ("gs",)], w=[("gsub",)])
                pgt = self.PS[2].ap().bitcast(BF16)[0:8, 0:128]
                self.tpose(pgt, self.GSUB[:],
                           r=[("gsub",), ("iden",)], w=[("ps", 2)])
                self.act(self.GFM[:, sub * 128:(sub + 1) * 128], pgt, AF.Copy,
                         r=[("ps", 2)], w=[("gfm", sub), ("ps", 2)])
            self.mm(self.PS[3][:], self.REP8[:], self.GFM[:], start=True, stop=True,
                    r=[("rep8",)] + [("gfm", ii) for ii in range(NSUB)],
                    w=[("ps", 3)])
            self.vcopy(BG[qb], self.PS[3][:], r=[("ps", 3)], w=[("bg", qb), ("ps", 3)])
            self.tt(BT[qb], BT[qb], BG[qb], ALU.mult,
                    r=[("texp", qb), ("bg", qb)], w=[("texp", qb)])

            for half in range(2):
                for fc in range(FC):
                    p = fc % 2
                    wb = self.WSTR[:, 3072 + p * 512:3072 + p * 512 + 512]
                    self.dma("sync", wb,
                             self.w0t_p[fc * 128:(fc + 1) * 128,
                                        half * 512:(half + 1) * 512],
                             f"wB{p}",
                             r=[("w0t_p", fc // 2)], w=[("wbB", p)])
                    for sub in range(NSUB):
                        self.mm(self.PS[4 + sub][:],
                                xfb(qb, fc)[:, sub * 128:(sub + 1) * 128],
                                wb[:, 0:512],
                                start=(fc == 0), stop=False,
                                r=[("wbB", p)] + xfbr, w=[("ps", 4 + sub)])
                for sub in range(NSUB):
                    self.mm(self.PS[4 + sub][:],
                            BT[qb][:, sub * 128:(sub + 1) * 128],
                            self.VSB[:, half * 512:(half + 1) * 512],
                            start=False, stop=True,
                            r=[("texp", qb), ("vsb",)], w=[("ps", 4 + sub)])
                for sub in range(NSUB):
                    self.act(BZ[qb][:, sub * 1024 + half * 512:
                                    sub * 1024 + (half + 1) * 512],
                             self.PS[4 + sub][:], AF.Copy,
                             r=[("ps", 4 + sub)],
                             w=[("bz", qb, sub), ("ps", 4 + sub)])
            for sub in range(NSUB):
                self.ln_stats(st, BZ[qb][:, sub * 1024:(sub + 1) * 1024], loz + sub,
                              [("bz", qb, sub)])
            self.ln_math(st, loz, NSUB)
            for sub in range(NSUB):
                g = c * NSUB + sub
                b = bx[0] % 3
                bx[0] += 1
                self.dma("sync", BXIN[b][:], self.xres_h[s, g * 128:(g + 1) * 128, :],
                         f"bx{b}", r=[("xrh", s, g)], w=[("bxin", b)])
                for half in range(2):
                    hp = half
                    self.ts(BTMP[hp], BZ[qb][:, sub * 1024 + half * 512:
                                              sub * 1024 + (half + 1) * 512],
                            st["negm"][:, loz + sub:loz + sub + 1],
                            st["rstd"][:, loz + sub:loz + sub + 1],
                            ALU.add, ALU.mult,
                            r=[("bz", qb, sub), (pf, "negm", loz), (pf, "rstd", loz)],
                            w=[("btmp", hp)])
                    self.tt(BXIN[b][:, half * 512:(half + 1) * 512],
                            BXIN[b][:, half * 512:(half + 1) * 512],
                            BTMP[hp], ALU.add,
                            r=[("btmp", hp)], w=[("bxin", b)])
                self.dma("gpsimd", self.xres_h[s, g * 128:(g + 1) * 128, :],
                         BXIN[b][:], "xw",
                         r=[("bxin", b)], w=[("xrh", s, g)])

    def phaseC(self, s):
        pf = ("C", s)
        st = self.STA
        spf = st["pf"]
        v = self.view
        XHT = v(BF16, TS * D)
        XFC = v(BF16, FC * SEQ)
        ES = v(BF16, TS * 512)
        AO = [v(F32, D) for _ in range(2)]
        XT = [v(BF16, D) for _ in range(2)]
        TMP = [v(F32, 512) for _ in range(2)]
        CXI = [v(F32, 1024) for _ in range(3)]
        CXR = [v(F32, 1024) for _ in range(2)]
        xht = lambda g: XHT[:, g * D:(g + 1) * D]
        xfc = lambda fc: XFC[:, fc * SEQ:(fc + 1) * SEQ]
        es = lambda kc: ES[:, kc * 512:(kc + 1) * 512]
        cx = [0]

        for g in range(TS):
            b = cx[0] % 3
            cx[0] += 1
            self.dma("sync", CXI[b][:], self.xres_h[s, g * 128:(g + 1) * 128, :],
                     f"cx{b}", r=[("xrh", s, g)], w=[("cxi", b)])
            self.ln_stats(st, CXI[b][:], g, [("cxi", b)])
        self.ln_math(st, 0, TS, want_r=True)
        for g in range(TS):
            b = cx[0] % 3
            cx[0] += 1
            p = g % 2
            self.dma("sync", CXI[b][:], self.xres_h[s, g * 128:(g + 1) * 128, :],
                     f"cx{b}", r=[("xrh", s, g)], w=[("cxi", b)])
            self.ts(xht(g), CXI[b][:],
                    st["negm"][:, g:g + 1], st["rstd"][:, g:g + 1],
                    ALU.add, ALU.mult,
                    r=[("cxi", b), (spf, "negm", 0), (spf, "rstd", 0)],
                    w=[(pf, "xht", g)])
            self.ts(XT[p], CXI[b][:],
                    st["negm"][:, g:g + 1], st["aux2"][:, g:g + 1],
                    ALU.add, ALU.mult,
                    r=[("cxi", b), (spf, "negm", 0), (spf, "aux2", 0)],
                    w=[(pf, "xt", p)])
            for fc in range(FC):
                self.dmat(xfc(fc)[:, g * 128:(g + 1) * 128],
                          XT[p][:, fc * 128:(fc + 1) * 128],
                          r=[(pf, "xt", p)], w=[(pf, "xfc", fc, g)])

        for qb in range(NCH):
            q0 = qb * 512
            xfcq = [(pf, "xfc", fc, qb * 4 + i) for fc in range(FC) for i in range(4)]
            for kc in range(TS):
                pq = kc % 2
                for fc in range(FC):
                    self.mm(self.PS[pq][:], xfc(fc)[:, kc * 128:(kc + 1) * 128],
                            xfc(fc)[:, q0:q0 + 512],
                            start=(fc == 0), stop=(fc == FC - 1),
                            r=[(pf, "xfc", fc, kc)] + xfcq, w=[("ps", pq)])
                self.act(es(kc), self.PS[pq][:], AF.Exp,
                         r=[("ps", pq)], w=[(pf, "es", kc), ("ps", pq)])
            for qs in range(NSUB):
                gq = qb * NSUB + qs
                i0, i1 = 2 + 2 * (qs % 2), 3 + 2 * (qs % 2)
                for kc in range(TS):
                    l = es(kc)[:, qs * 128:(qs + 1) * 128]
                    self.mm(self.PS[i0][:], l, xht(kc)[:, 0:512],
                            start=(kc == 0), stop=(kc == TS - 1),
                            r=[(pf, "es", kc), (pf, "xht", kc)], w=[("ps", i0)])
                    self.mm(self.PS[i1][:], l, xht(kc)[:, 512:1024],
                            start=(kc == 0), stop=(kc == TS - 1),
                            r=[(pf, "es", kc), (pf, "xht", kc)], w=[("ps", i1)])
                    self.mm(self.PS[6][:, 0:1], l, self.ONES1[:],
                            start=(kc == 0), stop=(kc == TS - 1),
                            r=[(pf, "es", kc), ("ones1",)], w=[("ps", 6)])
                self.recip(self.RINV[:, qs:qs + 1], self.PS[6][:, 0:1],
                           r=[("ps", 6)], w=[(pf, "rinv", qs), ("ps", 6)])
                pa = qs % 2
                for half, pi in ((0, i0), (1, i1)):
                    self.stt(AO[pa][:, half * 512:(half + 1) * 512],
                             self.PS[pi][:], self.RINV[:, qs:qs + 1],
                             xht(gq)[:, half * 512:(half + 1) * 512],
                             ALU.mult, ALU.add,
                             r=[("ps", pi), (pf, "rinv", qs), (pf, "xht", gq)],
                             w=[(pf, "ao", pa), ("ps", pi)])
                self.ln_stats(st, AO[pa][:], qs, [(pf, "ao", pa)])
                self.ln_math(st, qs, 1)
                b2 = cx[0] % 2
                cx[0] += 1
                self.dma("sync", CXR[b2][:],
                         self.xres_h[s, gq * 128:(gq + 1) * 128, :],
                         f"cr{b2}", r=[("xrh", s, gq)], w=[("cxr", b2)])
                for half in range(2):
                    self.ts(TMP[half], AO[pa][:, half * 512:(half + 1) * 512],
                            st["negm"][:, qs:qs + 1], st["rstd"][:, qs:qs + 1],
                            ALU.add, ALU.mult,
                            r=[(pf, "ao", pa), (spf, "negm", qs), (spf, "rstd", qs)],
                            w=[(pf, "tmp", half)])
                    self.tt(CXR[b2][:, half * 512:(half + 1) * 512],
                            CXR[b2][:, half * 512:(half + 1) * 512],
                            TMP[half], ALU.add,
                            r=[(pf, "tmp", half)], w=[("cxr", b2)])
                self.dma("gpsimd", self.xres_h[s, gq * 128:(gq + 1) * 128, :],
                         CXR[b2][:], "xw",
                         r=[("cxr", b2)], w=[("xrh", s, gq)])

    def phaseD(self, s):
        pf = ("D", s)
        st = self.STA
        spf = st["pf"]
        v = self.view
        XFD = v(BF16, FC * 544)
        XR = v(BF16, FC * 512)
        XK = v(BF16, FC * 512)
        KACT = v(BF16, FC * 512)
        RT = v(BF16, NSUB * D)
        VR = v(F32, NSUB * D)
        WS2 = v(BF16, FC * 512)
        XH = [v(BF16, D) for _ in range(2)]
        DD = v(BF16, 512)
        KR = v(BF16, 512)
        XO = [v(F32, D) for _ in range(2)]
        TMP = [v(F32, 512) for _ in range(2)]
        DXI = [v(F32, 1024) for _ in range(3)]
        xfd3 = XFD.rearrange("p (fc r) -> p fc r", fc=FC)
        xfd = lambda fc: XFD[:, fc * 544:(fc + 1) * 544]
        xr = lambda fc: XR[:, fc * 512:(fc + 1) * 512]
        xk = lambda fc: XK[:, fc * 512:(fc + 1) * 512]
        ka = lambda dcb: KACT[:, dcb * 512:(dcb + 1) * 512]
        dx = [0]

        self.vmemset(xfd3[:, :, 15:16], 0.0, w=[(pf, "bnd")])

        for c in range(NCH):
            xfdr = [(pf, "xfd", fc, sub) for fc in range(FC) for sub in range(NSUB)]
            if c > 0:
                self.vcopy(xfd3[:, :, 15:16], xfd3[:, :, 527:528],
                           r=xfdr, w=[(pf, "bnd")])
            for sub in range(NSUB):
                g = c * NSUB + sub
                b = dx[0] % 3
                dx[0] += 1
                self.dma("sync", DXI[b][:], self.xres_h[s, g * 128:(g + 1) * 128, :],
                         f"dx{b}", r=[("xrh", s, g)], w=[("dxi", b)])
                self.ln_stats(st, DXI[b][:], sub, [("dxi", b)])
            self.ln_math(st, 0, NSUB)
            for sub in range(NSUB):
                g = c * NSUB + sub
                b = dx[0] % 3
                dx[0] += 1
                p = sub % 2
                self.dma("sync", DXI[b][:], self.xres_h[s, g * 128:(g + 1) * 128, :],
                         f"dx{b}", r=[("xrh", s, g)], w=[("dxi", b)])
                self.ts(XH[p], DXI[b][:],
                        st["negm"][:, sub:sub + 1], st["rstd"][:, sub:sub + 1],
                        ALU.add, ALU.mult,
                        r=[("dxi", b), (spf, "negm", 0), (spf, "rstd", 0)],
                        w=[(pf, "xh", p)])
                for fc in range(FC):
                    self.dmat(xfd(fc)[:, 16 + sub * 128:16 + (sub + 1) * 128],
                              XH[p][:, fc * 128:(fc + 1) * 128],
                              r=[(pf, "xh", p), (pf, "bnd")],
                              w=[(pf, "xfd", fc, sub)])
            for fc in range(FC):
                cur = xfd(fc)[:, 16:528]
                prv = xfd(fc)[:, 15:527]
                self.tt(DD, cur, prv, ALU.subtract,
                        r=xfdr + [(pf, "bnd")], w=[(pf, "dd")])
                self.stt(xr(fc), DD, self.MUR[:, fc:fc + 1], prv, ALU.mult, ALU.add,
                         r=[(pf, "dd"), ("mur",)] + xfdr + [(pf, "bnd")],
                         w=[(pf, "xr", fc)])
                self.stt(xk(fc), DD, self.MUK[:, fc:fc + 1], prv, ALU.mult, ALU.add,
                         r=[(pf, "dd"), ("muk",)] + xfdr + [(pf, "bnd")],
                         w=[(pf, "xk", fc)])
            xrr = [(pf, "xr", fc) for fc in range(FC)]
            xkr = [(pf, "xk", fc) for fc in range(FC)]
            for dcb in range(FC):
                p = dcb % 2
                wb = self.WSTR[:, p * 1024:p * 1024 + 1024]
                self.dma("sync", wb, self.wkt_p[dcb], f"wb{p}",
                         r=[("wkt_p", j) for j in range(4)], w=[("wbd", p)])
                for fc in range(FC):
                    self.mm(self.PS[p][:], wb[:, fc * 128:(fc + 1) * 128], xk(fc),
                            start=(fc == 0), stop=(fc == FC - 1),
                            r=[("wbd", p)] + xkr, w=[("ps", p)])
                self.act(KR, self.PS[p][:], AF.Relu,
                         r=[("ps", p)], w=[(pf, "kr"), ("ps", p)])
                self.tt(ka(dcb), KR, KR, ALU.mult, r=[(pf, "kr")], w=[(pf, "ka", dcb)])
            for half in range(2):
                for fc in range(FC):
                    self.dma("sync", WS2[:, fc * 512:(fc + 1) * 512],
                             self.wrt_p[fc * 128:(fc + 1) * 128,
                                        half * 512:(half + 1) * 512],
                             "ws2",
                             r=[("wrt_p", fc // 2)], w=[(pf, "ws2d", fc)])
                for sub in range(NSUB):
                    for fc in range(FC):
                        self.mm(self.PS[2 + sub][:],
                                xr(fc)[:, sub * 128:(sub + 1) * 128],
                                WS2[:, fc * 512:(fc + 1) * 512],
                                start=(fc == 0), stop=(fc == FC - 1),
                                r=[(pf, "ws2d", fc)] + xrr, w=[("ps", 2 + sub)])
                    self.act(RT[:, sub * D + half * 512:sub * D + (half + 1) * 512],
                             self.PS[2 + sub][:], AF.Sigmoid,
                             r=[("ps", 2 + sub)],
                             w=[(pf, "rt", sub), ("ps", 2 + sub)])
            for half in range(2):
                for dcb in range(FC):
                    self.dma("sync", WS2[:, dcb * 512:(dcb + 1) * 512],
                             self.wvt_p[dcb * 128:(dcb + 1) * 128,
                                        half * 512:(half + 1) * 512],
                             "ws2",
                             r=[("wvt_p", dcb // 2)], w=[(pf, "ws2d", dcb)])
                for sub in range(NSUB):
                    for dcb in range(FC):
                        self.mm(self.PS[2 + sub][:],
                                ka(dcb)[:, sub * 128:(sub + 1) * 128],
                                WS2[:, dcb * 512:(dcb + 1) * 512],
                                start=(dcb == 0), stop=(dcb == FC - 1),
                                r=[(pf, "ws2d", dcb), (pf, "ka", dcb)],
                                w=[("ps", 2 + sub)])
                    self.tt(VR[:, sub * D + half * 512:sub * D + (half + 1) * 512],
                            self.PS[2 + sub][:],
                            RT[:, sub * D + half * 512:sub * D + (half + 1) * 512],
                            ALU.mult,
                            r=[("ps", 2 + sub), (pf, "rt", sub)],
                            w=[(pf, "vr", sub), ("ps", 2 + sub)])
            for sub in range(NSUB):
                self.ln_stats(st, VR[:, sub * D:(sub + 1) * D], sub,
                              [(pf, "vr", sub)])
            self.ln_math(st, 0, NSUB)
            for sub in range(NSUB):
                g = c * NSUB + sub
                b = dx[0] % 3
                dx[0] += 1
                p = sub % 2
                self.dma("sync", DXI[b][:], self.xres_h[s, g * 128:(g + 1) * 128, :],
                         f"dx{b}", r=[("xrh", s, g)], w=[("dxi", b)])
                for half in range(2):
                    self.ts(TMP[half],
                            VR[:, sub * D + half * 512:sub * D + (half + 1) * 512],
                            st["negm"][:, sub:sub + 1], st["rstd"][:, sub:sub + 1],
                            ALU.add, ALU.mult,
                            r=[(pf, "vr", sub), (spf, "negm", 0), (spf, "rstd", 0)],
                            w=[(pf, "tmp", half)])
                    self.tt(XO[p][:, half * 512:(half + 1) * 512],
                            DXI[b][:, half * 512:(half + 1) * 512],
                            TMP[half], ALU.add,
                            r=[(pf, "tmp", half), ("dxi", b)], w=[(pf, "xo", p)])
                self.dma("sync", self.out[s, g * 128:(g + 1) * 128, :], XO[p][:],
                         "out", r=[(pf, "xo", p)], w=[("out", s, g)])


def _layouts(i):
    def col16(v):
        return np.ascontiguousarray(v.reshape(16, 128).T)
    out = {}
    out["conv_w_r"] = np.ascontiguousarray(
        i["L0_attn_conv_w"].reshape(16, 128, 4).transpose(1, 0, 2).reshape(128, 64))
    out["conv_b_r"] = col16(i["L0_attn_conv_b"])
    out["b_dt_r"] = col16(i["L0_attn_b_dt"])
    out["d_r"] = col16(i["L0_attn_D"])
    out["a_log_r"] = np.ascontiguousarray(
        i["L0_attn_A_log"].reshape(16, 128, 16).transpose(1, 0, 2).reshape(128, 256))
    out["w_in"] = np.ascontiguousarray(
        i["L0_attn_W_in"].reshape(8, 128, 32, 128).transpose(2, 1, 0, 3)
        .reshape(32, 128, 1024))
    out["w_x"] = np.ascontiguousarray(
        i["L0_attn_W_x"].reshape(16, 128, 96).transpose(1, 0, 2).reshape(128, 16 * 96))
    out["w_dt"] = np.ascontiguousarray(i["L0_attn_W_dt"])
    out["w_out"] = np.ascontiguousarray(i["L0_attn_W_out"])
    out["w0t"] = np.ascontiguousarray(i["L0_ffn_weight"].T)
    out["gatet"] = np.ascontiguousarray(
        i["L0_ffn_gate_w"].T.reshape(8, 128, 8).transpose(1, 0, 2).reshape(128, 64))
    out["ut"] = np.ascontiguousarray(
        i["L0_ffn_lora_U"].reshape(128, 1024).T.reshape(8, 128, 128)
        .transpose(1, 0, 2).reshape(128, 8 * 128))
    out["v_r"] = np.ascontiguousarray(
        i["L0_ffn_lora_V"].transpose(0, 2, 1).reshape(128, 1024))
    out["wrt"] = np.ascontiguousarray(i["L1_ffn_W_r"].T)
    out["wkt"] = np.ascontiguousarray(
        i["L1_ffn_W_k"].T.reshape(8, 128, 8, 128).transpose(2, 1, 0, 3)
        .reshape(8, 128, 1024))
    out["wvt"] = np.ascontiguousarray(i["L1_ffn_W_v"].T)
    out["mur_r"] = np.ascontiguousarray(i["L1_ffn_mu_r"].reshape(8, 128).T)
    out["muk_r"] = np.ascontiguousarray(i["L1_ffn_mu_k"].reshape(8, 128).T)
    bf = ml_dtypes.bfloat16
    selb = np.zeros((32, NST * 128), np.float32)
    selc = np.zeros((32, NST * 128), np.float32)
    for n in range(NST):
        selb[n, n * 128:(n + 1) * 128] = 1.0
        selc[16 + n, n * 128:(n + 1) * 128] = 1.0
    out["selb"] = selb.astype(bf)
    out["selc"] = selc.astype(bf)
    rep = np.zeros((8, 128), np.float32)
    for e in range(8):
        rep[e, e * 16:(e + 1) * 16] = 1.0
    out["rep8"] = rep.astype(bf)
    out["ones1"] = np.ones((128, 1), np.float32).astype(bf)
    out["iden"] = np.eye(128, dtype=np.float32).astype(bf)
    return out


_NC_CACHE = None


_BUILDER = None


def _get_nc():
    global _NC_CACHE, _BUILDER
    if _NC_CACHE is None:
        _BUILDER = Builder()
        _NC_CACHE = _BUILDER.build()
    return _NC_CACHE


def kernel(**inputs):
    x = np.ascontiguousarray(np.asarray(inputs["L0_attn_x"], dtype=np.float32))
    shared = _layouts({k: np.asarray(v, dtype=np.float32)
                       for k, v in inputs.items() if k != "L0_attn_x"})
    nc = _get_nc()
    in_maps = []
    for i in range(8):
        m = dict(shared)
        m["x"] = np.ascontiguousarray(x[2 * i:2 * i + 2])
        in_maps.append(m)
    res = run_bass_kernel_spmd(nc, in_maps, core_ids=list(range(8)))
    out = np.stack([r["out"] for r in res.results], axis=0)
    return out.reshape(16, SEQ, D).astype(np.float32)


# version 07
# speedup vs baseline: 1.0433x; 1.0433x over previous
import sys
import numpy as np
import ml_dtypes
from contextlib import ExitStack

import concourse.bass as bass
import concourse.mybir as mybir
from concourse.bass_utils import run_bass_kernel_spmd

F32 = mybir.dt.float32
BF16 = mybir.dt.bfloat16
FP16 = mybir.dt.float16
AF = mybir.ActivationFunctionType
ALU = mybir.AluOpType
AXX = mybir.AxisListType.X

D = 1024
DI = 2048
SEQ = 2048
NSEQ = 2
NST = 16
DTR = 64
LC = 512
NCH = SEQ // LC
NSUB = LC // 128
FC = D // 128
DC = DI // 128
TS = SEQ // 128
EPS = 1e-5
UOF = 8
UW = UOF + 512

ENGS = ("tensor", "vector", "scalar", "gpsimd", "sync")
_A_ONLY = False
RELAXED = False


class Planner:

    def __init__(self):
        self.ops = []
        self.bufs = {}
        self.groups = {}
        self.nbar = 0
        self.last_eng = {}

    def _b(self, k):
        b = self.bufs.get(k)
        if b is None:
            b = {"w": None, "r": []}
            self.bufs[k] = b
        return b

    def _g(self, name):
        g = self.groups.get(name)
        if g is None:
            g = {"gen": 0, "count": 0, "bounds": [], "pending": set(), "last": None}
            self.groups[name] = g
        return g

    def add(self, eng, fn, r=(), w=(), grp=None, extra=(), bar_inc=False,
            bar_wait=None, async_out=False):
        i = len(self.ops)
        deps = set(extra)
        sdeps = set()
        for k in r:
            b = self._b(k)
            if b["w"] is not None:
                deps.add(b["w"])
        for k in w:
            b = self._b(k)
            if b["w"] is not None:
                sdeps.add(b["w"])
            sdeps.update(b["r"])
        deps.discard(i)
        sdeps.discard(i)
        deps -= sdeps
        gw = {}
        eng_deps = set()
        for j in deps | sdeps:
            oj = self.ops[j]
            if oj["grp"] is not None:
                gname, gen = oj["grp"]
                gw[gname] = max(gw.get(gname, -1), gen)
                gg = self._g(gname)
                if gen == gg["gen"]:
                    gg["pending"].add(i)
            else:
                eng_deps.add(j)
        gtag = None
        if grp is not None:
            g = self._g(grp)
            pend = {j for j in g["pending"] if j != i}
            if pend:
                g["bounds"].append(g["count"])
                g["gen"] += 1
                eng_deps.update(j for j in pend if self.ops[j]["grp"] is None)
                for j in pend:
                    oj = self.ops[j]
                    if oj["grp"] is not None:
                        gn, gg = oj["grp"]
                        gw[gn] = max(gw.get(gn, -1), gg)
                g["pending"] = set()
            done = g["bounds"][-1] if g["bounds"] else 0
            if gw.get(grp, -1) == g["gen"] and g["count"] > done:
                g["bounds"].append(g["count"])
                g["gen"] += 1
            g["count"] += 1
            g["last"] = i
            gtag = (grp, g["gen"])
        self.ops.append({"eng": eng, "fn": fn, "grp": gtag, "deps": eng_deps,
                         "gw": gw, "async_out": async_out,
                         "bar_inc": bar_inc, "bar_wait": bar_wait})
        for k in r:
            self._b(k)["r"].append(i)
        for k in w:
            b = self._b(k)
            b["w"] = i
            b["r"] = []
        if gtag is None and not bar_inc and bar_wait is None:
            self.last_eng[eng] = i
        return i

    def barrier(self):
        lasts = [i for i in self.last_eng.values()]
        last_dmas = [g["last"] for g in self.groups.values() if g["last"] is not None]
        for g in self.groups.values():
            done = g["bounds"][-1] if g["bounds"] else 0
            if g["count"] > done:
                g["bounds"].append(g["count"])
                g["gen"] += 1
        self.add("sync", lambda eng: eng.nop(), extra=lasts + last_dmas,
                 bar_inc=True)
        self.nbar += 1
        for e in ENGS:
            if e != "sync":
                self.add(e, lambda eng: eng.nop(), bar_wait=self.nbar)

    def finalize(self):
        n = len(self.ops)
        for g in self.groups.values():
            g["bounds"].append(g["count"])
        if RELAXED:
            for o in self.ops:
                if o["grp"] is None:
                    o["deps"] = {j for j in o["deps"]
                                 if self.ops[j]["eng"] != o["eng"]
                                 or self.ops[j]["grp"] is not None
                                 or self.ops[j]["async_out"]}
            need = [False] * n
            for o in self.ops:
                for j in o["deps"]:
                    need[j] = True
        else:
            need = [o["grp"] is None and not o["bar_inc"] and o["bar_wait"] is None
                    for o in self.ops]
        cum = {}
        val = [0] * n
        for i, o in enumerate(self.ops):
            if o["grp"] is not None:
                continue
            if need[i]:
                s = "e_" + o["eng"]
                cum[s] = cum.get(s, 0) + 1
                val[i] = cum[s]

        def gen_threshold(gname, gen):
            b = self.groups[gname]["bounds"]
            k = min(gen, len(b) - 1)
            return 16 * b[k]

        sched = {k: [] for k in ENGS}
        floor = {e: {} for e in ENGS}
        for i, o in enumerate(self.ops):
            wmap = {}
            for j in o["deps"]:
                if val[j] > 0:
                    s = "e_" + self.ops[j]["eng"]
                    wmap[s] = max(wmap.get(s, 0), val[j])
            for gname, gen in o["gw"].items():
                s = "g_" + gname
                wmap[s] = max(wmap.get(s, 0), gen_threshold(gname, gen))
            if o["bar_wait"] is not None:
                wmap["bar"] = o["bar_wait"]
            if o["bar_inc"]:
                inc = ("bar", 1)
                nbc = self.bar_cnt = getattr(self, "bar_cnt", 0) + 1
                if nbc > 1:
                    wmap["bar"] = max(wmap.get("bar", 0), nbc - 1)
            elif o["grp"] is not None:
                inc = ("g_" + o["grp"][0], 16)
            elif need[i]:
                inc = ("e_" + o["eng"], 1)
                if not RELAXED:
                    s = inc[0]
                    if val[i] > 1:
                        wmap[s] = max(wmap.get(s, 0), val[i] - 1)
            else:
                inc = None
            fl = floor[o["eng"]]
            wl = []
            for s, v in sorted(wmap.items()):
                if fl.get(s, -1) < v:
                    fl[s] = v
                    wl.append((s, v))
            sched[o["eng"]].append((o["fn"], wl, inc))
        return sched, cum


class Builder:
    def __init__(self):
        self.nc = bass.Bass("TRN2", target_bir_lowering=False,
                            detect_race_conditions=not RELAXED)
        self.pl = Planner()
        self.ctx = ExitStack()
        self.arena_off = 0

    def sb(self, name, shape, dt):
        return self.ctx.enter_context(self.nc.sbuf_tensor(name, shape, dt))

    def view(self, dt, cols):
        esz = 2 if dt in (BF16, FP16) else 4
        off = (self.arena_off + 63) & ~63
        self.arena_off = off + cols * esz
        assert self.arena_off <= self.ARENA_BYTES, \
            f"arena overflow: {self.arena_off} > {self.ARENA_BYTES}"
        base = self.arena.ap() if dt == BF16 else self.arena.ap().bitcast(dt)
        return base[:, off // esz: off // esz + cols]

    def reset_arena(self):
        self.arena_off = 0

    def dma(self, q, out, in_, grp, r=(), w=()):
        self.pl.add(q, lambda e, o=out, i=in_: e.dma_start(out=o, in_=i),
                    r=r, w=w, grp=grp)

    def dmat(self, out, in_, r=(), w=()):
        self.pl.add("scalar",
                    lambda e, o=out, i=in_: e.dma_start_transpose(out=o, in_=i),
                    r=r, w=w, grp="tp")

    def mm(self, out, lhsT, rhs, start, stop, r=(), w=()):
        self.pl.add("tensor",
                    lambda e, o=out, l=lhsT, rr=rhs, s1=start, s2=stop:
                    e.matmul(o, l, rr, start=s1, stop=s2), r=r, w=w)

    def tpose(self, out, in_, r=(), w=()):
        self.pl.add("tensor",
                    lambda e, o=out, i=in_, d=self.IDEN[:]: e.transpose(o, i, d),
                    r=r, w=w)

    def act(self, out, in_, func, bias=0.0, scale=1.0, accum=None, r=(), w=()):
        self.pl.add("scalar",
                    lambda e, o=out, i=in_, f=func, b=bias, s=scale, a=accum:
                    e.activation(out=o, in_=i, func=f, bias=b, scale=s, accum_out=a),
                    r=r, w=w, async_out=accum is not None)

    def tt(self, out, a, b, op, r=(), w=()):
        self.pl.add("vector",
                    lambda e, o=out, x=a, y=b, p=op:
                    e.tensor_tensor(out=o, in0=x, in1=y, op=p), r=r, w=w)

    def ts(self, out, a, s1, s2, op0, op1, accum=None, r=(), w=()):
        if s2 is None and accum is None:
            self.pl.add("vector",
                        lambda e, o=out, x=a, u=s1, p0=op0:
                        e.tensor_scalar(out=o, in0=x, scalar1=u, scalar2=None, op0=p0),
                        r=r, w=w)
        else:
            self.pl.add("vector",
                        lambda e, o=out, x=a, u=s1, v=s2, p0=op0, p1=op1, ac=accum:
                        e.tensor_scalar(out=o, in0=x, scalar1=u, scalar2=v,
                                        op0=p0, op1=p1, accum_out=ac), r=r, w=w,
                        async_out=accum is not None)

    def stt(self, out, in0, scal, in1, op0, op1, r=(), w=()):
        self.pl.add("vector",
                    lambda e, o=out, x=in0, u=scal, y=in1, p0=op0, p1=op1:
                    e.scalar_tensor_tensor(out=o, in0=x, scalar=u, in1=y,
                                           op0=p0, op1=p1), r=r, w=w)

    def scan(self, out, a, b, init, r=(), w=()):
        self.pl.add("vector",
                    lambda e, o=out, x=a, y=b, h0=init:
                    e.tensor_tensor_scan(out=o, data0=x, data1=y, initial=h0,
                                         op0=ALU.mult, op1=ALU.add), r=r, w=w)

    def vcopy(self, out, in_, r=(), w=()):
        self.pl.add("vector", lambda e, o=out, i=in_: e.tensor_copy(out=o, in_=i),
                    r=r, w=w)

    def gcopy(self, out, in_, r=(), w=()):
        self.pl.add("gpsimd", lambda e, o=out, i=in_: e.tensor_copy(out=o, in_=i),
                    r=r, w=w)

    def vmemset(self, ap, val, r=(), w=()):
        self.pl.add("vector", lambda e, a=ap, v=val: e.memset(a, v), r=r, w=w)

    def rmax(self, out, in_, negate, r=(), w=()):
        self.pl.add("vector",
                    lambda e, o=out, i=in_, ng=negate:
                    e.reduce_max(out=o, in_=i, axis=AXX, negate=ng), r=r, w=w)

    def rsumf(self, out, in_, r=(), w=()):
        self.pl.add("vector",
                    lambda e, o=out, i=in_: e.reduce_sum(out=o, in_=i, axis=AXX),
                    r=r, w=w)

    def recip(self, out, in_, r=(), w=()):
        self.pl.add("vector", lambda e, o=out, i=in_: e.reciprocal(out=o, in_=i),
                    r=r, w=w)

    def build(self):
        nc = self.nc
        self.x_in = nc.dram_tensor("x", [NSEQ, SEQ, D], F32, kind="ExternalInput").ap()
        self.out = nc.dram_tensor("out", [NSEQ, SEQ, D], F32, kind="ExternalOutput").ap()
        f32in = lambda n, sh: nc.dram_tensor(n, sh, F32, kind="ExternalInput").ap()
        bf16in = lambda n, sh: nc.dram_tensor(n, sh, BF16, kind="ExternalInput").ap()
        self.w_in = f32in("w_in", [2 * DC, 128, 8 * 128])
        self.conv_w_r = f32in("conv_w_r", [128, 64])
        self.conv_b_r = f32in("conv_b_r", [128, 16])
        self.b_dt_r = f32in("b_dt_r", [128, 16])
        self.d_r = f32in("d_r", [128, 16])
        self.a_log_r = f32in("a_log_r", [128, 256])
        self.w_x = f32in("w_x", [128, DC * 96])
        self.w_dt = f32in("w_dt", [DTR, DI])
        self.w_out = f32in("w_out", [DI, D])
        self.w0t = f32in("w0t", [D, D])
        self.gatet = f32in("gatet", [128, FC * 8])
        self.ut = f32in("ut", [128, FC * 128])
        self.v_r = f32in("v_r", [128, D])
        self.wrt = f32in("wrt", [D, D])
        self.wkt = f32in("wkt", [FC, 128, FC * 128])
        self.wvt = f32in("wvt", [D, D])
        self.mur_r = f32in("mur_r", [128, 8])
        self.muk_r = f32in("muk_r", [128, 8])
        self.selb = bf16in("selb", [32, NST * 128])
        self.selc = bf16in("selc", [32, NST * 128])
        self.rep8 = bf16in("rep8", [8, 128])
        self.ones1 = bf16in("ones1", [128, 1])
        self.iden = bf16in("iden", [128, 128])

        self.win_p = nc.dram_tensor("win_p", [2 * DC, 128, 8 * 128], BF16).ap()
        self.wout_p = nc.dram_tensor("wout_p", [DI, D], BF16).ap()
        self.w0t_p = nc.dram_tensor("w0t_p", [D, D], BF16).ap()
        self.wrt_p = nc.dram_tensor("wrt_p", [D, D], BF16).ap()
        self.wkt_p = nc.dram_tensor("wkt_p", [FC, 128, FC * 128], BF16).ap()
        self.wvt_p = nc.dram_tensor("wvt_p", [D, D], BF16).ap()
        self.xres_h = nc.dram_tensor("xres_h", [NSEQ, SEQ, D], F32).ap()

        self.ARENA_BYTES = 172544
        self.arena = self.sb("arena", [128, self.ARENA_BYTES // 2], BF16)
        self.WSTR = self.sb("wstr", [128, 4 * 1024], BF16)
        self.WX = self.sb("wxr", [128, DC * 96], BF16)
        self.WDT = self.sb("wdtr", [DTR, DI], BF16)
        self.UTS = self.sb("uts", [128, FC * 128], BF16)
        self.VSB = self.sb("vsb", [128, D], BF16)
        self.GATET = self.sb("gatetr", [128, FC * 8], BF16)
        self.SELB = self.sb("selbr", [128, NST * 128], BF16)
        self.SELC = self.sb("selcr", [128, NST * 128], BF16)
        self.REP8 = self.sb("rep8r", [8, 128], BF16)
        self.ONES1 = self.sb("ones1r", [128, 1], BF16)
        self.IDEN = self.sb("idenr", [128, 128], BF16)
        self.CONVW = self.sb("convw", [128, 64], F32)
        self.CONVB = self.sb("convb", [128, 16], F32)
        self.BDT = self.sb("bdt", [128, 16], F32)
        self.DCOL = self.sb("dcol", [128, 16], F32)
        self.ANEG = self.sb("aneg", [128, 256], F32)
        self.MUR = self.sb("mur", [128, 8], F32)
        self.MUK = self.sb("muk", [128, 8], F32)
        self.HST = self.sb("hst", [128, 512], F32)
        self.DV = self.sb("dv", [128, 1024], BF16)
        self.DA = self.sb("da", [128, 1024], BF16)
        def stats_set(tag):
            t = lambda nm, c: self.sb(tag + nm, [128, c], F32)
            return {"pf": tag, "s1": t("s1", 16), "s2": t("s2", 16),
                    "negm": t("nm", 16), "rstd": t("rs", 16),
                    "sa": t("sa", 16), "sb": t("sb", 16),
                    "sv": t("sv", 16), "sw": t("sw", 16),
                    "aux": t("ax", 16), "aux2": t("a2", 16),
                    "dv": self.DV, "da": self.DA}
        self.STA = stats_set("LA")
        self.STB = stats_set("LB")
        self.RINV = self.sb("rinv", [128, 4], F32)
        self.NM = self.sb("nm", [128, 1], F32)
        self.GE = self.sb("ge", [128, 8], F32)
        self.GS = self.sb("gs", [128, 2], F32)
        self.GSUB = self.sb("gsub", [128, 8], BF16)
        self.GFM = self.sb("gfm", [8, LC], BF16)
        self.FCONST = self.sb("fconst", [128, 2], F32)
        self.nc.const_aps.register(F32, EPS, self.FCONST[:, 0:1])
        self.nc.const_aps.register(F32, 1.0, self.FCONST[:, 1:2])
        self.PS = [self.ctx.enter_context(self.nc.psum_tensor(f"ps{i}", [128, 512], F32))
                   for i in range(8)]

        self.prep()
        self.pl.barrier()
        self.reset_arena()
        self.phaseA_merged()
        if not _A_ONLY:
            self.pl.barrier()
            self.reset_arena()
            self.phaseB_merged()
            for s in range(NSEQ):
                self.pl.barrier()
                self.reset_arena()
                self.phaseC(s)
                self.pl.barrier()
                self.reset_arena()
                self.phaseD(s)

        self.pl.add("sync", lambda e: e.nop(),
                    r=[("out", s, g) for s in range(NSEQ) for g in range(TS)])

        sched, cum = self.pl.finalize()
        sems = {}
        names = ["e_tensor", "e_vector", "e_scalar", "e_gpsimd", "e_sync", "bar"]
        names += ["g_" + g for g in self.pl.groups]
        for name in names:
            sems[name] = self.ctx.enter_context(nc.semaphore(name))

        def body(eng_obj, items):
            for fn, wl, inc in items:
                for sname, v in wl:
                    eng_obj.wait_ge(sems[sname], v)
                inst = fn(eng_obj)
                if inc is not None:
                    inst.then_inc(sems[inc[0]], inc[1])

        with nc.Block() as blk:
            blk.tensor(lambda e: body(e, sched["tensor"]))
            blk.vector(lambda e: body(e, sched["vector"]))
            blk.scalar(lambda e: body(e, sched["scalar"]))
            blk.gpsimd(lambda e: body(e, sched["gpsimd"]))
            blk.sync(lambda e: body(e, sched["sync"]))
        self.ctx.close()
        return nc

    def prep(self):
        self.vmemset(self.FCONST[:, 0:1], EPS, w=[("fconst",)])
        self.vmemset(self.FCONST[:, 1:2], 1.0, w=[("fconst",)])
        for dst, src, key in [(self.win_p, self.w_in, "win_p"),
                              (self.wout_p, self.w_out, "wout_p"),
                              (self.w0t_p, self.w0t, "w0t_p"),
                              (self.wrt_p, self.wrt, "wrt_p"),
                              (self.wkt_p, self.wkt, "wkt_p"),
                              (self.wvt_p, self.wvt, "wvt_p")]:
            rows = src.shape[0]
            q = rows // 4
            for j in range(4):
                self.dma("gpsimd", dst[j * q:(j + 1) * q], src[j * q:(j + 1) * q],
                         "p_" + key, w=[(key, j)])
        self.dma("gpsimd", self.WX[:], self.w_x[:], "pres", w=[("wx",)])
        self.dma("gpsimd", self.WDT[:], self.w_dt[:], "pres", w=[("wdt",)])
        self.dma("gpsimd", self.UTS[:], self.ut[:], "pres", w=[("uts",)])
        self.dma("gpsimd", self.VSB[:], self.v_r[:], "pres", w=[("vsb",)])
        self.dma("gpsimd", self.GATET[:], self.gatet[:], "pres", w=[("gatet",)])
        for dst, src, key in [(self.CONVW, self.conv_w_r, "convw"),
                              (self.CONVB, self.conv_b_r, "convb"),
                              (self.BDT, self.b_dt_r, "bdt"),
                              (self.DCOL, self.d_r, "dcol"),
                              (self.MUR, self.mur_r, "mur"),
                              (self.MUK, self.muk_r, "muk")]:
            self.dma("sync", dst[:], src[:], "psml", w=[(key,)])
        self.dma("sync", self.SELB[64:96, :], self.selb[:], "pcon", w=[("selb",)])
        self.dma("sync", self.SELC[64:96, :], self.selc[:], "pcon", w=[("selc",)])
        self.dma("sync", self.REP8[:], self.rep8[:], "pcon", w=[("rep8",)])
        self.dma("sync", self.ONES1[:], self.ones1[:], "pcon", w=[("ones1",)])
        self.dma("sync", self.IDEN[:], self.iden[:], "pcon", w=[("iden",)])
        self.dma("sync", self.ANEG[:], self.a_log_r[:], "pan", w=[("aneg",)])
        self.act(self.ANEG[:], self.ANEG[:], AF.Exp, r=[("aneg",)], w=[("aneg",)])
        self.ts(self.ANEG[:], self.ANEG[:], -1.0, None, ALU.mult, None,
                r=[("aneg",)], w=[("aneg",)])

    def ln_stats(self, st, src_ap, col, key_r):
        pf = st["pf"]
        self.ts(st["dv"][:, 0:src_ap.shape[1]], src_ap, 1.0, 0.0, ALU.mult, ALU.add,
                accum=st["s1"][:, col:col + 1],
                r=key_r, w=[(pf, "s1", col), (pf, "dumpv")])
        self.act(st["da"][:, 0:src_ap.shape[1]], src_ap, AF.Square,
                 accum=st["s2"][:, col:col + 1],
                 r=key_r, w=[(pf, "s2", col), (pf, "dumpa")])

    def ln_math(self, st, lo, n, want_r=False):
        pf = st["pf"]
        sl = slice(lo, lo + n)
        rs1 = [(pf, "s1", c) for c in range(lo, lo + n)]
        rs2 = [(pf, "s2", c) for c in range(lo, lo + n)]
        self.ts(st["negm"][:, sl], st["s1"][:, sl], -1.0 / D, None, ALU.mult, None,
                r=rs1, w=[(pf, "negm", lo)])
        self.ts(st["sa"][:, sl], st["s1"][:, sl], 1.0 / D, None, ALU.mult, None,
                r=rs1, w=[(pf, "sa", lo)])
        self.tt(st["sb"][:, sl], st["sa"][:, sl], st["sa"][:, sl], ALU.mult,
                r=[(pf, "sa", lo)], w=[(pf, "sb", lo)])
        self.stt(st["sv"][:, sl], st["s2"][:, sl], 1.0 / D, st["sb"][:, sl],
                 ALU.mult, ALU.subtract, r=rs2 + [(pf, "sb", lo)], w=[(pf, "sv", lo)])
        self.act(st["sw"][:, sl], st["sv"][:, sl], AF.Ln, bias=EPS,
                 r=[(pf, "sv", lo)], w=[(pf, "sw", lo)])
        self.act(st["rstd"][:, sl], st["sw"][:, sl], AF.Exp, scale=-0.5,
                 r=[(pf, "sw", lo)], w=[(pf, "rstd", lo)])
        if want_r:
            self.tt(st["aux"][:, sl], st["rstd"][:, sl], st["rstd"][:, sl], ALU.mult,
                    r=[(pf, "rstd", lo)], w=[(pf, "aux", lo)])
            self.stt(st["aux"][:, sl], st["sv"][:, sl], float(D), st["aux"][:, sl],
                     ALU.mult, ALU.mult,
                     r=[(pf, "sv", lo), (pf, "aux", lo)], w=[(pf, "aux", lo)])
            self.ts(st["aux"][:, sl], st["aux"][:, sl], 1e-20, None, ALU.max, None,
                    r=[(pf, "aux", lo)], w=[(pf, "aux", lo)])
            self.act(st["aux"][:, sl], st["aux"][:, sl], AF.Ln,
                     r=[(pf, "aux", lo)], w=[(pf, "aux", lo)])
            self.act(st["aux"][:, sl], st["aux"][:, sl], AF.Exp, scale=-0.5,
                     r=[(pf, "aux", lo)], w=[(pf, "aux", lo)])
            self.tt(st["aux2"][:, sl], st["rstd"][:, sl], st["aux"][:, sl], ALU.mult,
                    r=[(pf, "rstd", lo), (pf, "aux", lo)], w=[(pf, "aux2", lo)])

    def phaseA_merged(self):
        v = self.view
        self.AU = [v(BF16, DC * UW) for _ in range(2)]
        self.ADEL = v(BF16, DC * 512)
        self.AW = v(BF16, DC * 512)
        self.AY = v(FP16, DC * 512)
        self.AXFA = [v(BF16, FC * 512) for _ in range(2)]
        self.AX0 = [v(F32, 1024) for _ in range(4)]
        self.AXH = [v(BF16, 1024) for _ in range(4)]
        self.AXO = [v(F32, 1024) for _ in range(4)]
        self.ADR = v(BF16, 512)
        self.AXDB = [v(BF16, 512) for _ in range(2)]
        self.AE1 = v(F32, 512)
        self.ABBC = [v(BF16, 512) for _ in range(2)]
        self.ACBC = [v(BF16, 512) for _ in range(2)]
        self.AA2 = [v(FP16, 512) for _ in range(4)]
        self.AB2 = [v(BF16, 512) for _ in range(2)]
        self.AH2 = [v(BF16, 512) for _ in range(2)]
        self.AYC = [v(BF16, 512) for _ in range(2)]
        self.AACC = v(BF16, 512)
        self.AGR = [v(BF16, 512) for _ in range(8)]
        self.ATMP = [v(F32, 512) for _ in range(2)]
        self.x0cnt = 0
        self.xhcnt = 0
        self.a2c = 0
        self.rx0 = {}
        NSTEP = 2 * NCH

        self.vmemset(self.HST[:], 0.0,
                     w=[("hst", s, n, dc) for s in range(2)
                        for n in range(NST) for dc in range(DC)])
        for q in range(2):
            u3 = self.AU[q].rearrange("p (dc r) -> p dc r", dc=DC)
            self.vmemset(u3[:, :, UOF - 3:UOF], 0.0,
                         w=[("tl", q, dc) for dc in range(DC)])

        for j0 in (0, 1):
            self.frontA_LN(j0)
            self.frontA_dmats(j0, (0, 1))
            self.frontA_dmats(j0, (2, 3))
            self.frontA_win(j0)
            self.frontA_conv(j0)
        self.frontB_xdbl(0)
        self.frontB_delta(0, 0, 16)
        for j in range(NSTEP):
            nxt = j + 1 < NSTEP
            self.sweepA(j, 0, nxt)
            self.gateA(j, 0)
            if nxt and j + 1 >= 2:
                self.frontA_conv(j + 1)
            self.sweepA(j, 1, nxt)
            self.gateA(j, 8)
            if nxt:
                self.frontB_delta(j + 1, 8, 16)
            self.woutA_close(j)
            self.outLN_store(j)

    def autile(self, q, dc):
        return self.AU[q][:, dc * UW: dc * UW + UW]

    def adtile(self, dc):
        return self.ADEL[:, dc * 512:(dc + 1) * 512]

    def awtile(self, dc):
        return self.AW[:, dc * 512:(dc + 1) * 512]

    def aytile(self, dc):
        return self.AY[:, dc * 512:(dc + 1) * 512]

    def axfa(self, q, fc):
        return self.AXFA[q][:, fc * 512:(fc + 1) * 512]

    def frontA_LN(self, j):
        s, c = j % 2, j // 2
        q = j % 2
        lo = 4 * q
        st = self.STA
        pf = st["pf"]
        for sub in range(NSUB):
            g = c * NSUB + sub
            b = self.x0cnt % 4
            self.x0cnt += 1
            self.dma("sync", self.AX0[b][:], self.x_in[s, g * 128:(g + 1) * 128, :],
                     f"x0{b}", w=[("x0", b)])
            self.ln_stats(st, self.AX0[b][:], lo + sub, [("x0", b)])
        self.ln_math(st, lo, NSUB)
        for sub in range(NSUB):
            g = c * NSUB + sub
            b = self.x0cnt % 4
            self.x0cnt += 1
            rx = sub
            self.dma("sync", self.AX0[b][:], self.x_in[s, g * 128:(g + 1) * 128, :],
                     f"x0{b}", w=[("x0", b)])
            self.ts(self.AXH[rx], self.AX0[b][:],
                    st["negm"][:, lo + sub:lo + sub + 1],
                    st["rstd"][:, lo + sub:lo + sub + 1], ALU.add, ALU.mult,
                    r=[("x0", b), (pf, "negm", lo), (pf, "rstd", lo)],
                    w=[("xh", rx)])

    def frontA_dmats(self, j, subs):
        q = j % 2
        for sub in subs:
            rx = sub
            for fc in range(FC):
                self.dmat(self.axfa(q, fc)[:, sub * 128:(sub + 1) * 128],
                          self.AXH[rx][:, fc * 128:(fc + 1) * 128],
                          r=[("xh", rx)], w=[("xfa", q, fc, sub)])

    def frontA_win(self, j):
        q = j % 2
        xfar = [("xfa", q, fc, sub) for fc in range(FC) for sub in range(NSUB)]
        for oc in range(DC):
            p = oc % 2
            wb = self.WSTR[:, p * 1024:p * 1024 + 1024]
            self.dma("sync", wb, self.win_p[oc], f"wb{p}",
                     r=[("win_p", jj) for jj in range(4)], w=[("wbd", p)])
            pb = self.PS[p]
            for fc in range(FC):
                self.mm(pb[:], wb[:, fc * 128:(fc + 1) * 128], self.axfa(q, fc),
                        start=(fc == 0), stop=(fc == FC - 1),
                        r=[("wbd", p)] + xfar, w=[("ps", p)])
            self.act(self.autile(q, oc)[:, UOF:UOF + 512], pb[:], AF.Copy,
                     r=[("ps", p)], w=[("u", q, oc), ("ps", p)])

    def frontA_conv(self, j):
        q = j % 2
        for dc in range(DC):
            ut = self.autile(q, dc)
            self.ts(self.AACC, ut[:, UOF - 3:UOF + 509],
                    self.CONVW[:, dc * 4:dc * 4 + 1], None, ALU.mult, None,
                    r=[("u", q, dc), ("tl", q, dc), ("convw",)], w=[("acc",)])
            for k in range(1, 4):
                self.stt(self.AACC, ut[:, UOF - 3 + k:UOF + 509 + k],
                         self.CONVW[:, dc * 4 + k:dc * 4 + k + 1], self.AACC,
                         ALU.mult, ALU.add,
                         r=[("u", q, dc), ("tl", q, dc), ("acc",), ("convw",)],
                         w=[("acc",)])
            self.gcopy(ut[:, UOF - 3:UOF], ut[:, UOF + 509:UOF + 512],
                       r=[("u", q, dc)], w=[("tl", q, dc)])
            self.act(ut[:, UOF:UOF + 512], self.AACC, AF.Silu,
                     bias=self.CONVB[:, dc:dc + 1],
                     r=[("acc",), ("convb",)], w=[("u", q, dc)])

    def frontB_xdbl(self, j):
        q = j % 2
        px = self.PS[2]
        for kc in range(DC):
            self.mm(px[0:96, :], self.WX[:, kc * 96:(kc + 1) * 96],
                    self.autile(q, kc)[:, UOF:UOF + 512],
                    start=(kc == 0), stop=(kc == DC - 1),
                    r=[("u", q, kc), ("wx",)], w=[("ps", 2)])
        self.act(self.ADR[0:64, :], px[0:64, :], AF.Copy,
                 r=[("ps", 2)], w=[("dr",)])
        self.act(self.AXDB[q][64:96, :], px[64:96, :], AF.Copy,
                 r=[("ps", 2)], w=[("xdb", q), ("ps", 2)])

    def frontB_delta(self, j, d0, d1):
        q = j % 2
        for dc in range(d0, d1):
            pi = 3 if dc % 2 == 0 else 2
            pd = self.PS[pi]
            self.mm(pd[:], self.WDT[:, dc * 128:(dc + 1) * 128], self.ADR[0:64, :],
                    start=True, stop=True,
                    r=[("dr",), ("wdt",)], w=[("ps", pi)])
            self.act(self.AE1, pd[:], AF.Exp, bias=self.BDT[:, dc:dc + 1],
                     r=[("ps", pi), ("bdt",)], w=[("e1",), ("ps", pi)])
            self.act(self.adtile(dc), self.AE1, AF.Ln, bias=1.0,
                     r=[("e1",)], w=[("del", dc)])
            self.tt(self.awtile(dc), self.adtile(dc),
                    self.autile(q, dc)[:, UOF:UOF + 512], ALU.mult,
                    r=[("del", dc), ("u", q, dc)], w=[("w", dc)])

    def resA(self, j, base):
        q = j % 2
        xfar = [("xfa", q, fc, sub) for fc in range(FC) for sub in range(NSUB)]
        for k in range(8):
            oc = DC + base + k
            p = oc % 2
            wb = self.WSTR[:, p * 1024:p * 1024 + 1024]
            self.dma("sync", wb, self.win_p[oc], f"wb{p}",
                     r=[("win_p", jj) for jj in range(4)], w=[("wbd", p)])
            pb = self.PS[p]
            for fc in range(FC):
                self.mm(pb[:], wb[:, fc * 128:(fc + 1) * 128], self.axfa(q, fc),
                        start=(fc == 0), stop=(fc == FC - 1),
                        r=[("wbd", p)] + xfar, w=[("ps", p)])
            self.act(self.AGR[k], pb[:], AF.Silu,
                     r=[("ps", p)], w=[("gr", k), ("ps", p)])

    def gateA(self, j, base):
        q = j % 2
        for k in range(8):
            dc = base + k
            p = dc % 2
            self.stt(self.ATMP[p], self.autile(q, dc)[:, UOF:UOF + 512],
                     self.DCOL[:, dc:dc + 1], self.aytile(dc), ALU.mult, ALU.add,
                     r=[("u", q, dc), ("dcol",), ("y", dc)], w=[("tmp", p)])
            self.tt(self.autile(q, dc)[:, UOF:UOF + 512], self.ATMP[p],
                    self.AGR[k], ALU.mult,
                    r=[("tmp", p), ("gr", k), ("u", q, dc)], w=[("u", q, dc)])

    def prefX0(self, j):
        s, c = j % 2, j // 2
        bl = []
        for sub in range(NSUB):
            g = c * NSUB + sub
            b = self.x0cnt % 4
            self.x0cnt += 1
            self.dma("sync", self.AX0[b][:], self.x_in[s, g * 128:(g + 1) * 128, :],
                     f"x0{b}", w=[("x0", b)])
            bl.append(b)
        self.rx0[j] = bl

    WOBANK = (0, 1, 6, 7)

    def woutA_chain(self, j, half, d0, d1, stop_end):
        q = j % 2
        for dc in range(d0, d1):
            wsl = dc % 2
            wb = self.WSTR[:, 2048 + wsl * 512:2048 + wsl * 512 + 512]
            self.dma("sync", wb,
                     self.wout_p[dc * 128:(dc + 1) * 128,
                                 half * 512:(half + 1) * 512],
                     f"wo{wsl}",
                     r=[("wout_p", dc // 4)], w=[("wod", wsl)])
            for sub in range(NSUB):
                bk = self.WOBANK[sub]
                self.mm(self.PS[bk][:],
                        self.autile(q, dc)[:, UOF + sub * 128:UOF + (sub + 1) * 128],
                        wb[:, 0:512],
                        start=(dc == 0), stop=(stop_end and dc == d1 - 1),
                        r=[("wod", wsl), ("u", q, dc)], w=[("ps", bk)])

    def woutA_evac(self, j, half):
        for sub in range(NSUB):
            bk = self.WOBANK[sub]
            self.act(self.AXO[sub][:, half * 512:(half + 1) * 512],
                     self.PS[bk][:], AF.Copy,
                     r=[("ps", bk)], w=[("xo", sub), ("ps", bk)])

    def woutA_close(self, j):
        self.woutA_chain(j, 0, 0, 16, True)
        self.woutA_evac(j, 0)
        self.woutA_chain(j, 1, 0, 16, True)
        self.woutA_evac(j, 1)

    def outLN_store(self, j):
        s, c = j % 2, j // 2
        q = j % 2
        st = self.STA
        pf = st["pf"]
        lo = 8 + 4 * q
        if _A_ONLY:
            for sub in range(NSUB):
                g = c * NSUB + sub
                self.dma("gpsimd", self.out[s, g * 128:(g + 1) * 128, :],
                         self.AXO[sub][:], "out", r=[("xo", sub)],
                         w=[("odbg", s, g)])
        for sub in range(NSUB):
            self.ln_stats(st, self.AXO[sub][:], lo + sub, [("xo", sub)])
        self.ln_math(st, lo, NSUB)
        for sub in range(NSUB):
            g = c * NSUB + sub
            b = self.rx0[j][sub]
            for half in range(2):
                hp = half
                self.ts(self.ATMP[hp],
                        self.AXO[sub][:, half * 512:(half + 1) * 512],
                        st["negm"][:, lo + sub:lo + sub + 1],
                        st["rstd"][:, lo + sub:lo + sub + 1], ALU.add, ALU.mult,
                        r=[("xo", sub), (pf, "negm", lo), (pf, "rstd", lo)],
                        w=[("tmp", hp)])
                self.tt(self.AXO[sub][:, half * 512:(half + 1) * 512],
                        self.AX0[b][:, half * 512:(half + 1) * 512],
                        self.ATMP[hp], ALU.add,
                        r=[("x0", b), ("tmp", hp)], w=[("xo", sub)])
            self.dma("gpsimd", self.xres_h[s, g * 128:(g + 1) * 128, :],
                     self.AXO[sub][:], "xw",
                     r=[("xo", sub)], w=[("xrh", s, g)])

    def sweepA(self, j, sh, nxt):
        s = j % 2
        q = j % 2
        d0 = 8 * sh
        for n in range(NST):
            if sh == 0:
                if nxt and j + 1 >= 2:
                    if n == 4:
                        self.frontA_LN(j + 1)
                        self.frontA_dmats(j + 1, (0, 1))
                    elif n == 6:
                        self.frontA_dmats(j + 1, (2, 3))
                    elif n == 8:
                        self.frontA_win(j + 1)
                if n == 11:
                    self.resA(j, 0)
            else:
                if n == 2:
                    self.resA(j, 8)
                if nxt:
                    if n == 4:
                        self.frontB_xdbl(j + 1)
                    elif n == 6:
                        self.frontB_delta(j + 1, 0, 8)
                if n == 8:
                    self.prefX0(j)
            nb = n % 2
            self.mm(self.PS[4][:], self.SELB[64:96, n * 128:(n + 1) * 128],
                    self.AXDB[q][64:96, :], start=True, stop=True,
                    r=[("xdb", q), ("selb",)], w=[("ps", 4)])
            self.act(self.ABBC[nb], self.PS[4][:], AF.Copy,
                     r=[("ps", 4)], w=[("bbc", nb), ("ps", 4)])
            self.mm(self.PS[5][:], self.SELC[64:96, n * 128:(n + 1) * 128],
                    self.AXDB[q][64:96, :], start=True, stop=True,
                    r=[("xdb", q), ("selc",)], w=[("ps", 5)])
            self.vcopy(self.ACBC[nb], self.PS[5][:],
                       r=[("ps", 5)], w=[("cbc", nb), ("ps", 5)])
            for dc in range(d0, d0 + 8):
                p = dc % 2
                a2 = self.a2c % 4
                self.a2c += 1
                col = s * 256 + dc * 16 + n
                acol = dc * 16 + n
                self.act(self.AA2[a2], self.adtile(dc), AF.Exp,
                         scale=self.ANEG[:, acol:acol + 1],
                         r=[("del", dc), ("aneg",)], w=[("a2", a2)])
                self.tt(self.AB2[p], self.awtile(dc), self.ABBC[nb], ALU.mult,
                        r=[("w", dc), ("bbc", nb)], w=[("b2", p)])
                self.scan(self.AH2[p], self.AA2[a2], self.AB2[p],
                          self.HST[:, col:col + 1],
                          r=[("a2", a2), ("b2", p), ("hst", s, n, dc)],
                          w=[("h2", p)])
                self.gcopy(self.HST[:, col:col + 1], self.AH2[p][:, 511:512],
                           r=[("h2", p)], w=[("hst", s, n, dc)])
                if n == 0:
                    self.tt(self.aytile(dc), self.AH2[p], self.ACBC[nb], ALU.mult,
                            r=[("h2", p), ("cbc", nb)], w=[("y", dc)])
                else:
                    self.tt(self.AYC[p], self.AH2[p], self.ACBC[nb], ALU.mult,
                            r=[("h2", p), ("cbc", nb)], w=[("yc", p)])
                    self.tt(self.aytile(dc), self.aytile(dc), self.AYC[p], ALU.add,
                            r=[("yc", p), ("y", dc)], w=[("y", dc)])

    def phaseB_merged(self):
        v = self.view
        BXIN = [v(F32, 1024) for _ in range(3)]
        BXFB = [v(BF16, FC * 512) for _ in range(2)]
        BZ = [v(F32, NSUB * 1024) for _ in range(2)]
        BXH = [v(BF16, 1024) for _ in range(2)]
        BT = [v(BF16, 512) for _ in range(2)]
        BG = [v(BF16, 512) for _ in range(2)]
        BTMP = [v(F32, 512) for _ in range(2)]
        st = self.STB
        pf = st["pf"]
        bx = [0]
        xfb = lambda qb, fc: BXFB[qb][:, fc * 512:(fc + 1) * 512]

        for i in range(2 * NCH):
            s, c = i % 2, i // 2
            qb = i % 2
            lo = 4 * qb
            loz = 8 + 4 * qb
            for sub in range(NSUB):
                g = c * NSUB + sub
                b = bx[0] % 3
                bx[0] += 1
                self.dma("sync", BXIN[b][:], self.xres_h[s, g * 128:(g + 1) * 128, :],
                         f"bx{b}", r=[("xrh", s, g)], w=[("bxin", b)])
                self.ln_stats(st, BXIN[b][:], lo + sub, [("bxin", b)])
            self.ln_math(st, lo, NSUB)
            for sub in range(NSUB):
                g = c * NSUB + sub
                b = bx[0] % 3
                bx[0] += 1
                rx = sub % 2
                self.dma("sync", BXIN[b][:], self.xres_h[s, g * 128:(g + 1) * 128, :],
                         f"bx{b}", r=[("xrh", s, g)], w=[("bxin", b)])
                self.ts(BXH[rx], BXIN[b][:],
                        st["negm"][:, lo + sub:lo + sub + 1],
                        st["rstd"][:, lo + sub:lo + sub + 1], ALU.add, ALU.mult,
                        r=[("bxin", b), (pf, "negm", lo), (pf, "rstd", lo)],
                        w=[("bxh", rx)])
                for fc in range(FC):
                    self.dmat(xfb(qb, fc)[:, sub * 128:(sub + 1) * 128],
                              BXH[rx][:, fc * 128:(fc + 1) * 128],
                              r=[("bxh", rx)], w=[("xfb", qb, fc, sub)])
            xfbr = [("xfb", qb, fc, sub) for fc in range(FC) for sub in range(NSUB)]

            for fc in range(FC):
                self.mm(self.PS[0][:], self.UTS[:, fc * 128:(fc + 1) * 128],
                        xfb(qb, fc), start=(fc == 0), stop=(fc == FC - 1),
                        r=[("uts",)] + xfbr, w=[("ps", 0)])
            self.vcopy(BT[qb], self.PS[0][:],
                       r=[("ps", 0)], w=[("texp", qb), ("ps", 0)])
            for sub in range(NSUB):
                for fc in range(FC):
                    self.mm(self.PS[1][:, 0:8],
                            xfb(qb, fc)[:, sub * 128:(sub + 1) * 128],
                            self.GATET[:, fc * 8:(fc + 1) * 8],
                            start=(fc == 0), stop=(fc == FC - 1),
                            r=[("gatet",)] + xfbr, w=[("ps", 1)])
                self.rmax(self.NM[:], self.PS[1][:, 0:8], True,
                          r=[("ps", 1)], w=[("nm",)])
                self.act(self.GE[:], self.PS[1][:, 0:8], AF.Exp,
                         bias=self.NM[:, 0:1],
                         r=[("ps", 1), ("nm",)], w=[("ge",), ("ps", 1)])
                self.rsumf(self.GS[:, 0:1], self.GE[:], r=[("ge",)], w=[("gs",)])
                self.ts(self.GS[:, 1:2], self.GS[:, 0:1], 16.0, None, ALU.mult,
                        None, r=[("gs",)], w=[("gs16",)])
                self.recip(self.GS[:, 0:1], self.GS[:, 1:2],
                           r=[("gs16",)], w=[("gs",)])
                self.ts(self.GSUB[:], self.GE[:], self.GS[:, 0:1], None, ALU.mult,
                        None, r=[("ge",), ("gs",)], w=[("gsub",)])
                pgt = self.PS[2].ap().bitcast(BF16)[0:8, 0:128]
                self.tpose(pgt, self.GSUB[:],
                           r=[("gsub",), ("iden",)], w=[("ps", 2)])
                self.act(self.GFM[:, sub * 128:(sub + 1) * 128], pgt, AF.Copy,
                         r=[("ps", 2)], w=[("gfm", sub), ("ps", 2)])
            self.mm(self.PS[3][:], self.REP8[:], self.GFM[:], start=True, stop=True,
                    r=[("rep8",)] + [("gfm", ii) for ii in range(NSUB)],
                    w=[("ps", 3)])
            self.vcopy(BG[qb], self.PS[3][:], r=[("ps", 3)], w=[("bg", qb), ("ps", 3)])
            self.tt(BT[qb], BT[qb], BG[qb], ALU.mult,
                    r=[("texp", qb), ("bg", qb)], w=[("texp", qb)])

            for half in range(2):
                for fc in range(FC):
                    p = fc % 2
                    wb = self.WSTR[:, 3072 + p * 512:3072 + p * 512 + 512]
                    self.dma("sync", wb,
                             self.w0t_p[fc * 128:(fc + 1) * 128,
                                        half * 512:(half + 1) * 512],
                             f"wB{p}",
                             r=[("w0t_p", fc // 2)], w=[("wbB", p)])
                    for sub in range(NSUB):
                        self.mm(self.PS[4 + sub][:],
                                xfb(qb, fc)[:, sub * 128:(sub + 1) * 128],
                                wb[:, 0:512],
                                start=(fc == 0), stop=False,
                                r=[("wbB", p)] + xfbr, w=[("ps", 4 + sub)])
                for sub in range(NSUB):
                    self.mm(self.PS[4 + sub][:],
                            BT[qb][:, sub * 128:(sub + 1) * 128],
                            self.VSB[:, half * 512:(half + 1) * 512],
                            start=False, stop=True,
                            r=[("texp", qb), ("vsb",)], w=[("ps", 4 + sub)])
                for sub in range(NSUB):
                    self.act(BZ[qb][:, sub * 1024 + half * 512:
                                    sub * 1024 + (half + 1) * 512],
                             self.PS[4 + sub][:], AF.Copy,
                             r=[("ps", 4 + sub)],
                             w=[("bz", qb, sub), ("ps", 4 + sub)])
            for sub in range(NSUB):
                self.ln_stats(st, BZ[qb][:, sub * 1024:(sub + 1) * 1024], loz + sub,
                              [("bz", qb, sub)])
            self.ln_math(st, loz, NSUB)
            for sub in range(NSUB):
                g = c * NSUB + sub
                b = bx[0] % 3
                bx[0] += 1
                self.dma("sync", BXIN[b][:], self.xres_h[s, g * 128:(g + 1) * 128, :],
                         f"bx{b}", r=[("xrh", s, g)], w=[("bxin", b)])
                for half in range(2):
                    hp = half
                    self.ts(BTMP[hp], BZ[qb][:, sub * 1024 + half * 512:
                                              sub * 1024 + (half + 1) * 512],
                            st["negm"][:, loz + sub:loz + sub + 1],
                            st["rstd"][:, loz + sub:loz + sub + 1],
                            ALU.add, ALU.mult,
                            r=[("bz", qb, sub), (pf, "negm", loz), (pf, "rstd", loz)],
                            w=[("btmp", hp)])
                    self.tt(BXIN[b][:, half * 512:(half + 1) * 512],
                            BXIN[b][:, half * 512:(half + 1) * 512],
                            BTMP[hp], ALU.add,
                            r=[("btmp", hp)], w=[("bxin", b)])
                self.dma("gpsimd", self.xres_h[s, g * 128:(g + 1) * 128, :],
                         BXIN[b][:], "xw",
                         r=[("bxin", b)], w=[("xrh", s, g)])

    def phaseC(self, s):
        pf = ("C", s)
        st = self.STA
        spf = st["pf"]
        v = self.view
        XHT = v(BF16, TS * D)
        XFC = v(BF16, FC * SEQ)
        ES = v(BF16, TS * 512)
        AO = [v(F32, D) for _ in range(2)]
        XT = [v(BF16, D) for _ in range(2)]
        TMP = [v(F32, 512) for _ in range(2)]
        CXI = [v(F32, 1024) for _ in range(3)]
        CXR = [v(F32, 1024) for _ in range(2)]
        xht = lambda g: XHT[:, g * D:(g + 1) * D]
        xfc = lambda fc: XFC[:, fc * SEQ:(fc + 1) * SEQ]
        es = lambda kc: ES[:, kc * 512:(kc + 1) * 512]
        cx = [0]

        for g in range(TS):
            b = cx[0] % 3
            cx[0] += 1
            self.dma("sync", CXI[b][:], self.xres_h[s, g * 128:(g + 1) * 128, :],
                     f"cx{b}", r=[("xrh", s, g)], w=[("cxi", b)])
            self.ln_stats(st, CXI[b][:], g, [("cxi", b)])
        self.ln_math(st, 0, TS, want_r=True)
        for g in range(TS):
            b = cx[0] % 3
            cx[0] += 1
            p = g % 2
            self.dma("sync", CXI[b][:], self.xres_h[s, g * 128:(g + 1) * 128, :],
                     f"cx{b}", r=[("xrh", s, g)], w=[("cxi", b)])
            self.ts(xht(g), CXI[b][:],
                    st["negm"][:, g:g + 1], st["rstd"][:, g:g + 1],
                    ALU.add, ALU.mult,
                    r=[("cxi", b), (spf, "negm", 0), (spf, "rstd", 0)],
                    w=[(pf, "xht", g)])
            self.ts(XT[p], CXI[b][:],
                    st["negm"][:, g:g + 1], st["aux2"][:, g:g + 1],
                    ALU.add, ALU.mult,
                    r=[("cxi", b), (spf, "negm", 0), (spf, "aux2", 0)],
                    w=[(pf, "xt", p)])
            for fc in range(FC):
                self.dmat(xfc(fc)[:, g * 128:(g + 1) * 128],
                          XT[p][:, fc * 128:(fc + 1) * 128],
                          r=[(pf, "xt", p)], w=[(pf, "xfc", fc, g)])

        for qb in range(NCH):
            q0 = qb * 512
            xfcq = [(pf, "xfc", fc, qb * 4 + i) for fc in range(FC) for i in range(4)]
            for kc in range(TS):
                pq = kc % 2
                for fc in range(FC):
                    self.mm(self.PS[pq][:], xfc(fc)[:, kc * 128:(kc + 1) * 128],
                            xfc(fc)[:, q0:q0 + 512],
                            start=(fc == 0), stop=(fc == FC - 1),
                            r=[(pf, "xfc", fc, kc)] + xfcq, w=[("ps", pq)])
                self.act(es(kc), self.PS[pq][:], AF.Exp,
                         r=[("ps", pq)], w=[(pf, "es", kc), ("ps", pq)])
            for qs in range(NSUB):
                gq = qb * NSUB + qs
                i0, i1 = 2 + 2 * (qs % 2), 3 + 2 * (qs % 2)
                for kc in range(TS):
                    l = es(kc)[:, qs * 128:(qs + 1) * 128]
                    self.mm(self.PS[i0][:], l, xht(kc)[:, 0:512],
                            start=(kc == 0), stop=(kc == TS - 1),
                            r=[(pf, "es", kc), (pf, "xht", kc)], w=[("ps", i0)])
                    self.mm(self.PS[i1][:], l, xht(kc)[:, 512:1024],
                            start=(kc == 0), stop=(kc == TS - 1),
                            r=[(pf, "es", kc), (pf, "xht", kc)], w=[("ps", i1)])
                    self.mm(self.PS[6][:, 0:1], l, self.ONES1[:],
                            start=(kc == 0), stop=(kc == TS - 1),
                            r=[(pf, "es", kc), ("ones1",)], w=[("ps", 6)])
                self.recip(self.RINV[:, qs:qs + 1], self.PS[6][:, 0:1],
                           r=[("ps", 6)], w=[(pf, "rinv", qs), ("ps", 6)])
                pa = qs % 2
                for half, pi in ((0, i0), (1, i1)):
                    self.stt(AO[pa][:, half * 512:(half + 1) * 512],
                             self.PS[pi][:], self.RINV[:, qs:qs + 1],
                             xht(gq)[:, half * 512:(half + 1) * 512],
                             ALU.mult, ALU.add,
                             r=[("ps", pi), (pf, "rinv", qs), (pf, "xht", gq)],
                             w=[(pf, "ao", pa), ("ps", pi)])
                self.ln_stats(st, AO[pa][:], qs, [(pf, "ao", pa)])
                self.ln_math(st, qs, 1)
                b2 = cx[0] % 2
                cx[0] += 1
                self.dma("sync", CXR[b2][:],
                         self.xres_h[s, gq * 128:(gq + 1) * 128, :],
                         f"cr{b2}", r=[("xrh", s, gq)], w=[("cxr", b2)])
                for half in range(2):
                    self.ts(TMP[half], AO[pa][:, half * 512:(half + 1) * 512],
                            st["negm"][:, qs:qs + 1], st["rstd"][:, qs:qs + 1],
                            ALU.add, ALU.mult,
                            r=[(pf, "ao", pa), (spf, "negm", qs), (spf, "rstd", qs)],
                            w=[(pf, "tmp", half)])
                    self.tt(CXR[b2][:, half * 512:(half + 1) * 512],
                            CXR[b2][:, half * 512:(half + 1) * 512],
                            TMP[half], ALU.add,
                            r=[(pf, "tmp", half)], w=[("cxr", b2)])
                self.dma("gpsimd", self.xres_h[s, gq * 128:(gq + 1) * 128, :],
                         CXR[b2][:], "xw",
                         r=[("cxr", b2)], w=[("xrh", s, gq)])

    def phaseD(self, s):
        pf = ("D", s)
        st = self.STA
        spf = st["pf"]
        v = self.view
        XFD = v(BF16, FC * 544)
        XR = v(BF16, FC * 512)
        XK = v(BF16, FC * 512)
        KACT = v(BF16, FC * 512)
        RT = v(BF16, NSUB * D)
        VR = v(F32, NSUB * D)
        WS2 = v(BF16, FC * 512)
        XH = [v(BF16, D) for _ in range(2)]
        DD = v(BF16, 512)
        KR = v(BF16, 512)
        XO = [v(F32, D) for _ in range(2)]
        TMP = [v(F32, 512) for _ in range(2)]
        DXI = [v(F32, 1024) for _ in range(3)]
        xfd3 = XFD.rearrange("p (fc r) -> p fc r", fc=FC)
        xfd = lambda fc: XFD[:, fc * 544:(fc + 1) * 544]
        xr = lambda fc: XR[:, fc * 512:(fc + 1) * 512]
        xk = lambda fc: XK[:, fc * 512:(fc + 1) * 512]
        ka = lambda dcb: KACT[:, dcb * 512:(dcb + 1) * 512]
        dx = [0]

        self.vmemset(xfd3[:, :, 15:16], 0.0, w=[(pf, "bnd")])

        for c in range(NCH):
            xfdr = [(pf, "xfd", fc, sub) for fc in range(FC) for sub in range(NSUB)]
            if c > 0:
                self.vcopy(xfd3[:, :, 15:16], xfd3[:, :, 527:528],
                           r=xfdr, w=[(pf, "bnd")])
            for sub in range(NSUB):
                g = c * NSUB + sub
                b = dx[0] % 3
                dx[0] += 1
                self.dma("sync", DXI[b][:], self.xres_h[s, g * 128:(g + 1) * 128, :],
                         f"dx{b}", r=[("xrh", s, g)], w=[("dxi", b)])
                self.ln_stats(st, DXI[b][:], sub, [("dxi", b)])
            self.ln_math(st, 0, NSUB)
            for sub in range(NSUB):
                g = c * NSUB + sub
                b = dx[0] % 3
                dx[0] += 1
                p = sub % 2
                self.dma("sync", DXI[b][:], self.xres_h[s, g * 128:(g + 1) * 128, :],
                         f"dx{b}", r=[("xrh", s, g)], w=[("dxi", b)])
                self.ts(XH[p], DXI[b][:],
                        st["negm"][:, sub:sub + 1], st["rstd"][:, sub:sub + 1],
                        ALU.add, ALU.mult,
                        r=[("dxi", b), (spf, "negm", 0), (spf, "rstd", 0)],
                        w=[(pf, "xh", p)])
                for fc in range(FC):
                    self.dmat(xfd(fc)[:, 16 + sub * 128:16 + (sub + 1) * 128],
                              XH[p][:, fc * 128:(fc + 1) * 128],
                              r=[(pf, "xh", p), (pf, "bnd")],
                              w=[(pf, "xfd", fc, sub)])
            for fc in range(FC):
                cur = xfd(fc)[:, 16:528]
                prv = xfd(fc)[:, 15:527]
                self.tt(DD, cur, prv, ALU.subtract,
                        r=xfdr + [(pf, "bnd")], w=[(pf, "dd")])
                self.stt(xr(fc), DD, self.MUR[:, fc:fc + 1], prv, ALU.mult, ALU.add,
                         r=[(pf, "dd"), ("mur",)] + xfdr + [(pf, "bnd")],
                         w=[(pf, "xr", fc)])
                self.stt(xk(fc), DD, self.MUK[:, fc:fc + 1], prv, ALU.mult, ALU.add,
                         r=[(pf, "dd"), ("muk",)] + xfdr + [(pf, "bnd")],
                         w=[(pf, "xk", fc)])
            xrr = [(pf, "xr", fc) for fc in range(FC)]
            xkr = [(pf, "xk", fc) for fc in range(FC)]
            for dcb in range(FC):
                p = dcb % 2
                wb = self.WSTR[:, p * 1024:p * 1024 + 1024]
                self.dma("sync", wb, self.wkt_p[dcb], f"wb{p}",
                         r=[("wkt_p", j) for j in range(4)], w=[("wbd", p)])
                for fc in range(FC):
                    self.mm(self.PS[p][:], wb[:, fc * 128:(fc + 1) * 128], xk(fc),
                            start=(fc == 0), stop=(fc == FC - 1),
                            r=[("wbd", p)] + xkr, w=[("ps", p)])
                self.act(KR, self.PS[p][:], AF.Relu,
                         r=[("ps", p)], w=[(pf, "kr"), ("ps", p)])
                self.tt(ka(dcb), KR, KR, ALU.mult, r=[(pf, "kr")], w=[(pf, "ka", dcb)])
            for half in range(2):
                for fc in range(FC):
                    self.dma("sync", WS2[:, fc * 512:(fc + 1) * 512],
                             self.wrt_p[fc * 128:(fc + 1) * 128,
                                        half * 512:(half + 1) * 512],
                             "ws2",
                             r=[("wrt_p", fc // 2)], w=[(pf, "ws2d", fc)])
                for sub in range(NSUB):
                    for fc in range(FC):
                        self.mm(self.PS[2 + sub][:],
                                xr(fc)[:, sub * 128:(sub + 1) * 128],
                                WS2[:, fc * 512:(fc + 1) * 512],
                                start=(fc == 0), stop=(fc == FC - 1),
                                r=[(pf, "ws2d", fc)] + xrr, w=[("ps", 2 + sub)])
                    self.act(RT[:, sub * D + half * 512:sub * D + (half + 1) * 512],
                             self.PS[2 + sub][:], AF.Sigmoid,
                             r=[("ps", 2 + sub)],
                             w=[(pf, "rt", sub), ("ps", 2 + sub)])
            for half in range(2):
                for dcb in range(FC):
                    self.dma("sync", WS2[:, dcb * 512:(dcb + 1) * 512],
                             self.wvt_p[dcb * 128:(dcb + 1) * 128,
                                        half * 512:(half + 1) * 512],
                             "ws2",
                             r=[("wvt_p", dcb // 2)], w=[(pf, "ws2d", dcb)])
                for sub in range(NSUB):
                    for dcb in range(FC):
                        self.mm(self.PS[2 + sub][:],
                                ka(dcb)[:, sub * 128:(sub + 1) * 128],
                                WS2[:, dcb * 512:(dcb + 1) * 512],
                                start=(dcb == 0), stop=(dcb == FC - 1),
                                r=[(pf, "ws2d", dcb), (pf, "ka", dcb)],
                                w=[("ps", 2 + sub)])
                    self.tt(VR[:, sub * D + half * 512:sub * D + (half + 1) * 512],
                            self.PS[2 + sub][:],
                            RT[:, sub * D + half * 512:sub * D + (half + 1) * 512],
                            ALU.mult,
                            r=[("ps", 2 + sub), (pf, "rt", sub)],
                            w=[(pf, "vr", sub), ("ps", 2 + sub)])
            for sub in range(NSUB):
                self.ln_stats(st, VR[:, sub * D:(sub + 1) * D], sub,
                              [(pf, "vr", sub)])
            self.ln_math(st, 0, NSUB)
            for sub in range(NSUB):
                g = c * NSUB + sub
                b = dx[0] % 3
                dx[0] += 1
                p = sub % 2
                self.dma("sync", DXI[b][:], self.xres_h[s, g * 128:(g + 1) * 128, :],
                         f"dx{b}", r=[("xrh", s, g)], w=[("dxi", b)])
                for half in range(2):
                    self.ts(TMP[half],
                            VR[:, sub * D + half * 512:sub * D + (half + 1) * 512],
                            st["negm"][:, sub:sub + 1], st["rstd"][:, sub:sub + 1],
                            ALU.add, ALU.mult,
                            r=[(pf, "vr", sub), (spf, "negm", 0), (spf, "rstd", 0)],
                            w=[(pf, "tmp", half)])
                    self.tt(XO[p][:, half * 512:(half + 1) * 512],
                            DXI[b][:, half * 512:(half + 1) * 512],
                            TMP[half], ALU.add,
                            r=[(pf, "tmp", half), ("dxi", b)], w=[(pf, "xo", p)])
                self.dma("sync", self.out[s, g * 128:(g + 1) * 128, :], XO[p][:],
                         "out", r=[(pf, "xo", p)], w=[("out", s, g)])


def _layouts(i):
    def col16(v):
        return np.ascontiguousarray(v.reshape(16, 128).T)
    out = {}
    out["conv_w_r"] = np.ascontiguousarray(
        i["L0_attn_conv_w"].reshape(16, 128, 4).transpose(1, 0, 2).reshape(128, 64))
    out["conv_b_r"] = col16(i["L0_attn_conv_b"])
    out["b_dt_r"] = col16(i["L0_attn_b_dt"])
    out["d_r"] = col16(i["L0_attn_D"])
    out["a_log_r"] = np.ascontiguousarray(
        i["L0_attn_A_log"].reshape(16, 128, 16).transpose(1, 0, 2).reshape(128, 256))
    out["w_in"] = np.ascontiguousarray(
        i["L0_attn_W_in"].reshape(8, 128, 32, 128).transpose(2, 1, 0, 3)
        .reshape(32, 128, 1024))
    out["w_x"] = np.ascontiguousarray(
        i["L0_attn_W_x"].reshape(16, 128, 96).transpose(1, 0, 2).reshape(128, 16 * 96))
    out["w_dt"] = np.ascontiguousarray(i["L0_attn_W_dt"])
    out["w_out"] = np.ascontiguousarray(i["L0_attn_W_out"])
    out["w0t"] = np.ascontiguousarray(i["L0_ffn_weight"].T)
    out["gatet"] = np.ascontiguousarray(
        i["L0_ffn_gate_w"].T.reshape(8, 128, 8).transpose(1, 0, 2).reshape(128, 64))
    out["ut"] = np.ascontiguousarray(
        i["L0_ffn_lora_U"].reshape(128, 1024).T.reshape(8, 128, 128)
        .transpose(1, 0, 2).reshape(128, 8 * 128))
    out["v_r"] = np.ascontiguousarray(
        i["L0_ffn_lora_V"].transpose(0, 2, 1).reshape(128, 1024))
    out["wrt"] = np.ascontiguousarray(i["L1_ffn_W_r"].T)
    out["wkt"] = np.ascontiguousarray(
        i["L1_ffn_W_k"].T.reshape(8, 128, 8, 128).transpose(2, 1, 0, 3)
        .reshape(8, 128, 1024))
    out["wvt"] = np.ascontiguousarray(i["L1_ffn_W_v"].T)
    out["mur_r"] = np.ascontiguousarray(i["L1_ffn_mu_r"].reshape(8, 128).T)
    out["muk_r"] = np.ascontiguousarray(i["L1_ffn_mu_k"].reshape(8, 128).T)
    bf = ml_dtypes.bfloat16
    selb = np.zeros((32, NST * 128), np.float32)
    selc = np.zeros((32, NST * 128), np.float32)
    for n in range(NST):
        selb[n, n * 128:(n + 1) * 128] = 1.0
        selc[16 + n, n * 128:(n + 1) * 128] = 1.0
    out["selb"] = selb.astype(bf)
    out["selc"] = selc.astype(bf)
    rep = np.zeros((8, 128), np.float32)
    for e in range(8):
        rep[e, e * 16:(e + 1) * 16] = 1.0
    out["rep8"] = rep.astype(bf)
    out["ones1"] = np.ones((128, 1), np.float32).astype(bf)
    out["iden"] = np.eye(128, dtype=np.float32).astype(bf)
    return out


_NC_CACHE = None


_BUILDER = None


def _get_nc():
    global _NC_CACHE, _BUILDER
    if _NC_CACHE is None:
        _BUILDER = Builder()
        _NC_CACHE = _BUILDER.build()
    return _NC_CACHE


def kernel(**inputs):
    x = np.ascontiguousarray(np.asarray(inputs["L0_attn_x"], dtype=np.float32))
    shared = _layouts({k: np.asarray(v, dtype=np.float32)
                       for k, v in inputs.items() if k != "L0_attn_x"})
    nc = _get_nc()
    in_maps = []
    for i in range(8):
        m = dict(shared)
        m["x"] = np.ascontiguousarray(x[2 * i:2 * i + 2])
        in_maps.append(m)
    res = run_bass_kernel_spmd(nc, in_maps, core_ids=list(range(8)))
    out = np.stack([r["out"] for r in res.results], axis=0)
    return out.reshape(16, SEQ, D).astype(np.float32)


# version 08
# speedup vs baseline: 1.0528x; 1.0091x over previous
import sys
import numpy as np
import ml_dtypes
from contextlib import ExitStack

import concourse.bass as bass
import concourse.mybir as mybir
from concourse.bass_utils import run_bass_kernel_spmd

F32 = mybir.dt.float32
BF16 = mybir.dt.bfloat16
FP16 = mybir.dt.float16
AF = mybir.ActivationFunctionType
ALU = mybir.AluOpType
AXX = mybir.AxisListType.X

D = 1024
DI = 2048
SEQ = 2048
NSEQ = 2
NST = 16
DTR = 64
LC = 512
NCH = SEQ // LC
NSUB = LC // 128
FC = D // 128
DC = DI // 128
TS = SEQ // 128
EPS = 1e-5
UOF = 8
UW = UOF + 512

ENGS = ("tensor", "vector", "scalar", "gpsimd", "sync")
_A_ONLY = False
RELAXED = False


class Planner:

    def __init__(self):
        self.ops = []
        self.bufs = {}
        self.groups = {}
        self.nbar = 0
        self.last_eng = {}

    def _b(self, k):
        b = self.bufs.get(k)
        if b is None:
            b = {"w": None, "r": []}
            self.bufs[k] = b
        return b

    def _g(self, name):
        g = self.groups.get(name)
        if g is None:
            g = {"gen": 0, "count": 0, "bounds": [], "pending": set(), "last": None}
            self.groups[name] = g
        return g

    def add(self, eng, fn, r=(), w=(), grp=None, extra=(), bar_inc=False,
            bar_wait=None, async_out=False):
        i = len(self.ops)
        deps = set(extra)
        sdeps = set()
        for k in r:
            b = self._b(k)
            if b["w"] is not None:
                deps.add(b["w"])
        for k in w:
            b = self._b(k)
            if b["w"] is not None:
                sdeps.add(b["w"])
            sdeps.update(b["r"])
        deps.discard(i)
        sdeps.discard(i)
        deps -= sdeps
        gw = {}
        eng_deps = set()
        for j in deps | sdeps:
            oj = self.ops[j]
            if oj["grp"] is not None:
                gname, gen = oj["grp"]
                gw[gname] = max(gw.get(gname, -1), gen)
                gg = self._g(gname)
                if gen == gg["gen"]:
                    gg["pending"].add(i)
            else:
                eng_deps.add(j)
        gtag = None
        if grp is not None:
            g = self._g(grp)
            pend = {j for j in g["pending"] if j != i}
            if pend:
                g["bounds"].append(g["count"])
                g["gen"] += 1
                eng_deps.update(j for j in pend if self.ops[j]["grp"] is None)
                for j in pend:
                    oj = self.ops[j]
                    if oj["grp"] is not None:
                        gn, gg = oj["grp"]
                        gw[gn] = max(gw.get(gn, -1), gg)
                g["pending"] = set()
            done = g["bounds"][-1] if g["bounds"] else 0
            if gw.get(grp, -1) == g["gen"] and g["count"] > done:
                g["bounds"].append(g["count"])
                g["gen"] += 1
            g["count"] += 1
            g["last"] = i
            gtag = (grp, g["gen"])
        self.ops.append({"eng": eng, "fn": fn, "grp": gtag, "deps": eng_deps,
                         "gw": gw, "async_out": async_out,
                         "bar_inc": bar_inc, "bar_wait": bar_wait})
        for k in r:
            self._b(k)["r"].append(i)
        for k in w:
            b = self._b(k)
            b["w"] = i
            b["r"] = []
        if gtag is None and not bar_inc and bar_wait is None:
            self.last_eng[eng] = i
        return i

    def barrier(self):
        lasts = [i for i in self.last_eng.values()]
        last_dmas = [g["last"] for g in self.groups.values() if g["last"] is not None]
        for g in self.groups.values():
            done = g["bounds"][-1] if g["bounds"] else 0
            if g["count"] > done:
                g["bounds"].append(g["count"])
                g["gen"] += 1
        self.add("sync", lambda eng: eng.nop(), extra=lasts + last_dmas,
                 bar_inc=True)
        self.nbar += 1
        for e in ENGS:
            if e != "sync":
                self.add(e, lambda eng: eng.nop(), bar_wait=self.nbar)

    def finalize(self):
        n = len(self.ops)
        for g in self.groups.values():
            g["bounds"].append(g["count"])
        if RELAXED:
            for o in self.ops:
                if o["grp"] is None:
                    o["deps"] = {j for j in o["deps"]
                                 if self.ops[j]["eng"] != o["eng"]
                                 or self.ops[j]["grp"] is not None
                                 or self.ops[j]["async_out"]}
            need = [False] * n
            for o in self.ops:
                for j in o["deps"]:
                    need[j] = True
        else:
            need = [o["grp"] is None and not o["bar_inc"] and o["bar_wait"] is None
                    for o in self.ops]
        cum = {}
        val = [0] * n
        for i, o in enumerate(self.ops):
            if o["grp"] is not None:
                continue
            if need[i]:
                s = "e_" + o["eng"]
                cum[s] = cum.get(s, 0) + 1
                val[i] = cum[s]

        def gen_threshold(gname, gen):
            b = self.groups[gname]["bounds"]
            k = min(gen, len(b) - 1)
            return 16 * b[k]

        sched = {k: [] for k in ENGS}
        floor = {e: {} for e in ENGS}
        for i, o in enumerate(self.ops):
            wmap = {}
            for j in o["deps"]:
                if val[j] > 0:
                    s = "e_" + self.ops[j]["eng"]
                    wmap[s] = max(wmap.get(s, 0), val[j])
            for gname, gen in o["gw"].items():
                s = "g_" + gname
                wmap[s] = max(wmap.get(s, 0), gen_threshold(gname, gen))
            if o["bar_wait"] is not None:
                wmap["bar"] = o["bar_wait"]
            if o["bar_inc"]:
                inc = ("bar", 1)
                nbc = self.bar_cnt = getattr(self, "bar_cnt", 0) + 1
                if nbc > 1:
                    wmap["bar"] = max(wmap.get("bar", 0), nbc - 1)
            elif o["grp"] is not None:
                inc = ("g_" + o["grp"][0], 16)
            elif need[i]:
                inc = ("e_" + o["eng"], 1)
                if not RELAXED:
                    s = inc[0]
                    if val[i] > 1:
                        wmap[s] = max(wmap.get(s, 0), val[i] - 1)
            else:
                inc = None
            fl = floor[o["eng"]]
            wl = []
            for s, v in sorted(wmap.items()):
                if fl.get(s, -1) < v:
                    fl[s] = v
                    wl.append((s, v))
            sched[o["eng"]].append((o["fn"], wl, inc))
        return sched, cum


class Builder:
    def __init__(self):
        self.nc = bass.Bass("TRN2", target_bir_lowering=False,
                            detect_race_conditions=not RELAXED)
        self.pl = Planner()
        self.ctx = ExitStack()
        self.arena_off = 0

    def sb(self, name, shape, dt):
        return self.ctx.enter_context(self.nc.sbuf_tensor(name, shape, dt))

    def view(self, dt, cols):
        esz = 2 if dt in (BF16, FP16) else 4
        off = (self.arena_off + 63) & ~63
        self.arena_off = off + cols * esz
        assert self.arena_off <= self.ARENA_BYTES, \
            f"arena overflow: {self.arena_off} > {self.ARENA_BYTES}"
        base = self.arena.ap() if dt == BF16 else self.arena.ap().bitcast(dt)
        return base[:, off // esz: off // esz + cols]

    def reset_arena(self):
        self.arena_off = 0

    def dma(self, q, out, in_, grp, r=(), w=()):
        self.pl.add(q, lambda e, o=out, i=in_: e.dma_start(out=o, in_=i),
                    r=r, w=w, grp=grp)

    def dmat(self, out, in_, r=(), w=()):
        self.pl.add("scalar",
                    lambda e, o=out, i=in_: e.dma_start_transpose(out=o, in_=i),
                    r=r, w=w, grp="tp")

    def mm(self, out, lhsT, rhs, start, stop, r=(), w=()):
        self.pl.add("tensor",
                    lambda e, o=out, l=lhsT, rr=rhs, s1=start, s2=stop:
                    e.matmul(o, l, rr, start=s1, stop=s2), r=r, w=w)

    def tpose(self, out, in_, r=(), w=()):
        self.pl.add("tensor",
                    lambda e, o=out, i=in_, d=self.IDEN[:]: e.transpose(o, i, d),
                    r=r, w=w)

    def act(self, out, in_, func, bias=0.0, scale=1.0, accum=None, r=(), w=()):
        self.pl.add("scalar",
                    lambda e, o=out, i=in_, f=func, b=bias, s=scale, a=accum:
                    e.activation(out=o, in_=i, func=f, bias=b, scale=s, accum_out=a),
                    r=r, w=w, async_out=accum is not None)

    def tt(self, out, a, b, op, r=(), w=()):
        self.pl.add("vector",
                    lambda e, o=out, x=a, y=b, p=op:
                    e.tensor_tensor(out=o, in0=x, in1=y, op=p), r=r, w=w)

    def ts(self, out, a, s1, s2, op0, op1, accum=None, r=(), w=()):
        if s2 is None and accum is None:
            self.pl.add("vector",
                        lambda e, o=out, x=a, u=s1, p0=op0:
                        e.tensor_scalar(out=o, in0=x, scalar1=u, scalar2=None, op0=p0),
                        r=r, w=w)
        else:
            self.pl.add("vector",
                        lambda e, o=out, x=a, u=s1, v=s2, p0=op0, p1=op1, ac=accum:
                        e.tensor_scalar(out=o, in0=x, scalar1=u, scalar2=v,
                                        op0=p0, op1=p1, accum_out=ac), r=r, w=w,
                        async_out=accum is not None)

    def stt(self, out, in0, scal, in1, op0, op1, r=(), w=()):
        self.pl.add("vector",
                    lambda e, o=out, x=in0, u=scal, y=in1, p0=op0, p1=op1:
                    e.scalar_tensor_tensor(out=o, in0=x, scalar=u, in1=y,
                                           op0=p0, op1=p1), r=r, w=w)

    def scan(self, out, a, b, init, r=(), w=()):
        self.pl.add("vector",
                    lambda e, o=out, x=a, y=b, h0=init:
                    e.tensor_tensor_scan(out=o, data0=x, data1=y, initial=h0,
                                         op0=ALU.mult, op1=ALU.add), r=r, w=w)

    def vcopy(self, out, in_, r=(), w=()):
        self.pl.add("vector", lambda e, o=out, i=in_: e.tensor_copy(out=o, in_=i),
                    r=r, w=w)

    def gcopy(self, out, in_, r=(), w=()):
        self.pl.add("gpsimd", lambda e, o=out, i=in_: e.tensor_copy(out=o, in_=i),
                    r=r, w=w)

    def vmemset(self, ap, val, r=(), w=()):
        self.pl.add("vector", lambda e, a=ap, v=val: e.memset(a, v), r=r, w=w)

    def rmax(self, out, in_, negate, r=(), w=()):
        self.pl.add("vector",
                    lambda e, o=out, i=in_, ng=negate:
                    e.reduce_max(out=o, in_=i, axis=AXX, negate=ng), r=r, w=w)

    def rsumf(self, out, in_, r=(), w=()):
        self.pl.add("vector",
                    lambda e, o=out, i=in_: e.reduce_sum(out=o, in_=i, axis=AXX),
                    r=r, w=w)

    def recip(self, out, in_, r=(), w=()):
        self.pl.add("vector", lambda e, o=out, i=in_: e.reciprocal(out=o, in_=i),
                    r=r, w=w)

    def build(self):
        nc = self.nc
        self.x_in = nc.dram_tensor("x", [NSEQ, SEQ, D], F32, kind="ExternalInput").ap()
        self.out = nc.dram_tensor("out", [NSEQ, SEQ, D], F32, kind="ExternalOutput").ap()
        f32in = lambda n, sh: nc.dram_tensor(n, sh, F32, kind="ExternalInput").ap()
        bf16in = lambda n, sh: nc.dram_tensor(n, sh, BF16, kind="ExternalInput").ap()
        self.w_in = f32in("w_in", [2 * DC, 128, 8 * 128])
        self.conv_w_r = f32in("conv_w_r", [128, 64])
        self.conv_b_r = f32in("conv_b_r", [128, 16])
        self.b_dt_r = f32in("b_dt_r", [128, 16])
        self.d_r = f32in("d_r", [128, 16])
        self.a_log_r = f32in("a_log_r", [128, 256])
        self.w_x = f32in("w_x", [128, DC * 96])
        self.w_dt = f32in("w_dt", [DTR, DI])
        self.w_out = f32in("w_out", [DI, D])
        self.w0t = f32in("w0t", [D, D])
        self.gatet = f32in("gatet", [128, FC * 8])
        self.ut = f32in("ut", [128, FC * 128])
        self.v_r = f32in("v_r", [128, D])
        self.wrt = f32in("wrt", [D, D])
        self.wkt = f32in("wkt", [FC, 128, FC * 128])
        self.wvt = f32in("wvt", [D, D])
        self.mur_r = f32in("mur_r", [128, 8])
        self.muk_r = f32in("muk_r", [128, 8])
        self.selb = bf16in("selb", [32, NST * 128])
        self.selc = bf16in("selc", [32, NST * 128])
        self.rep8 = bf16in("rep8", [8, 128])
        self.ones1 = bf16in("ones1", [128, 1])
        self.iden = bf16in("iden", [128, 128])

        self.win_p = nc.dram_tensor("win_p", [2 * DC, 128, 8 * 128], BF16).ap()
        self.wout_p = nc.dram_tensor("wout_p", [DI, D], BF16).ap()
        self.w0t_p = nc.dram_tensor("w0t_p", [D, D], BF16).ap()
        self.wrt_p = nc.dram_tensor("wrt_p", [D, D], BF16).ap()
        self.wkt_p = nc.dram_tensor("wkt_p", [FC, 128, FC * 128], BF16).ap()
        self.wvt_p = nc.dram_tensor("wvt_p", [D, D], BF16).ap()
        self.xres_h = nc.dram_tensor("xres_h", [NSEQ, SEQ, D], F32).ap()

        self.ARENA_BYTES = 172544
        self.arena = self.sb("arena", [128, self.ARENA_BYTES // 2], BF16)
        self.WSTR = self.sb("wstr", [128, 4 * 1024], BF16)
        self.WX = self.sb("wxr", [128, DC * 96], BF16)
        self.WDT = self.sb("wdtr", [DTR, DI], BF16)
        self.UTS = self.sb("uts", [128, FC * 128], BF16)
        self.VSB = self.sb("vsb", [128, D], BF16)
        self.GATET = self.sb("gatetr", [128, FC * 8], BF16)
        self.SELB = self.sb("selbr", [128, NST * 128], BF16)
        self.SELC = self.sb("selcr", [128, NST * 128], BF16)
        self.REP8 = self.sb("rep8r", [8, 128], BF16)
        self.ONES1 = self.sb("ones1r", [128, 1], BF16)
        self.IDEN = self.sb("idenr", [128, 128], BF16)
        self.CONVW = self.sb("convw", [128, 64], F32)
        self.CONVB = self.sb("convb", [128, 16], F32)
        self.BDT = self.sb("bdt", [128, 16], F32)
        self.DCOL = self.sb("dcol", [128, 16], F32)
        self.ANEG = self.sb("aneg", [128, 256], F32)
        self.MUR = self.sb("mur", [128, 8], F32)
        self.MUK = self.sb("muk", [128, 8], F32)
        self.HST = self.sb("hst", [128, 512], F32)
        self.DV = self.sb("dv", [128, 1024], BF16)
        self.DA = self.sb("da", [128, 1024], BF16)
        def stats_set(tag):
            t = lambda nm, c: self.sb(tag + nm, [128, c], F32)
            return {"pf": tag, "s1": t("s1", 16), "s2": t("s2", 16),
                    "negm": t("nm", 16), "rstd": t("rs", 16),
                    "sa": t("sa", 16), "sb": t("sb", 16),
                    "sv": t("sv", 16), "sw": t("sw", 16),
                    "aux": t("ax", 16), "aux2": t("a2", 16),
                    "dv": self.DV, "da": self.DA}
        self.STA = stats_set("LA")
        self.STB = stats_set("LB")
        self.RINV = self.sb("rinv", [128, 4], F32)
        self.NM = self.sb("nm", [128, 2], F32)
        self.GE = self.sb("ge", [128, 16], F32)
        self.GS = self.sb("gs", [128, 4], F32)
        self.GSUB = self.sb("gsub", [128, 16], BF16)
        self.GFM = self.sb("gfm", [8, 2 * LC], BF16)
        self.FCONST = self.sb("fconst", [128, 2], F32)
        self.nc.const_aps.register(F32, EPS, self.FCONST[:, 0:1])
        self.nc.const_aps.register(F32, 1.0, self.FCONST[:, 1:2])
        self.PS = [self.ctx.enter_context(self.nc.psum_tensor(f"ps{i}", [128, 512], F32))
                   for i in range(8)]

        self.prep()
        self.pl.barrier()
        self.reset_arena()
        self.phaseA_merged()
        if not _A_ONLY:
            self.pl.barrier()
            self.reset_arena()
            self.phaseB_merged()
            for s in range(NSEQ):
                self.pl.barrier()
                self.reset_arena()
                self.phaseC(s)
                self.pl.barrier()
                self.reset_arena()
                self.phaseD(s)

        self.pl.add("sync", lambda e: e.nop(),
                    r=[("out", s, g) for s in range(NSEQ) for g in range(TS)])

        sched, cum = self.pl.finalize()
        sems = {}
        names = ["e_tensor", "e_vector", "e_scalar", "e_gpsimd", "e_sync", "bar"]
        names += ["g_" + g for g in self.pl.groups]
        for name in names:
            sems[name] = self.ctx.enter_context(nc.semaphore(name))

        def body(eng_obj, items):
            for fn, wl, inc in items:
                for sname, v in wl:
                    eng_obj.wait_ge(sems[sname], v)
                inst = fn(eng_obj)
                if inc is not None:
                    inst.then_inc(sems[inc[0]], inc[1])

        with nc.Block() as blk:
            blk.tensor(lambda e: body(e, sched["tensor"]))
            blk.vector(lambda e: body(e, sched["vector"]))
            blk.scalar(lambda e: body(e, sched["scalar"]))
            blk.gpsimd(lambda e: body(e, sched["gpsimd"]))
            blk.sync(lambda e: body(e, sched["sync"]))
        self.ctx.close()
        return nc

    def prep(self):
        self.vmemset(self.FCONST[:, 0:1], EPS, w=[("fconst",)])
        self.vmemset(self.FCONST[:, 1:2], 1.0, w=[("fconst",)])
        for dst, src, key in [(self.win_p, self.w_in, "win_p"),
                              (self.wout_p, self.w_out, "wout_p"),
                              (self.w0t_p, self.w0t, "w0t_p"),
                              (self.wrt_p, self.wrt, "wrt_p"),
                              (self.wkt_p, self.wkt, "wkt_p"),
                              (self.wvt_p, self.wvt, "wvt_p")]:
            rows = src.shape[0]
            q = rows // 4
            for j in range(4):
                self.dma("gpsimd", dst[j * q:(j + 1) * q], src[j * q:(j + 1) * q],
                         "p_" + key, w=[(key, j)])
        self.dma("gpsimd", self.WX[:], self.w_x[:], "pres", w=[("wx",)])
        self.dma("gpsimd", self.WDT[:], self.w_dt[:], "pres", w=[("wdt",)])
        self.dma("gpsimd", self.UTS[:], self.ut[:], "pres", w=[("uts",)])
        self.dma("gpsimd", self.VSB[:], self.v_r[:], "pres", w=[("vsb",)])
        self.dma("gpsimd", self.GATET[:], self.gatet[:], "pres", w=[("gatet",)])
        for dst, src, key in [(self.CONVW, self.conv_w_r, "convw"),
                              (self.CONVB, self.conv_b_r, "convb"),
                              (self.BDT, self.b_dt_r, "bdt"),
                              (self.DCOL, self.d_r, "dcol"),
                              (self.MUR, self.mur_r, "mur"),
                              (self.MUK, self.muk_r, "muk")]:
            self.dma("sync", dst[:], src[:], "psml", w=[(key,)])
        self.dma("sync", self.SELB[64:96, :], self.selb[:], "pcon", w=[("selb",)])
        self.dma("sync", self.SELC[64:96, :], self.selc[:], "pcon", w=[("selc",)])
        self.dma("sync", self.REP8[:], self.rep8[:], "pcon", w=[("rep8",)])
        self.dma("sync", self.ONES1[:], self.ones1[:], "pcon", w=[("ones1",)])
        self.dma("sync", self.IDEN[:], self.iden[:], "pcon", w=[("iden",)])
        self.dma("sync", self.ANEG[:], self.a_log_r[:], "pan", w=[("aneg",)])
        self.act(self.ANEG[:], self.ANEG[:], AF.Exp, r=[("aneg",)], w=[("aneg",)])
        self.ts(self.ANEG[:], self.ANEG[:], -1.0, None, ALU.mult, None,
                r=[("aneg",)], w=[("aneg",)])

    def ln_stats(self, st, src_ap, col, key_r):
        pf = st["pf"]
        self.ts(st["dv"][:, 0:src_ap.shape[1]], src_ap, 1.0, 0.0, ALU.mult, ALU.add,
                accum=st["s1"][:, col:col + 1],
                r=key_r, w=[(pf, "s1", col), (pf, "dumpv")])
        self.act(st["da"][:, 0:src_ap.shape[1]], src_ap, AF.Square,
                 accum=st["s2"][:, col:col + 1],
                 r=key_r, w=[(pf, "s2", col), (pf, "dumpa")])

    def ln_math(self, st, lo, n, want_r=False):
        pf = st["pf"]
        sl = slice(lo, lo + n)
        rs1 = [(pf, "s1", c) for c in range(lo, lo + n)]
        rs2 = [(pf, "s2", c) for c in range(lo, lo + n)]
        self.ts(st["negm"][:, sl], st["s1"][:, sl], -1.0 / D, None, ALU.mult, None,
                r=rs1, w=[(pf, "negm", lo)])
        self.ts(st["sa"][:, sl], st["s1"][:, sl], 1.0 / D, None, ALU.mult, None,
                r=rs1, w=[(pf, "sa", lo)])
        self.tt(st["sb"][:, sl], st["sa"][:, sl], st["sa"][:, sl], ALU.mult,
                r=[(pf, "sa", lo)], w=[(pf, "sb", lo)])
        self.stt(st["sv"][:, sl], st["s2"][:, sl], 1.0 / D, st["sb"][:, sl],
                 ALU.mult, ALU.subtract, r=rs2 + [(pf, "sb", lo)], w=[(pf, "sv", lo)])
        self.act(st["sw"][:, sl], st["sv"][:, sl], AF.Ln, bias=EPS,
                 r=[(pf, "sv", lo)], w=[(pf, "sw", lo)])
        self.act(st["rstd"][:, sl], st["sw"][:, sl], AF.Exp, scale=-0.5,
                 r=[(pf, "sw", lo)], w=[(pf, "rstd", lo)])
        if want_r:
            self.tt(st["aux"][:, sl], st["rstd"][:, sl], st["rstd"][:, sl], ALU.mult,
                    r=[(pf, "rstd", lo)], w=[(pf, "aux", lo)])
            self.stt(st["aux"][:, sl], st["sv"][:, sl], float(D), st["aux"][:, sl],
                     ALU.mult, ALU.mult,
                     r=[(pf, "sv", lo), (pf, "aux", lo)], w=[(pf, "aux", lo)])
            self.ts(st["aux"][:, sl], st["aux"][:, sl], 1e-20, None, ALU.max, None,
                    r=[(pf, "aux", lo)], w=[(pf, "aux", lo)])
            self.act(st["aux"][:, sl], st["aux"][:, sl], AF.Ln,
                     r=[(pf, "aux", lo)], w=[(pf, "aux", lo)])
            self.act(st["aux"][:, sl], st["aux"][:, sl], AF.Exp, scale=-0.5,
                     r=[(pf, "aux", lo)], w=[(pf, "aux", lo)])
            self.tt(st["aux2"][:, sl], st["rstd"][:, sl], st["aux"][:, sl], ALU.mult,
                    r=[(pf, "rstd", lo), (pf, "aux", lo)], w=[(pf, "aux2", lo)])

    def phaseA_merged(self):
        v = self.view
        self.AU = [v(BF16, DC * UW) for _ in range(2)]
        self.ADEL = v(BF16, DC * 512)
        self.AW = v(BF16, DC * 512)
        self.AY = v(FP16, DC * 512)
        self.AXFA = [v(BF16, FC * 512) for _ in range(2)]
        self.AX0 = [v(F32, 1024) for _ in range(4)]
        self.AXH = [v(BF16, 1024) for _ in range(4)]
        self.AXO = [v(F32, 1024) for _ in range(4)]
        self.ADR = v(BF16, 512)
        self.AXDB = [v(BF16, 512) for _ in range(2)]
        self.AE1 = v(F32, 512)
        self.ABBC = [v(BF16, 512) for _ in range(2)]
        self.ACBC = [v(BF16, 512) for _ in range(2)]
        self.AA2 = [v(FP16, 512) for _ in range(4)]
        self.AB2 = [v(BF16, 512) for _ in range(2)]
        self.AH2 = [v(BF16, 512) for _ in range(2)]
        self.AYC = [v(BF16, 512) for _ in range(2)]
        self.AACC = v(BF16, 512)
        self.AGR = [v(BF16, 512) for _ in range(8)]
        self.ATMP = [v(F32, 512) for _ in range(2)]
        self.x0cnt = 0
        self.xhcnt = 0
        self.a2c = 0
        self.rx0 = {}
        NSTEP = 2 * NCH

        self.vmemset(self.HST[:], 0.0,
                     w=[("hst", s, n, dc) for s in range(2)
                        for n in range(NST) for dc in range(DC)])
        for q in range(2):
            u3 = self.AU[q].rearrange("p (dc r) -> p dc r", dc=DC)
            self.vmemset(u3[:, :, UOF - 3:UOF], 0.0,
                         w=[("tl", q, dc) for dc in range(DC)])

        for j0 in (0, 1):
            self.frontA_LN(j0)
            self.frontA_dmats(j0, (0, 1))
            self.frontA_dmats(j0, (2, 3))
            self.frontA_win(j0)
            self.frontA_conv(j0)
        self.frontB_xdbl(0)
        self.frontB_delta(0, 0, 16)
        for j in range(NSTEP):
            nxt = j + 1 < NSTEP
            self.sweepA(j, 0, nxt)
            self.gateA(j, 0)
            if nxt and j + 1 >= 2:
                self.frontA_conv(j + 1)
            self.sweepA(j, 1, nxt)
            self.gateA(j, 8)
            if nxt:
                self.frontB_delta(j + 1, 8, 16)
            self.woutA_close(j)
            self.outLN_store(j)

    def autile(self, q, dc):
        return self.AU[q][:, dc * UW: dc * UW + UW]

    def adtile(self, dc):
        return self.ADEL[:, dc * 512:(dc + 1) * 512]

    def awtile(self, dc):
        return self.AW[:, dc * 512:(dc + 1) * 512]

    def aytile(self, dc):
        return self.AY[:, dc * 512:(dc + 1) * 512]

    def axfa(self, q, fc):
        return self.AXFA[q][:, fc * 512:(fc + 1) * 512]

    def frontA_LN(self, j):
        s, c = j % 2, j // 2
        q = j % 2
        lo = 4 * q
        st = self.STA
        pf = st["pf"]
        for sub in range(NSUB):
            g = c * NSUB + sub
            b = self.x0cnt % 4
            self.x0cnt += 1
            self.dma("sync", self.AX0[b][:], self.x_in[s, g * 128:(g + 1) * 128, :],
                     f"x0{b}", w=[("x0", b)])
            self.ln_stats(st, self.AX0[b][:], lo + sub, [("x0", b)])
        self.ln_math(st, lo, NSUB)
        for sub in range(NSUB):
            g = c * NSUB + sub
            b = self.x0cnt % 4
            self.x0cnt += 1
            rx = sub
            self.dma("sync", self.AX0[b][:], self.x_in[s, g * 128:(g + 1) * 128, :],
                     f"x0{b}", w=[("x0", b)])
            self.ts(self.AXH[rx], self.AX0[b][:],
                    st["negm"][:, lo + sub:lo + sub + 1],
                    st["rstd"][:, lo + sub:lo + sub + 1], ALU.add, ALU.mult,
                    r=[("x0", b), (pf, "negm", lo), (pf, "rstd", lo)],
                    w=[("xh", rx)])

    def frontA_dmats(self, j, subs):
        q = j % 2
        for sub in subs:
            rx = sub
            for fc in range(FC):
                self.dmat(self.axfa(q, fc)[:, sub * 128:(sub + 1) * 128],
                          self.AXH[rx][:, fc * 128:(fc + 1) * 128],
                          r=[("xh", rx)], w=[("xfa", q, fc, sub)])

    def frontA_win(self, j):
        q = j % 2
        xfar = [("xfa", q, fc, sub) for fc in range(FC) for sub in range(NSUB)]
        for oc in range(DC):
            p = oc % 2
            wb = self.WSTR[:, p * 1024:p * 1024 + 1024]
            self.dma("sync", wb, self.win_p[oc], f"wb{p}",
                     r=[("win_p", jj) for jj in range(4)], w=[("wbd", p)])
            pb = self.PS[p]
            for fc in range(FC):
                self.mm(pb[:], wb[:, fc * 128:(fc + 1) * 128], self.axfa(q, fc),
                        start=(fc == 0), stop=(fc == FC - 1),
                        r=[("wbd", p)] + xfar, w=[("ps", p)])
            self.act(self.autile(q, oc)[:, UOF:UOF + 512], pb[:], AF.Copy,
                     r=[("ps", p)], w=[("u", q, oc), ("ps", p)])

    def frontA_conv(self, j):
        q = j % 2
        for dc in range(DC):
            ut = self.autile(q, dc)
            self.ts(self.AACC, ut[:, UOF - 3:UOF + 509],
                    self.CONVW[:, dc * 4:dc * 4 + 1], None, ALU.mult, None,
                    r=[("u", q, dc), ("tl", q, dc), ("convw",)], w=[("acc",)])
            for k in range(1, 4):
                self.stt(self.AACC, ut[:, UOF - 3 + k:UOF + 509 + k],
                         self.CONVW[:, dc * 4 + k:dc * 4 + k + 1], self.AACC,
                         ALU.mult, ALU.add,
                         r=[("u", q, dc), ("tl", q, dc), ("acc",), ("convw",)],
                         w=[("acc",)])
            self.gcopy(ut[:, UOF - 3:UOF], ut[:, UOF + 509:UOF + 512],
                       r=[("u", q, dc)], w=[("tl", q, dc)])
            self.act(ut[:, UOF:UOF + 512], self.AACC, AF.Silu,
                     bias=self.CONVB[:, dc:dc + 1],
                     r=[("acc",), ("convb",)], w=[("u", q, dc)])

    def frontB_xdbl(self, j):
        q = j % 2
        px = self.PS[2]
        for kc in range(DC):
            self.mm(px[0:96, :], self.WX[:, kc * 96:(kc + 1) * 96],
                    self.autile(q, kc)[:, UOF:UOF + 512],
                    start=(kc == 0), stop=(kc == DC - 1),
                    r=[("u", q, kc), ("wx",)], w=[("ps", 2)])
        self.act(self.ADR[0:64, :], px[0:64, :], AF.Copy,
                 r=[("ps", 2)], w=[("dr",)])
        self.act(self.AXDB[q][64:96, :], px[64:96, :], AF.Copy,
                 r=[("ps", 2)], w=[("xdb", q), ("ps", 2)])

    def frontB_delta(self, j, d0, d1):
        q = j % 2
        for dc in range(d0, d1):
            pi = 3 if dc % 2 == 0 else 2
            pd = self.PS[pi]
            self.mm(pd[:], self.WDT[:, dc * 128:(dc + 1) * 128], self.ADR[0:64, :],
                    start=True, stop=True,
                    r=[("dr",), ("wdt",)], w=[("ps", pi)])
            self.act(self.AE1, pd[:], AF.Exp, bias=self.BDT[:, dc:dc + 1],
                     r=[("ps", pi), ("bdt",)], w=[("e1",), ("ps", pi)])
            self.act(self.adtile(dc), self.AE1, AF.Ln, bias=1.0,
                     r=[("e1",)], w=[("del", dc)])
            self.tt(self.awtile(dc), self.adtile(dc),
                    self.autile(q, dc)[:, UOF:UOF + 512], ALU.mult,
                    r=[("del", dc), ("u", q, dc)], w=[("w", dc)])

    def resA(self, j, base):
        q = j % 2
        xfar = [("xfa", q, fc, sub) for fc in range(FC) for sub in range(NSUB)]
        for k in range(8):
            oc = DC + base + k
            p = oc % 2
            wb = self.WSTR[:, p * 1024:p * 1024 + 1024]
            self.dma("sync", wb, self.win_p[oc], f"wb{p}",
                     r=[("win_p", jj) for jj in range(4)], w=[("wbd", p)])
            pb = self.PS[p]
            for fc in range(FC):
                self.mm(pb[:], wb[:, fc * 128:(fc + 1) * 128], self.axfa(q, fc),
                        start=(fc == 0), stop=(fc == FC - 1),
                        r=[("wbd", p)] + xfar, w=[("ps", p)])
            self.act(self.AGR[k], pb[:], AF.Silu,
                     r=[("ps", p)], w=[("gr", k), ("ps", p)])

    def gateA(self, j, base):
        q = j % 2
        for k in range(8):
            dc = base + k
            p = dc % 2
            self.stt(self.ATMP[p], self.autile(q, dc)[:, UOF:UOF + 512],
                     self.DCOL[:, dc:dc + 1], self.aytile(dc), ALU.mult, ALU.add,
                     r=[("u", q, dc), ("dcol",), ("y", dc)], w=[("tmp", p)])
            self.tt(self.autile(q, dc)[:, UOF:UOF + 512], self.ATMP[p],
                    self.AGR[k], ALU.mult,
                    r=[("tmp", p), ("gr", k), ("u", q, dc)], w=[("u", q, dc)])

    def prefX0(self, j):
        s, c = j % 2, j // 2
        bl = []
        for sub in range(NSUB):
            g = c * NSUB + sub
            b = self.x0cnt % 4
            self.x0cnt += 1
            self.dma("sync", self.AX0[b][:], self.x_in[s, g * 128:(g + 1) * 128, :],
                     f"x0{b}", w=[("x0", b)])
            bl.append(b)
        self.rx0[j] = bl

    WOBANK = (0, 1, 6, 7)

    def woutA_chain(self, j, half, d0, d1, stop_end):
        q = j % 2
        for dc in range(d0, d1):
            wsl = dc % 2
            wb = self.WSTR[:, 2048 + wsl * 512:2048 + wsl * 512 + 512]
            self.dma("sync", wb,
                     self.wout_p[dc * 128:(dc + 1) * 128,
                                 half * 512:(half + 1) * 512],
                     f"wo{wsl}",
                     r=[("wout_p", dc // 4)], w=[("wod", wsl)])
            for sub in range(NSUB):
                bk = self.WOBANK[sub]
                self.mm(self.PS[bk][:],
                        self.autile(q, dc)[:, UOF + sub * 128:UOF + (sub + 1) * 128],
                        wb[:, 0:512],
                        start=(dc == 0), stop=(stop_end and dc == d1 - 1),
                        r=[("wod", wsl), ("u", q, dc)], w=[("ps", bk)])

    def woutA_evac(self, j, half):
        for sub in range(NSUB):
            bk = self.WOBANK[sub]
            self.act(self.AXO[sub][:, half * 512:(half + 1) * 512],
                     self.PS[bk][:], AF.Copy,
                     r=[("ps", bk)], w=[("xo", sub), ("ps", bk)])

    def woutA_close(self, j):
        self.woutA_chain(j, 0, 0, 16, True)
        self.woutA_evac(j, 0)
        self.woutA_chain(j, 1, 0, 16, True)
        self.woutA_evac(j, 1)

    def outLN_store(self, j):
        s, c = j % 2, j // 2
        q = j % 2
        st = self.STA
        pf = st["pf"]
        lo = 8 + 4 * q
        if _A_ONLY:
            for sub in range(NSUB):
                g = c * NSUB + sub
                self.dma("gpsimd", self.out[s, g * 128:(g + 1) * 128, :],
                         self.AXO[sub][:], "out", r=[("xo", sub)],
                         w=[("odbg", s, g)])
        for sub in range(NSUB):
            self.ln_stats(st, self.AXO[sub][:], lo + sub, [("xo", sub)])
        self.ln_math(st, lo, NSUB)
        for sub in range(NSUB):
            g = c * NSUB + sub
            b = self.rx0[j][sub]
            for half in range(2):
                hp = half
                self.ts(self.ATMP[hp],
                        self.AXO[sub][:, half * 512:(half + 1) * 512],
                        st["negm"][:, lo + sub:lo + sub + 1],
                        st["rstd"][:, lo + sub:lo + sub + 1], ALU.add, ALU.mult,
                        r=[("xo", sub), (pf, "negm", lo), (pf, "rstd", lo)],
                        w=[("tmp", hp)])
                self.tt(self.AXO[sub][:, half * 512:(half + 1) * 512],
                        self.AX0[b][:, half * 512:(half + 1) * 512],
                        self.ATMP[hp], ALU.add,
                        r=[("x0", b), ("tmp", hp)], w=[("xo", sub)])
            self.dma("gpsimd", self.xres_h[s, g * 128:(g + 1) * 128, :],
                     self.AXO[sub][:], "xw",
                     r=[("xo", sub)], w=[("xrh", s, g)])

    def sweepA(self, j, sh, nxt):
        s = j % 2
        q = j % 2
        d0 = 8 * sh
        for n in range(NST):
            if sh == 0:
                if nxt and j + 1 >= 2:
                    if n == 4:
                        self.frontA_LN(j + 1)
                        self.frontA_dmats(j + 1, (0, 1))
                    elif n == 6:
                        self.frontA_dmats(j + 1, (2, 3))
                    elif n == 8:
                        self.frontA_win(j + 1)
                if n == 11:
                    self.resA(j, 0)
            else:
                if n == 2:
                    self.resA(j, 8)
                if nxt:
                    if n == 4:
                        self.frontB_xdbl(j + 1)
                    elif n == 6:
                        self.frontB_delta(j + 1, 0, 8)
                if n == 8:
                    self.prefX0(j)
            nb = n % 2
            self.mm(self.PS[4][:], self.SELB[64:96, n * 128:(n + 1) * 128],
                    self.AXDB[q][64:96, :], start=True, stop=True,
                    r=[("xdb", q), ("selb",)], w=[("ps", 4)])
            self.act(self.ABBC[nb], self.PS[4][:], AF.Copy,
                     r=[("ps", 4)], w=[("bbc", nb), ("ps", 4)])
            self.mm(self.PS[5][:], self.SELC[64:96, n * 128:(n + 1) * 128],
                    self.AXDB[q][64:96, :], start=True, stop=True,
                    r=[("xdb", q), ("selc",)], w=[("ps", 5)])
            self.vcopy(self.ACBC[nb], self.PS[5][:],
                       r=[("ps", 5)], w=[("cbc", nb), ("ps", 5)])
            for dc in range(d0, d0 + 8):
                p = dc % 2
                a2 = self.a2c % 4
                self.a2c += 1
                col = s * 256 + dc * 16 + n
                acol = dc * 16 + n
                self.act(self.AA2[a2], self.adtile(dc), AF.Exp,
                         scale=self.ANEG[:, acol:acol + 1],
                         r=[("del", dc), ("aneg",)], w=[("a2", a2)])
                self.tt(self.AB2[p], self.awtile(dc), self.ABBC[nb], ALU.mult,
                        r=[("w", dc), ("bbc", nb)], w=[("b2", p)])
                self.scan(self.AH2[p], self.AA2[a2], self.AB2[p],
                          self.HST[:, col:col + 1],
                          r=[("a2", a2), ("b2", p), ("hst", s, n, dc)],
                          w=[("h2", p)])
                self.gcopy(self.HST[:, col:col + 1], self.AH2[p][:, 511:512],
                           r=[("h2", p)], w=[("hst", s, n, dc)])
                if n == 0:
                    self.tt(self.aytile(dc), self.AH2[p], self.ACBC[nb], ALU.mult,
                            r=[("h2", p), ("cbc", nb)], w=[("y", dc)])
                else:
                    self.tt(self.AYC[p], self.AH2[p], self.ACBC[nb], ALU.mult,
                            r=[("h2", p), ("cbc", nb)], w=[("yc", p)])
                    self.tt(self.aytile(dc), self.aytile(dc), self.AYC[p], ALU.add,
                            r=[("yc", p), ("y", dc)], w=[("y", dc)])

    def phaseB_merged(self):
        v = self.view
        BXIN = [v(F32, 1024) for _ in range(6)]
        BXFB = [v(BF16, FC * 512) for _ in range(2)]
        BZ = [v(F32, NSUB * 1024) for _ in range(2)]
        BXH = [v(BF16, 1024) for _ in range(2)]
        BT = [v(BF16, 512) for _ in range(2)]
        BG = [v(BF16, 512) for _ in range(2)]
        BTMP = [v(F32, 512) for _ in range(2)]
        st = self.STB
        pf = st["pf"]
        xfb = lambda qb, fc: BXFB[qb][:, fc * 512:(fc + 1) * 512]

        for i in range(2 * NCH):
            s, c = i % 2, i // 2
            qb = i % 2
            lo = 4 * qb
            loz = 8 + 4 * qb
            tb = [(i * NSUB + sub) % 6 for sub in range(NSUB)]
            for sub in range(NSUB):
                g = c * NSUB + sub
                b = tb[sub]
                self.dma("sync", BXIN[b][:], self.xres_h[s, g * 128:(g + 1) * 128, :],
                         f"bx{b % 3}", r=[("xrh", s, g)], w=[("bxin", b)])
                self.ln_stats(st, BXIN[b][:], lo + sub, [("bxin", b)])
            self.ln_math(st, lo, NSUB)
            for sub in range(NSUB):
                b = tb[sub]
                rx = sub % 2
                self.ts(BXH[rx], BXIN[b][:],
                        st["negm"][:, lo + sub:lo + sub + 1],
                        st["rstd"][:, lo + sub:lo + sub + 1], ALU.add, ALU.mult,
                        r=[("bxin", b), (pf, "negm", lo), (pf, "rstd", lo)],
                        w=[("bxh", rx)])
                for fc in range(FC):
                    self.dmat(xfb(qb, fc)[:, sub * 128:(sub + 1) * 128],
                              BXH[rx][:, fc * 128:(fc + 1) * 128],
                              r=[("bxh", rx)], w=[("xfb", qb, fc, sub)])
            xfbr = [("xfb", qb, fc, sub) for fc in range(FC) for sub in range(NSUB)]

            pt = qb
            for fc in range(FC):
                self.mm(self.PS[pt][:], self.UTS[:, fc * 128:(fc + 1) * 128],
                        xfb(qb, fc), start=(fc == 0), stop=(fc == FC - 1),
                        r=[("uts",)] + xfbr, w=[("ps", pt)])
            self.vcopy(BT[qb], self.PS[pt][:],
                       r=[("ps", pt)], w=[("texp", qb), ("ps", pt)])
            pg = 2 + qb
            for sub in range(NSUB):
                for fc in range(FC):
                    self.mm(self.PS[pg][:, 0:8],
                            xfb(qb, fc)[:, sub * 128:(sub + 1) * 128],
                            self.GATET[:, fc * 8:(fc + 1) * 8],
                            start=(fc == 0), stop=(fc == FC - 1),
                            r=[("gatet",)] + xfbr, w=[("ps", pg)])
                self.rmax(self.NM[:, qb:qb + 1], self.PS[pg][:, 0:8], True,
                          r=[("ps", pg)], w=[("nm", qb)])
                self.act(self.GE[:, 8 * qb:8 * qb + 8], self.PS[pg][:, 0:8], AF.Exp,
                         bias=self.NM[:, qb:qb + 1],
                         r=[("ps", pg), ("nm", qb)], w=[("ge", qb), ("ps", pg)])
                self.rsumf(self.GS[:, 2 * qb:2 * qb + 1], self.GE[:, 8 * qb:8 * qb + 8],
                           r=[("ge", qb)], w=[("gs", qb)])
                self.ts(self.GS[:, 2 * qb + 1:2 * qb + 2], self.GS[:, 2 * qb:2 * qb + 1],
                        16.0, None, ALU.mult, None,
                        r=[("gs", qb)], w=[("gs16", qb)])
                self.recip(self.GS[:, 2 * qb:2 * qb + 1], self.GS[:, 2 * qb + 1:2 * qb + 2],
                           r=[("gs16", qb)], w=[("gs", qb)])
                self.ts(self.GSUB[:, 8 * qb:8 * qb + 8], self.GE[:, 8 * qb:8 * qb + 8],
                        self.GS[:, 2 * qb:2 * qb + 1], None, ALU.mult,
                        None, r=[("ge", qb), ("gs", qb)], w=[("gsub", qb)])
                pgt = self.PS[pg].ap().bitcast(BF16)[0:8, 0:128]
                self.tpose(pgt, self.GSUB[:, 8 * qb:8 * qb + 8],
                           r=[("gsub", qb), ("iden",)], w=[("ps", pg)])
                self.act(self.GFM[:, qb * LC + sub * 128:qb * LC + (sub + 1) * 128],
                         pgt, AF.Copy,
                         r=[("ps", pg)], w=[("gfm", qb, sub), ("ps", pg)])
            self.mm(self.PS[pt][:], self.REP8[:], self.GFM[:, qb * LC:(qb + 1) * LC],
                    start=True, stop=True,
                    r=[("rep8",)] + [("gfm", qb, ii) for ii in range(NSUB)],
                    w=[("ps", pt)])
            self.vcopy(BG[qb], self.PS[pt][:], r=[("ps", pt)], w=[("bg", qb), ("ps", pt)])
            self.tt(BT[qb], BT[qb], BG[qb], ALU.mult,
                    r=[("texp", qb), ("bg", qb)], w=[("texp", qb)])

            for half in range(2):
                for fc in range(FC):
                    p = fc % 2
                    wb = self.WSTR[:, 3072 + p * 512:3072 + p * 512 + 512]
                    self.dma("sync", wb,
                             self.w0t_p[fc * 128:(fc + 1) * 128,
                                        half * 512:(half + 1) * 512],
                             f"wB{p}",
                             r=[("w0t_p", fc // 2)], w=[("wbB", p)])
                    for sub in range(NSUB):
                        self.mm(self.PS[4 + sub][:],
                                xfb(qb, fc)[:, sub * 128:(sub + 1) * 128],
                                wb[:, 0:512],
                                start=(fc == 0), stop=False,
                                r=[("wbB", p)] + xfbr, w=[("ps", 4 + sub)])
                for sub in range(NSUB):
                    self.mm(self.PS[4 + sub][:],
                            BT[qb][:, sub * 128:(sub + 1) * 128],
                            self.VSB[:, half * 512:(half + 1) * 512],
                            start=False, stop=True,
                            r=[("texp", qb), ("vsb",)], w=[("ps", 4 + sub)])
                for sub in range(NSUB):
                    self.act(BZ[qb][:, sub * 1024 + half * 512:
                                    sub * 1024 + (half + 1) * 512],
                             self.PS[4 + sub][:], AF.Copy,
                             r=[("ps", 4 + sub)],
                             w=[("bz", qb, sub), ("ps", 4 + sub)])
            for sub in range(NSUB):
                self.ln_stats(st, BZ[qb][:, sub * 1024:(sub + 1) * 1024], loz + sub,
                              [("bz", qb, sub)])
            self.ln_math(st, loz, NSUB)
            for sub in range(NSUB):
                g = c * NSUB + sub
                b = tb[sub]
                for half in range(2):
                    hp = half
                    self.ts(BTMP[hp], BZ[qb][:, sub * 1024 + half * 512:
                                              sub * 1024 + (half + 1) * 512],
                            st["negm"][:, loz + sub:loz + sub + 1],
                            st["rstd"][:, loz + sub:loz + sub + 1],
                            ALU.add, ALU.mult,
                            r=[("bz", qb, sub), (pf, "negm", loz), (pf, "rstd", loz)],
                            w=[("btmp", hp)])
                    self.tt(BXIN[b][:, half * 512:(half + 1) * 512],
                            BXIN[b][:, half * 512:(half + 1) * 512],
                            BTMP[hp], ALU.add,
                            r=[("btmp", hp)], w=[("bxin", b)])
                self.dma("gpsimd", self.xres_h[s, g * 128:(g + 1) * 128, :],
                         BXIN[b][:], "xw",
                         r=[("bxin", b)], w=[("xrh", s, g)])

    def phaseC(self, s):
        pf = ("C", s)
        st = self.STA
        spf = st["pf"]
        v = self.view
        XHT = v(BF16, TS * D)
        XFC = v(BF16, FC * SEQ)
        ES = v(BF16, TS * 512)
        AO = [v(F32, D) for _ in range(2)]
        XT = [v(BF16, D) for _ in range(2)]
        TMP = [v(F32, 512) for _ in range(2)]
        CXI = [v(F32, 1024) for _ in range(3)]
        CXR = [v(F32, 1024) for _ in range(2)]
        xht = lambda g: XHT[:, g * D:(g + 1) * D]
        xfc = lambda fc: XFC[:, fc * SEQ:(fc + 1) * SEQ]
        es = lambda kc: ES[:, kc * 512:(kc + 1) * 512]
        cx = [0]

        for g in range(TS):
            b = cx[0] % 3
            cx[0] += 1
            self.dma("sync", CXI[b][:], self.xres_h[s, g * 128:(g + 1) * 128, :],
                     f"cx{b}", r=[("xrh", s, g)], w=[("cxi", b)])
            self.ln_stats(st, CXI[b][:], g, [("cxi", b)])
        self.ln_math(st, 0, TS, want_r=True)
        for g in range(TS):
            b = cx[0] % 3
            cx[0] += 1
            p = g % 2
            self.dma("sync", CXI[b][:], self.xres_h[s, g * 128:(g + 1) * 128, :],
                     f"cx{b}", r=[("xrh", s, g)], w=[("cxi", b)])
            self.ts(xht(g), CXI[b][:],
                    st["negm"][:, g:g + 1], st["rstd"][:, g:g + 1],
                    ALU.add, ALU.mult,
                    r=[("cxi", b), (spf, "negm", 0), (spf, "rstd", 0)],
                    w=[(pf, "xht", g)])
            self.ts(XT[p], CXI[b][:],
                    st["negm"][:, g:g + 1], st["aux2"][:, g:g + 1],
                    ALU.add, ALU.mult,
                    r=[("cxi", b), (spf, "negm", 0), (spf, "aux2", 0)],
                    w=[(pf, "xt", p)])
            for fc in range(FC):
                self.dmat(xfc(fc)[:, g * 128:(g + 1) * 128],
                          XT[p][:, fc * 128:(fc + 1) * 128],
                          r=[(pf, "xt", p)], w=[(pf, "xfc", fc, g)])

        for qb in range(NCH):
            q0 = qb * 512
            xfcq = [(pf, "xfc", fc, qb * 4 + i) for fc in range(FC) for i in range(4)]
            for kc in range(TS):
                pq = kc % 2
                for fc in range(FC):
                    self.mm(self.PS[pq][:], xfc(fc)[:, kc * 128:(kc + 1) * 128],
                            xfc(fc)[:, q0:q0 + 512],
                            start=(fc == 0), stop=(fc == FC - 1),
                            r=[(pf, "xfc", fc, kc)] + xfcq, w=[("ps", pq)])
                self.act(es(kc), self.PS[pq][:], AF.Exp,
                         r=[("ps", pq)], w=[(pf, "es", kc), ("ps", pq)])
            for qs in range(NSUB):
                gq = qb * NSUB + qs
                i0, i1 = 2 + 2 * (qs % 2), 3 + 2 * (qs % 2)
                for kc in range(TS):
                    l = es(kc)[:, qs * 128:(qs + 1) * 128]
                    self.mm(self.PS[i0][:], l, xht(kc)[:, 0:512],
                            start=(kc == 0), stop=(kc == TS - 1),
                            r=[(pf, "es", kc), (pf, "xht", kc)], w=[("ps", i0)])
                    self.mm(self.PS[i1][:], l, xht(kc)[:, 512:1024],
                            start=(kc == 0), stop=(kc == TS - 1),
                            r=[(pf, "es", kc), (pf, "xht", kc)], w=[("ps", i1)])
                    self.mm(self.PS[6][:, 0:1], l, self.ONES1[:],
                            start=(kc == 0), stop=(kc == TS - 1),
                            r=[(pf, "es", kc), ("ones1",)], w=[("ps", 6)])
                self.recip(self.RINV[:, qs:qs + 1], self.PS[6][:, 0:1],
                           r=[("ps", 6)], w=[(pf, "rinv", qs), ("ps", 6)])
                pa = qs % 2
                for half, pi in ((0, i0), (1, i1)):
                    self.stt(AO[pa][:, half * 512:(half + 1) * 512],
                             self.PS[pi][:], self.RINV[:, qs:qs + 1],
                             xht(gq)[:, half * 512:(half + 1) * 512],
                             ALU.mult, ALU.add,
                             r=[("ps", pi), (pf, "rinv", qs), (pf, "xht", gq)],
                             w=[(pf, "ao", pa), ("ps", pi)])
                self.ln_stats(st, AO[pa][:], qs, [(pf, "ao", pa)])
                self.ln_math(st, qs, 1)
                b2 = cx[0] % 2
                cx[0] += 1
                self.dma("sync", CXR[b2][:],
                         self.xres_h[s, gq * 128:(gq + 1) * 128, :],
                         f"cr{b2}", r=[("xrh", s, gq)], w=[("cxr", b2)])
                for half in range(2):
                    self.ts(TMP[half], AO[pa][:, half * 512:(half + 1) * 512],
                            st["negm"][:, qs:qs + 1], st["rstd"][:, qs:qs + 1],
                            ALU.add, ALU.mult,
                            r=[(pf, "ao", pa), (spf, "negm", qs), (spf, "rstd", qs)],
                            w=[(pf, "tmp", half)])
                    self.tt(CXR[b2][:, half * 512:(half + 1) * 512],
                            CXR[b2][:, half * 512:(half + 1) * 512],
                            TMP[half], ALU.add,
                            r=[(pf, "tmp", half)], w=[("cxr", b2)])
                self.dma("gpsimd", self.xres_h[s, gq * 128:(gq + 1) * 128, :],
                         CXR[b2][:], "xw",
                         r=[("cxr", b2)], w=[("xrh", s, gq)])

    def phaseD(self, s):
        pf = ("D", s)
        st = self.STA
        spf = st["pf"]
        v = self.view
        XFD = v(BF16, FC * 544)
        XR = v(BF16, FC * 512)
        XK = v(BF16, FC * 512)
        KACT = v(BF16, FC * 512)
        RT = v(BF16, NSUB * D)
        VR = v(F32, NSUB * D)
        WS2 = v(BF16, FC * 512)
        XH = [v(BF16, D) for _ in range(2)]
        DD = v(BF16, 512)
        KR = v(BF16, 512)
        XO = [v(F32, D) for _ in range(2)]
        TMP = [v(F32, 512) for _ in range(2)]
        DXI = [v(F32, 1024) for _ in range(6)]
        xfd3 = XFD.rearrange("p (fc r) -> p fc r", fc=FC)
        xfd = lambda fc: XFD[:, fc * 544:(fc + 1) * 544]
        xr = lambda fc: XR[:, fc * 512:(fc + 1) * 512]
        xk = lambda fc: XK[:, fc * 512:(fc + 1) * 512]
        ka = lambda dcb: KACT[:, dcb * 512:(dcb + 1) * 512]
        dx = [0]

        self.vmemset(xfd3[:, :, 15:16], 0.0, w=[(pf, "bnd")])

        for c in range(NCH):
            xfdr = [(pf, "xfd", fc, sub) for fc in range(FC) for sub in range(NSUB)]
            if c > 0:
                self.vcopy(xfd3[:, :, 15:16], xfd3[:, :, 527:528],
                           r=xfdr, w=[(pf, "bnd")])
            tb = [(c * NSUB + sub) % 6 for sub in range(NSUB)]
            for sub in range(NSUB):
                g = c * NSUB + sub
                b = tb[sub]
                self.dma("sync", DXI[b][:], self.xres_h[s, g * 128:(g + 1) * 128, :],
                         f"dx{b % 3}", r=[("xrh", s, g)], w=[("dxi", b)])
                self.ln_stats(st, DXI[b][:], sub, [("dxi", b)])
            self.ln_math(st, 0, NSUB)
            for sub in range(NSUB):
                g = c * NSUB + sub
                b = tb[sub]
                p = sub % 2
                self.ts(XH[p], DXI[b][:],
                        st["negm"][:, sub:sub + 1], st["rstd"][:, sub:sub + 1],
                        ALU.add, ALU.mult,
                        r=[("dxi", b), (spf, "negm", 0), (spf, "rstd", 0)],
                        w=[(pf, "xh", p)])
                for fc in range(FC):
                    self.dmat(xfd(fc)[:, 16 + sub * 128:16 + (sub + 1) * 128],
                              XH[p][:, fc * 128:(fc + 1) * 128],
                              r=[(pf, "xh", p), (pf, "bnd")],
                              w=[(pf, "xfd", fc, sub)])
            for fc in range(FC):
                cur = xfd(fc)[:, 16:528]
                prv = xfd(fc)[:, 15:527]
                self.tt(DD, cur, prv, ALU.subtract,
                        r=xfdr + [(pf, "bnd")], w=[(pf, "dd")])
                self.stt(xr(fc), DD, self.MUR[:, fc:fc + 1], prv, ALU.mult, ALU.add,
                         r=[(pf, "dd"), ("mur",)] + xfdr + [(pf, "bnd")],
                         w=[(pf, "xr", fc)])
                self.stt(xk(fc), DD, self.MUK[:, fc:fc + 1], prv, ALU.mult, ALU.add,
                         r=[(pf, "dd"), ("muk",)] + xfdr + [(pf, "bnd")],
                         w=[(pf, "xk", fc)])
            xrr = [(pf, "xr", fc) for fc in range(FC)]
            xkr = [(pf, "xk", fc) for fc in range(FC)]
            for dcb in range(FC):
                p = dcb % 2
                wb = self.WSTR[:, p * 1024:p * 1024 + 1024]
                self.dma("sync", wb, self.wkt_p[dcb], f"wb{p}",
                         r=[("wkt_p", j) for j in range(4)], w=[("wbd", p)])
                for fc in range(FC):
                    self.mm(self.PS[p][:], wb[:, fc * 128:(fc + 1) * 128], xk(fc),
                            start=(fc == 0), stop=(fc == FC - 1),
                            r=[("wbd", p)] + xkr, w=[("ps", p)])
                self.act(KR, self.PS[p][:], AF.Relu,
                         r=[("ps", p)], w=[(pf, "kr"), ("ps", p)])
                self.tt(ka(dcb), KR, KR, ALU.mult, r=[(pf, "kr")], w=[(pf, "ka", dcb)])
            for half in range(2):
                for fc in range(FC):
                    self.dma("sync", WS2[:, fc * 512:(fc + 1) * 512],
                             self.wrt_p[fc * 128:(fc + 1) * 128,
                                        half * 512:(half + 1) * 512],
                             "ws2",
                             r=[("wrt_p", fc // 2)], w=[(pf, "ws2d", fc)])
                for sub in range(NSUB):
                    for fc in range(FC):
                        self.mm(self.PS[2 + sub][:],
                                xr(fc)[:, sub * 128:(sub + 1) * 128],
                                WS2[:, fc * 512:(fc + 1) * 512],
                                start=(fc == 0), stop=(fc == FC - 1),
                                r=[(pf, "ws2d", fc)] + xrr, w=[("ps", 2 + sub)])
                    self.act(RT[:, sub * D + half * 512:sub * D + (half + 1) * 512],
                             self.PS[2 + sub][:], AF.Sigmoid,
                             r=[("ps", 2 + sub)],
                             w=[(pf, "rt", sub), ("ps", 2 + sub)])
            for half in range(2):
                for dcb in range(FC):
                    self.dma("sync", WS2[:, dcb * 512:(dcb + 1) * 512],
                             self.wvt_p[dcb * 128:(dcb + 1) * 128,
                                        half * 512:(half + 1) * 512],
                             "ws2",
                             r=[("wvt_p", dcb // 2)], w=[(pf, "ws2d", dcb)])
                for sub in range(NSUB):
                    for dcb in range(FC):
                        self.mm(self.PS[2 + sub][:],
                                ka(dcb)[:, sub * 128:(sub + 1) * 128],
                                WS2[:, dcb * 512:(dcb + 1) * 512],
                                start=(dcb == 0), stop=(dcb == FC - 1),
                                r=[(pf, "ws2d", dcb), (pf, "ka", dcb)],
                                w=[("ps", 2 + sub)])
                    self.tt(VR[:, sub * D + half * 512:sub * D + (half + 1) * 512],
                            self.PS[2 + sub][:],
                            RT[:, sub * D + half * 512:sub * D + (half + 1) * 512],
                            ALU.mult,
                            r=[("ps", 2 + sub), (pf, "rt", sub)],
                            w=[(pf, "vr", sub), ("ps", 2 + sub)])
            for sub in range(NSUB):
                self.ln_stats(st, VR[:, sub * D:(sub + 1) * D], sub,
                              [(pf, "vr", sub)])
            self.ln_math(st, 0, NSUB)
            for sub in range(NSUB):
                g = c * NSUB + sub
                b = tb[sub]
                p = sub % 2
                for half in range(2):
                    self.ts(TMP[half],
                            VR[:, sub * D + half * 512:sub * D + (half + 1) * 512],
                            st["negm"][:, sub:sub + 1], st["rstd"][:, sub:sub + 1],
                            ALU.add, ALU.mult,
                            r=[(pf, "vr", sub), (spf, "negm", 0), (spf, "rstd", 0)],
                            w=[(pf, "tmp", half)])
                    self.tt(XO[p][:, half * 512:(half + 1) * 512],
                            DXI[b][:, half * 512:(half + 1) * 512],
                            TMP[half], ALU.add,
                            r=[(pf, "tmp", half), ("dxi", b)], w=[(pf, "xo", p)])
                self.dma("sync", self.out[s, g * 128:(g + 1) * 128, :], XO[p][:],
                         "out", r=[(pf, "xo", p)], w=[("out", s, g)])


def _layouts(i):
    def col16(v):
        return np.ascontiguousarray(v.reshape(16, 128).T)
    out = {}
    out["conv_w_r"] = np.ascontiguousarray(
        i["L0_attn_conv_w"].reshape(16, 128, 4).transpose(1, 0, 2).reshape(128, 64))
    out["conv_b_r"] = col16(i["L0_attn_conv_b"])
    out["b_dt_r"] = col16(i["L0_attn_b_dt"])
    out["d_r"] = col16(i["L0_attn_D"])
    out["a_log_r"] = np.ascontiguousarray(
        i["L0_attn_A_log"].reshape(16, 128, 16).transpose(1, 0, 2).reshape(128, 256))
    out["w_in"] = np.ascontiguousarray(
        i["L0_attn_W_in"].reshape(8, 128, 32, 128).transpose(2, 1, 0, 3)
        .reshape(32, 128, 1024))
    out["w_x"] = np.ascontiguousarray(
        i["L0_attn_W_x"].reshape(16, 128, 96).transpose(1, 0, 2).reshape(128, 16 * 96))
    out["w_dt"] = np.ascontiguousarray(i["L0_attn_W_dt"])
    out["w_out"] = np.ascontiguousarray(i["L0_attn_W_out"])
    out["w0t"] = np.ascontiguousarray(i["L0_ffn_weight"].T)
    out["gatet"] = np.ascontiguousarray(
        i["L0_ffn_gate_w"].T.reshape(8, 128, 8).transpose(1, 0, 2).reshape(128, 64))
    out["ut"] = np.ascontiguousarray(
        i["L0_ffn_lora_U"].reshape(128, 1024).T.reshape(8, 128, 128)
        .transpose(1, 0, 2).reshape(128, 8 * 128))
    out["v_r"] = np.ascontiguousarray(
        i["L0_ffn_lora_V"].transpose(0, 2, 1).reshape(128, 1024))
    out["wrt"] = np.ascontiguousarray(i["L1_ffn_W_r"].T)
    out["wkt"] = np.ascontiguousarray(
        i["L1_ffn_W_k"].T.reshape(8, 128, 8, 128).transpose(2, 1, 0, 3)
        .reshape(8, 128, 1024))
    out["wvt"] = np.ascontiguousarray(i["L1_ffn_W_v"].T)
    out["mur_r"] = np.ascontiguousarray(i["L1_ffn_mu_r"].reshape(8, 128).T)
    out["muk_r"] = np.ascontiguousarray(i["L1_ffn_mu_k"].reshape(8, 128).T)
    bf = ml_dtypes.bfloat16
    selb = np.zeros((32, NST * 128), np.float32)
    selc = np.zeros((32, NST * 128), np.float32)
    for n in range(NST):
        selb[n, n * 128:(n + 1) * 128] = 1.0
        selc[16 + n, n * 128:(n + 1) * 128] = 1.0
    out["selb"] = selb.astype(bf)
    out["selc"] = selc.astype(bf)
    rep = np.zeros((8, 128), np.float32)
    for e in range(8):
        rep[e, e * 16:(e + 1) * 16] = 1.0
    out["rep8"] = rep.astype(bf)
    out["ones1"] = np.ones((128, 1), np.float32).astype(bf)
    out["iden"] = np.eye(128, dtype=np.float32).astype(bf)
    return out


_NC_CACHE = None


_BUILDER = None


def _get_nc():
    global _NC_CACHE, _BUILDER
    if _NC_CACHE is None:
        _BUILDER = Builder()
        _NC_CACHE = _BUILDER.build()
    return _NC_CACHE


def kernel(**inputs):
    x = np.ascontiguousarray(np.asarray(inputs["L0_attn_x"], dtype=np.float32))
    shared = _layouts({k: np.asarray(v, dtype=np.float32)
                       for k, v in inputs.items() if k != "L0_attn_x"})
    nc = _get_nc()
    in_maps = []
    for i in range(8):
        m = dict(shared)
        m["x"] = np.ascontiguousarray(x[2 * i:2 * i + 2])
        in_maps.append(m)
    res = run_bass_kernel_spmd(nc, in_maps, core_ids=list(range(8)))
    out = np.stack([r["out"] for r in res.results], axis=0)
    return out.reshape(16, SEQ, D).astype(np.float32)


# version 10
# speedup vs baseline: 1.3021x; 1.2368x over previous
import sys
import numpy as np
import ml_dtypes
from contextlib import ExitStack

import concourse.bass as bass
import concourse.mybir as mybir
from concourse.bass_utils import run_bass_kernel_spmd

F32 = mybir.dt.float32
BF16 = mybir.dt.bfloat16
FP16 = mybir.dt.float16
AF = mybir.ActivationFunctionType
ALU = mybir.AluOpType
AXX = mybir.AxisListType.X

D = 1024
DI = 2048
SEQ = 2048
NSEQ = 2
NST = 16
DTR = 64
LC = 512
NCH = SEQ // LC
NSUB = LC // 128
FC = D // 128
DC = DI // 128
TS = SEQ // 128
EPS = 1e-5
UOF = 8
UW = UOF + 512

ENGS = ("tensor", "vector", "scalar", "gpsimd", "sync")
_A_ONLY = False
RELAXED = False


class Planner:

    def __init__(self):
        self.ops = []
        self.bufs = {}
        self.groups = {}
        self.nbar = 0
        self.last_eng = {}

    def _b(self, k):
        b = self.bufs.get(k)
        if b is None:
            b = {"w": None, "r": []}
            self.bufs[k] = b
        return b

    def _g(self, name):
        g = self.groups.get(name)
        if g is None:
            g = {"gen": 0, "count": 0, "bounds": [], "pending": set(), "last": None}
            self.groups[name] = g
        return g

    def add(self, eng, fn, r=(), w=(), grp=None, extra=(), bar_inc=False,
            bar_wait=None, async_out=False):
        i = len(self.ops)
        deps = set(extra)
        sdeps = set()
        for k in r:
            b = self._b(k)
            if b["w"] is not None:
                deps.add(b["w"])
        for k in w:
            b = self._b(k)
            if b["w"] is not None:
                sdeps.add(b["w"])
            sdeps.update(b["r"])
        deps.discard(i)
        sdeps.discard(i)
        deps -= sdeps
        gw = {}
        eng_deps = set()
        for j in deps | sdeps:
            oj = self.ops[j]
            if oj["grp"] is not None:
                gname, gen = oj["grp"]
                gw[gname] = max(gw.get(gname, -1), gen)
                gg = self._g(gname)
                if gen == gg["gen"]:
                    gg["pending"].add(i)
            else:
                eng_deps.add(j)
        gtag = None
        if grp is not None:
            g = self._g(grp)
            pend = {j for j in g["pending"] if j != i}
            if pend:
                g["bounds"].append(g["count"])
                g["gen"] += 1
                eng_deps.update(j for j in pend if self.ops[j]["grp"] is None)
                for j in pend:
                    oj = self.ops[j]
                    if oj["grp"] is not None:
                        gn, gg = oj["grp"]
                        gw[gn] = max(gw.get(gn, -1), gg)
                g["pending"] = set()
            done = g["bounds"][-1] if g["bounds"] else 0
            if gw.get(grp, -1) == g["gen"] and g["count"] > done:
                g["bounds"].append(g["count"])
                g["gen"] += 1
            g["count"] += 1
            g["last"] = i
            gtag = (grp, g["gen"])
        self.ops.append({"eng": eng, "fn": fn, "grp": gtag, "deps": eng_deps,
                         "gw": gw, "async_out": async_out,
                         "bar_inc": bar_inc, "bar_wait": bar_wait})
        for k in r:
            self._b(k)["r"].append(i)
        for k in w:
            b = self._b(k)
            b["w"] = i
            b["r"] = []
        if gtag is None and not bar_inc and bar_wait is None:
            self.last_eng[eng] = i
        return i

    def barrier(self):
        lasts = [i for i in self.last_eng.values()]
        last_dmas = [g["last"] for g in self.groups.values() if g["last"] is not None]
        for g in self.groups.values():
            done = g["bounds"][-1] if g["bounds"] else 0
            if g["count"] > done:
                g["bounds"].append(g["count"])
                g["gen"] += 1
        self.add("sync", lambda eng: eng.nop(), extra=lasts + last_dmas,
                 bar_inc=True)
        self.nbar += 1
        for e in ENGS:
            if e != "sync":
                self.add(e, lambda eng: eng.nop(), bar_wait=self.nbar)

    def finalize(self):
        n = len(self.ops)
        for g in self.groups.values():
            g["bounds"].append(g["count"])
        for o in self.ops:
            if o["grp"] is None and o["eng"] == "tensor":
                o["deps"] = {j for j in o["deps"]
                             if self.ops[j]["eng"] != "tensor"
                             or self.ops[j]["grp"] is not None
                             or self.ops[j]["async_out"]}
        if RELAXED:
            for o in self.ops:
                if o["grp"] is None:
                    o["deps"] = {j for j in o["deps"]
                                 if self.ops[j]["eng"] != o["eng"]
                                 or self.ops[j]["grp"] is not None
                                 or self.ops[j]["async_out"]}
            need = [False] * n
            for o in self.ops:
                for j in o["deps"]:
                    need[j] = True
        else:
            need = [o["grp"] is None and not o["bar_inc"] and o["bar_wait"] is None
                    for o in self.ops]
        cum = {}
        val = [0] * n
        for i, o in enumerate(self.ops):
            if o["grp"] is not None:
                continue
            if need[i]:
                s = "e_" + o["eng"]
                cum[s] = cum.get(s, 0) + 1
                val[i] = cum[s]

        def gen_threshold(gname, gen):
            b = self.groups[gname]["bounds"]
            k = min(gen, len(b) - 1)
            return 16 * b[k]

        sched = {k: [] for k in ENGS}
        floor = {e: {} for e in ENGS}
        for i, o in enumerate(self.ops):
            wmap = {}
            for j in o["deps"]:
                if val[j] > 0:
                    s = "e_" + self.ops[j]["eng"]
                    wmap[s] = max(wmap.get(s, 0), val[j])
            for gname, gen in o["gw"].items():
                s = "g_" + gname
                wmap[s] = max(wmap.get(s, 0), gen_threshold(gname, gen))
            if o["bar_wait"] is not None:
                wmap["bar"] = o["bar_wait"]
            if o["bar_inc"]:
                inc = ("bar", 1)
                nbc = self.bar_cnt = getattr(self, "bar_cnt", 0) + 1
                if nbc > 1:
                    wmap["bar"] = max(wmap.get("bar", 0), nbc - 1)
            elif o["grp"] is not None:
                inc = ("g_" + o["grp"][0], 16)
            elif need[i]:
                inc = ("e_" + o["eng"], 1)
                if not RELAXED and o["eng"] != "tensor":
                    s = inc[0]
                    if val[i] > 1:
                        wmap[s] = max(wmap.get(s, 0), val[i] - 1)
            else:
                inc = None
            fl = floor[o["eng"]]
            wl = []
            for s, v in sorted(wmap.items()):
                if fl.get(s, -1) < v:
                    fl[s] = v
                    wl.append((s, v))
            sched[o["eng"]].append((o["fn"], wl, inc))
        return sched, cum


class Builder:
    def __init__(self):
        self.nc = bass.Bass("TRN2", target_bir_lowering=False,
                            detect_race_conditions=not RELAXED)
        self.pl = Planner()
        self.ctx = ExitStack()
        self.arena_off = 0

    def sb(self, name, shape, dt):
        return self.ctx.enter_context(self.nc.sbuf_tensor(name, shape, dt))

    def view(self, dt, cols):
        esz = 2 if dt in (BF16, FP16) else 4
        off = (self.arena_off + 63) & ~63
        self.arena_off = off + cols * esz
        assert self.arena_off <= self.ARENA_BYTES, \
            f"arena overflow: {self.arena_off} > {self.ARENA_BYTES}"
        base = self.arena.ap() if dt == BF16 else self.arena.ap().bitcast(dt)
        return base[:, off // esz: off // esz + cols]

    def reset_arena(self):
        self.arena_off = 0

    def dma(self, q, out, in_, grp, r=(), w=()):
        self.pl.add(q, lambda e, o=out, i=in_: e.dma_start(out=o, in_=i),
                    r=r, w=w, grp=grp)

    def dmat(self, out, in_, r=(), w=()):
        self.pl.add("scalar",
                    lambda e, o=out, i=in_: e.dma_start_transpose(out=o, in_=i),
                    r=r, w=w, grp="tp")

    def mm(self, out, lhsT, rhs, start, stop, r=(), w=()):
        self.pl.add("tensor",
                    lambda e, o=out, l=lhsT, rr=rhs, s1=start, s2=stop:
                    e.matmul(o, l, rr, start=s1, stop=s2), r=r, w=w)

    def tpose(self, out, in_, r=(), w=()):
        self.pl.add("tensor",
                    lambda e, o=out, i=in_, d=self.IDEN[:]: e.transpose(o, i, d),
                    r=r, w=w)

    def act(self, out, in_, func, bias=0.0, scale=1.0, accum=None, r=(), w=()):
        self.pl.add("scalar",
                    lambda e, o=out, i=in_, f=func, b=bias, s=scale, a=accum:
                    e.activation(out=o, in_=i, func=f, bias=b, scale=s, accum_out=a),
                    r=r, w=w, async_out=accum is not None)

    def tt(self, out, a, b, op, r=(), w=()):
        self.pl.add("vector",
                    lambda e, o=out, x=a, y=b, p=op:
                    e.tensor_tensor(out=o, in0=x, in1=y, op=p), r=r, w=w)

    def ts(self, out, a, s1, s2, op0, op1, accum=None, r=(), w=()):
        if s2 is None and accum is None:
            self.pl.add("vector",
                        lambda e, o=out, x=a, u=s1, p0=op0:
                        e.tensor_scalar(out=o, in0=x, scalar1=u, scalar2=None, op0=p0),
                        r=r, w=w)
        else:
            self.pl.add("vector",
                        lambda e, o=out, x=a, u=s1, v=s2, p0=op0, p1=op1, ac=accum:
                        e.tensor_scalar(out=o, in0=x, scalar1=u, scalar2=v,
                                        op0=p0, op1=p1, accum_out=ac), r=r, w=w,
                        async_out=accum is not None)

    def stt(self, out, in0, scal, in1, op0, op1, r=(), w=()):
        self.pl.add("vector",
                    lambda e, o=out, x=in0, u=scal, y=in1, p0=op0, p1=op1:
                    e.scalar_tensor_tensor(out=o, in0=x, scalar=u, in1=y,
                                           op0=p0, op1=p1), r=r, w=w)

    def scan(self, out, a, b, init, r=(), w=()):
        self.pl.add("vector",
                    lambda e, o=out, x=a, y=b, h0=init:
                    e.tensor_tensor_scan(out=o, data0=x, data1=y, initial=h0,
                                         op0=ALU.mult, op1=ALU.add), r=r, w=w)

    def vcopy(self, out, in_, r=(), w=()):
        self.pl.add("vector", lambda e, o=out, i=in_: e.tensor_copy(out=o, in_=i),
                    r=r, w=w)

    def gcopy(self, out, in_, r=(), w=()):
        self.pl.add("gpsimd", lambda e, o=out, i=in_: e.tensor_copy(out=o, in_=i),
                    r=r, w=w)

    def vmemset(self, ap, val, r=(), w=()):
        self.pl.add("vector", lambda e, a=ap, v=val: e.memset(a, v), r=r, w=w)

    def rmax(self, out, in_, negate, r=(), w=()):
        self.pl.add("vector",
                    lambda e, o=out, i=in_, ng=negate:
                    e.reduce_max(out=o, in_=i, axis=AXX, negate=ng), r=r, w=w)

    def rsumf(self, out, in_, r=(), w=()):
        self.pl.add("vector",
                    lambda e, o=out, i=in_: e.reduce_sum(out=o, in_=i, axis=AXX),
                    r=r, w=w)

    def recip(self, out, in_, r=(), w=()):
        self.pl.add("vector", lambda e, o=out, i=in_: e.reciprocal(out=o, in_=i),
                    r=r, w=w)

    def build(self):
        nc = self.nc
        self.x_in = nc.dram_tensor("x", [NSEQ, SEQ, D], F32, kind="ExternalInput").ap()
        self.out = nc.dram_tensor("out", [NSEQ, SEQ, D], F32, kind="ExternalOutput").ap()
        f32in = lambda n, sh: nc.dram_tensor(n, sh, F32, kind="ExternalInput").ap()
        bf16in = lambda n, sh: nc.dram_tensor(n, sh, BF16, kind="ExternalInput").ap()
        self.w_in = f32in("w_in", [2 * DC, 128, 8 * 128])
        self.conv_w_r = f32in("conv_w_r", [128, 64])
        self.conv_b_r = f32in("conv_b_r", [128, 16])
        self.b_dt_r = f32in("b_dt_r", [128, 16])
        self.d_r = f32in("d_r", [128, 16])
        self.a_log_r = f32in("a_log_r", [128, 256])
        self.w_x = f32in("w_x", [128, DC * 96])
        self.w_dt = f32in("w_dt", [DTR, DI])
        self.w_out = f32in("w_out", [DI, D])
        self.w0t = f32in("w0t", [D, D])
        self.gatet = f32in("gatet", [128, FC * 8])
        self.ut = f32in("ut", [128, FC * 128])
        self.v_r = f32in("v_r", [128, D])
        self.wrt = f32in("wrt", [D, D])
        self.wkt = f32in("wkt", [FC, 128, FC * 128])
        self.wvt = f32in("wvt", [D, D])
        self.mur_r = f32in("mur_r", [128, 8])
        self.muk_r = f32in("muk_r", [128, 8])
        self.selb = bf16in("selb", [32, NST * 128])
        self.selc = bf16in("selc", [32, NST * 128])
        self.rep8 = bf16in("rep8", [8, 128])
        self.ones1 = bf16in("ones1", [128, 1])
        self.iden = bf16in("iden", [128, 128])

        self.win_p = nc.dram_tensor("win_p", [2 * DC, 128, 8 * 128], BF16).ap()
        self.wout_p = nc.dram_tensor("wout_p", [DI, D], BF16).ap()
        self.w0t_p = nc.dram_tensor("w0t_p", [D, D], BF16).ap()
        self.wrt_p = nc.dram_tensor("wrt_p", [D, D], BF16).ap()
        self.wkt_p = nc.dram_tensor("wkt_p", [FC, 128, FC * 128], BF16).ap()
        self.wvt_p = nc.dram_tensor("wvt_p", [D, D], BF16).ap()
        self.xres_h = nc.dram_tensor("xres_h", [NSEQ, SEQ, D], F32).ap()

        self.ARENA_BYTES = 172544
        self.arena = self.sb("arena", [128, self.ARENA_BYTES // 2], BF16)
        self.WSTR = self.sb("wstr", [128, 4 * 1024], BF16)
        self.WX = self.sb("wxr", [128, DC * 96], BF16)
        self.WDT = self.sb("wdtr", [DTR, DI], BF16)
        self.UTS = self.sb("uts", [128, FC * 128], BF16)
        self.VSB = self.sb("vsb", [128, D], BF16)
        self.GATET = self.sb("gatetr", [128, FC * 8], BF16)
        self.SELB = self.sb("selbr", [128, NST * 128], BF16)
        self.SELC = self.sb("selcr", [128, NST * 128], BF16)
        self.REP8 = self.sb("rep8r", [8, 128], BF16)
        self.ONES1 = self.sb("ones1r", [128, 1], BF16)
        self.IDEN = self.sb("idenr", [128, 128], BF16)
        self.CONVW = self.sb("convw", [128, 64], F32)
        self.CONVB = self.sb("convb", [128, 16], F32)
        self.BDT = self.sb("bdt", [128, 16], F32)
        self.DCOL = self.sb("dcol", [128, 16], F32)
        self.ANEG = self.sb("aneg", [128, 256], F32)
        self.MUR = self.sb("mur", [128, 8], F32)
        self.MUK = self.sb("muk", [128, 8], F32)
        self.HST = self.sb("hst", [128, 512], F32)
        self.DV = self.sb("dv", [128, 1024], BF16)
        self.DA = self.sb("da", [128, 1024], BF16)
        def stats_set(tag):
            t = lambda nm, c: self.sb(tag + nm, [128, c], F32)
            return {"pf": tag, "s1": t("s1", 16), "s2": t("s2", 16),
                    "negm": t("nm", 16), "rstd": t("rs", 16),
                    "sa": t("sa", 16), "sb": t("sb", 16),
                    "sv": t("sv", 16), "sw": t("sw", 16),
                    "aux": t("ax", 16), "aux2": t("a2", 16),
                    "dv": self.DV, "da": self.DA}
        self.STA = stats_set("LA")
        self.STB = stats_set("LB")
        self.RINV = self.sb("rinv", [128, 4], F32)
        self.NM = self.sb("nm", [128, 2], F32)
        self.GE = self.sb("ge", [128, 16], F32)
        self.GS = self.sb("gs", [128, 4], F32)
        self.GSUB = self.sb("gsub", [128, 16], BF16)
        self.GFM = self.sb("gfm", [8, 2 * LC], BF16)
        self.FCONST = self.sb("fconst", [128, 2], F32)
        self.nc.const_aps.register(F32, EPS, self.FCONST[:, 0:1])
        self.nc.const_aps.register(F32, 1.0, self.FCONST[:, 1:2])
        self.PS = [self.ctx.enter_context(self.nc.psum_tensor(f"ps{i}", [128, 512], F32))
                   for i in range(8)]

        self.prep()
        self.pl.barrier()
        self.reset_arena()
        self.phaseA_merged()
        if not _A_ONLY:
            self.pl.barrier()
            self.reset_arena()
            self.phaseB_merged()
            for s in range(NSEQ):
                self.pl.barrier()
                self.reset_arena()
                self.phaseC(s)
                self.pl.barrier()
                self.reset_arena()
                self.phaseD(s)

        self.pl.add("sync", lambda e: e.nop(),
                    r=[("out", s, g) for s in range(NSEQ) for g in range(TS)])

        sched, cum = self.pl.finalize()
        sems = {}
        names = ["e_tensor", "e_vector", "e_scalar", "e_gpsimd", "e_sync", "bar"]
        names += ["g_" + g for g in self.pl.groups]
        for name in names:
            sems[name] = self.ctx.enter_context(nc.semaphore(name))

        def body(eng_obj, items):
            for fn, wl, inc in items:
                for sname, v in wl:
                    eng_obj.wait_ge(sems[sname], v)
                inst = fn(eng_obj)
                if inc is not None:
                    inst.then_inc(sems[inc[0]], inc[1])

        with nc.Block() as blk:
            blk.tensor(lambda e: body(e, sched["tensor"]))
            blk.vector(lambda e: body(e, sched["vector"]))
            blk.scalar(lambda e: body(e, sched["scalar"]))
            blk.gpsimd(lambda e: body(e, sched["gpsimd"]))
            blk.sync(lambda e: body(e, sched["sync"]))
        self.ctx.close()
        return nc

    def prep(self):
        self.vmemset(self.FCONST[:, 0:1], EPS, w=[("fconst",)])
        self.vmemset(self.FCONST[:, 1:2], 1.0, w=[("fconst",)])
        for dst, src, key in [(self.win_p, self.w_in, "win_p"),
                              (self.wout_p, self.w_out, "wout_p"),
                              (self.w0t_p, self.w0t, "w0t_p"),
                              (self.wrt_p, self.wrt, "wrt_p"),
                              (self.wkt_p, self.wkt, "wkt_p"),
                              (self.wvt_p, self.wvt, "wvt_p")]:
            rows = src.shape[0]
            q = rows // 4
            for j in range(4):
                self.dma("gpsimd", dst[j * q:(j + 1) * q], src[j * q:(j + 1) * q],
                         "p_" + key, w=[(key, j)])
        self.dma("gpsimd", self.WX[:], self.w_x[:], "pres", w=[("wx",)])
        self.dma("gpsimd", self.WDT[:], self.w_dt[:], "pres", w=[("wdt",)])
        self.dma("gpsimd", self.UTS[:], self.ut[:], "pres", w=[("uts",)])
        self.dma("gpsimd", self.VSB[:], self.v_r[:], "pres", w=[("vsb",)])
        self.dma("gpsimd", self.GATET[:], self.gatet[:], "pres", w=[("gatet",)])
        for dst, src, key in [(self.CONVW, self.conv_w_r, "convw"),
                              (self.CONVB, self.conv_b_r, "convb"),
                              (self.BDT, self.b_dt_r, "bdt"),
                              (self.DCOL, self.d_r, "dcol"),
                              (self.MUR, self.mur_r, "mur"),
                              (self.MUK, self.muk_r, "muk")]:
            self.dma("sync", dst[:], src[:], "psml", w=[(key,)])
        self.dma("sync", self.SELB[64:96, :], self.selb[:], "pcon", w=[("selb",)])
        self.dma("sync", self.SELC[64:96, :], self.selc[:], "pcon", w=[("selc",)])
        self.dma("sync", self.REP8[:], self.rep8[:], "pcon", w=[("rep8",)])
        self.dma("sync", self.ONES1[:], self.ones1[:], "pcon", w=[("ones1",)])
        self.dma("sync", self.IDEN[:], self.iden[:], "pcon", w=[("iden",)])
        self.dma("sync", self.ANEG[:], self.a_log_r[:], "pan", w=[("aneg",)])
        self.act(self.ANEG[:], self.ANEG[:], AF.Exp, r=[("aneg",)], w=[("aneg",)])
        self.ts(self.ANEG[:], self.ANEG[:], -1.0, None, ALU.mult, None,
                r=[("aneg",)], w=[("aneg",)])

    def ln_stats(self, st, src_ap, col, key_r):
        pf = st["pf"]
        self.ts(st["dv"][:, 0:src_ap.shape[1]], src_ap, 1.0, 0.0, ALU.mult, ALU.add,
                accum=st["s1"][:, col:col + 1],
                r=key_r, w=[(pf, "s1", col), (pf, "dumpv")])
        self.act(st["da"][:, 0:src_ap.shape[1]], src_ap, AF.Square,
                 accum=st["s2"][:, col:col + 1],
                 r=key_r, w=[(pf, "s2", col), (pf, "dumpa")])

    def ln_math(self, st, lo, n, want_r=False):
        pf = st["pf"]
        sl = slice(lo, lo + n)
        rs1 = [(pf, "s1", c) for c in range(lo, lo + n)]
        rs2 = [(pf, "s2", c) for c in range(lo, lo + n)]
        self.ts(st["negm"][:, sl], st["s1"][:, sl], -1.0 / D, None, ALU.mult, None,
                r=rs1, w=[(pf, "negm", lo)])
        self.ts(st["sa"][:, sl], st["s1"][:, sl], 1.0 / D, None, ALU.mult, None,
                r=rs1, w=[(pf, "sa", lo)])
        self.tt(st["sb"][:, sl], st["sa"][:, sl], st["sa"][:, sl], ALU.mult,
                r=[(pf, "sa", lo)], w=[(pf, "sb", lo)])
        self.stt(st["sv"][:, sl], st["s2"][:, sl], 1.0 / D, st["sb"][:, sl],
                 ALU.mult, ALU.subtract, r=rs2 + [(pf, "sb", lo)], w=[(pf, "sv", lo)])
        self.act(st["sw"][:, sl], st["sv"][:, sl], AF.Ln, bias=EPS,
                 r=[(pf, "sv", lo)], w=[(pf, "sw", lo)])
        self.act(st["rstd"][:, sl], st["sw"][:, sl], AF.Exp, scale=-0.5,
                 r=[(pf, "sw", lo)], w=[(pf, "rstd", lo)])
        if want_r:
            self.tt(st["aux"][:, sl], st["rstd"][:, sl], st["rstd"][:, sl], ALU.mult,
                    r=[(pf, "rstd", lo)], w=[(pf, "aux", lo)])
            self.stt(st["aux"][:, sl], st["sv"][:, sl], float(D), st["aux"][:, sl],
                     ALU.mult, ALU.mult,
                     r=[(pf, "sv", lo), (pf, "aux", lo)], w=[(pf, "aux", lo)])
            self.ts(st["aux"][:, sl], st["aux"][:, sl], 1e-20, None, ALU.max, None,
                    r=[(pf, "aux", lo)], w=[(pf, "aux", lo)])
            self.act(st["aux"][:, sl], st["aux"][:, sl], AF.Ln,
                     r=[(pf, "aux", lo)], w=[(pf, "aux", lo)])
            self.act(st["aux"][:, sl], st["aux"][:, sl], AF.Exp, scale=-0.5,
                     r=[(pf, "aux", lo)], w=[(pf, "aux", lo)])
            self.tt(st["aux2"][:, sl], st["rstd"][:, sl], st["aux"][:, sl], ALU.mult,
                    r=[(pf, "rstd", lo), (pf, "aux", lo)], w=[(pf, "aux2", lo)])

    def phaseA_merged(self):
        v = self.view
        self.AU = [v(BF16, DC * UW) for _ in range(2)]
        self.ADEL = v(BF16, DC * 512)
        self.AW = v(BF16, DC * 512)
        self.AY = v(FP16, DC * 512)
        self.AXFA = [v(BF16, FC * 512) for _ in range(2)]
        self.AX0 = [v(F32, 1024) for _ in range(4)]
        self.AXH = [v(BF16, 1024) for _ in range(4)]
        self.AXO = [v(F32, 1024) for _ in range(4)]
        self.ADR = v(BF16, 512)
        self.AXDB = [v(BF16, 512) for _ in range(2)]
        self.AE1 = v(F32, 512)
        self.ABBC = [v(BF16, 512) for _ in range(2)]
        self.ACBC = [v(BF16, 512) for _ in range(2)]
        self.AA2 = [v(FP16, 512) for _ in range(4)]
        self.AB2 = [v(BF16, 512) for _ in range(2)]
        self.AH2 = [v(BF16, 512) for _ in range(2)]
        self.AYC = [v(BF16, 512) for _ in range(2)]
        self.AACC = v(BF16, 512)
        self.AGR = [v(BF16, 512) for _ in range(8)]
        self.ATMP = [v(F32, 512) for _ in range(2)]
        self.x0cnt = 0
        self.xhcnt = 0
        self.a2c = 0
        self.rx0 = {}
        NSTEP = 2 * NCH

        self.vmemset(self.HST[:], 0.0,
                     w=[("hst", s, n, dc) for s in range(2)
                        for n in range(NST) for dc in range(DC)])
        for q in range(2):
            u3 = self.AU[q].rearrange("p (dc r) -> p dc r", dc=DC)
            self.vmemset(u3[:, :, UOF - 3:UOF], 0.0,
                         w=[("tl", q, dc) for dc in range(DC)])

        for j0 in (0, 1):
            self.frontA_LN(j0)
            self.frontA_dmats(j0, (0, 1))
            self.frontA_dmats(j0, (2, 3))
            self.frontA_win(j0)
            self.frontA_conv(j0)
        self.frontB_xdbl(0)
        self.frontB_delta(0, 0, 16)
        for j in range(NSTEP):
            nxt = j + 1 < NSTEP
            self.sweepA(j, 0, nxt)
            self.gateA(j, 0)
            if nxt and j + 1 >= 2:
                self.frontA_conv(j + 1)
            self.sweepA(j, 1, nxt)
            self.gateA(j, 8)
            if nxt:
                self.frontB_delta(j + 1, 8, 16)
            self.woutA_close(j)
            self.outLN_store(j)

    def autile(self, q, dc):
        return self.AU[q][:, dc * UW: dc * UW + UW]

    def adtile(self, dc):
        return self.ADEL[:, dc * 512:(dc + 1) * 512]

    def awtile(self, dc):
        return self.AW[:, dc * 512:(dc + 1) * 512]

    def aytile(self, dc):
        return self.AY[:, dc * 512:(dc + 1) * 512]

    def axfa(self, q, fc):
        return self.AXFA[q][:, fc * 512:(fc + 1) * 512]

    def frontA_LN(self, j):
        s, c = j % 2, j // 2
        q = j % 2
        lo = 4 * q
        st = self.STA
        pf = st["pf"]
        for sub in range(NSUB):
            g = c * NSUB + sub
            b = self.x0cnt % 4
            self.x0cnt += 1
            self.dma("sync", self.AX0[b][:], self.x_in[s, g * 128:(g + 1) * 128, :],
                     f"x0{b}", w=[("x0", b)])
            self.ln_stats(st, self.AX0[b][:], lo + sub, [("x0", b)])
        self.ln_math(st, lo, NSUB)
        for sub in range(NSUB):
            g = c * NSUB + sub
            b = self.x0cnt % 4
            self.x0cnt += 1
            rx = sub
            self.dma("sync", self.AX0[b][:], self.x_in[s, g * 128:(g + 1) * 128, :],
                     f"x0{b}", w=[("x0", b)])
            self.ts(self.AXH[rx], self.AX0[b][:],
                    st["negm"][:, lo + sub:lo + sub + 1],
                    st["rstd"][:, lo + sub:lo + sub + 1], ALU.add, ALU.mult,
                    r=[("x0", b), (pf, "negm", lo), (pf, "rstd", lo)],
                    w=[("xh", rx)])

    def frontA_dmats(self, j, subs):
        q = j % 2
        for sub in subs:
            rx = sub
            for fc in range(FC):
                self.dmat(self.axfa(q, fc)[:, sub * 128:(sub + 1) * 128],
                          self.AXH[rx][:, fc * 128:(fc + 1) * 128],
                          r=[("xh", rx)], w=[("xfa", q, fc, sub)])

    def frontA_win(self, j):
        q = j % 2
        xfar = [("xfa", q, fc, sub) for fc in range(FC) for sub in range(NSUB)]
        for oc in range(DC):
            p = oc % 2
            wb = self.WSTR[:, p * 1024:p * 1024 + 1024]
            self.dma("sync", wb, self.win_p[oc], f"wb{p}",
                     r=[("win_p", jj) for jj in range(4)], w=[("wbd", p)])
            pb = self.PS[p]
            for fc in range(FC):
                self.mm(pb[:], wb[:, fc * 128:(fc + 1) * 128], self.axfa(q, fc),
                        start=(fc == 0), stop=(fc == FC - 1),
                        r=[("wbd", p)] + xfar, w=[("ps", p)])
            self.act(self.autile(q, oc)[:, UOF:UOF + 512], pb[:], AF.Copy,
                     r=[("ps", p)], w=[("u", q, oc), ("ps", p)])

    def frontA_conv(self, j):
        q = j % 2
        for dc in range(DC):
            ut = self.autile(q, dc)
            self.ts(self.AACC, ut[:, UOF - 3:UOF + 509],
                    self.CONVW[:, dc * 4:dc * 4 + 1], None, ALU.mult, None,
                    r=[("u", q, dc), ("tl", q, dc), ("convw",)], w=[("acc",)])
            for k in range(1, 4):
                self.stt(self.AACC, ut[:, UOF - 3 + k:UOF + 509 + k],
                         self.CONVW[:, dc * 4 + k:dc * 4 + k + 1], self.AACC,
                         ALU.mult, ALU.add,
                         r=[("u", q, dc), ("tl", q, dc), ("acc",), ("convw",)],
                         w=[("acc",)])
            self.gcopy(ut[:, UOF - 3:UOF], ut[:, UOF + 509:UOF + 512],
                       r=[("u", q, dc)], w=[("tl", q, dc)])
            self.act(ut[:, UOF:UOF + 512], self.AACC, AF.Silu,
                     bias=self.CONVB[:, dc:dc + 1],
                     r=[("acc",), ("convb",)], w=[("u", q, dc)])

    def frontB_xdbl(self, j):
        q = j % 2
        px = self.PS[2]
        for kc in range(DC):
            self.mm(px[0:96, :], self.WX[:, kc * 96:(kc + 1) * 96],
                    self.autile(q, kc)[:, UOF:UOF + 512],
                    start=(kc == 0), stop=(kc == DC - 1),
                    r=[("u", q, kc), ("wx",)], w=[("ps", 2)])
        self.act(self.ADR[0:64, :], px[0:64, :], AF.Copy,
                 r=[("ps", 2)], w=[("dr",)])
        self.act(self.AXDB[q][64:96, :], px[64:96, :], AF.Copy,
                 r=[("ps", 2)], w=[("xdb", q), ("ps", 2)])

    def frontB_delta(self, j, d0, d1):
        q = j % 2
        for dc in range(d0, d1):
            pi = 3 if dc % 2 == 0 else 2
            pd = self.PS[pi]
            self.mm(pd[:], self.WDT[:, dc * 128:(dc + 1) * 128], self.ADR[0:64, :],
                    start=True, stop=True,
                    r=[("dr",), ("wdt",)], w=[("ps", pi)])
            self.act(self.AE1, pd[:], AF.Exp, bias=self.BDT[:, dc:dc + 1],
                     r=[("ps", pi), ("bdt",)], w=[("e1",), ("ps", pi)])
            self.act(self.adtile(dc), self.AE1, AF.Ln, bias=1.0,
                     r=[("e1",)], w=[("del", dc)])
            self.tt(self.awtile(dc), self.adtile(dc),
                    self.autile(q, dc)[:, UOF:UOF + 512], ALU.mult,
                    r=[("del", dc), ("u", q, dc)], w=[("w", dc)])

    def resA(self, j, base):
        q = j % 2
        xfar = [("xfa", q, fc, sub) for fc in range(FC) for sub in range(NSUB)]
        for k in range(8):
            oc = DC + base + k
            p = oc % 2
            wb = self.WSTR[:, p * 1024:p * 1024 + 1024]
            self.dma("sync", wb, self.win_p[oc], f"wb{p}",
                     r=[("win_p", jj) for jj in range(4)], w=[("wbd", p)])
            pb = self.PS[p]
            for fc in range(FC):
                self.mm(pb[:], wb[:, fc * 128:(fc + 1) * 128], self.axfa(q, fc),
                        start=(fc == 0), stop=(fc == FC - 1),
                        r=[("wbd", p)] + xfar, w=[("ps", p)])
            self.act(self.AGR[k], pb[:], AF.Silu,
                     r=[("ps", p)], w=[("gr", k), ("ps", p)])

    def gateA(self, j, base):
        q = j % 2
        for k in range(8):
            dc = base + k
            p = dc % 2
            self.stt(self.ATMP[p], self.autile(q, dc)[:, UOF:UOF + 512],
                     self.DCOL[:, dc:dc + 1], self.aytile(dc), ALU.mult, ALU.add,
                     r=[("u", q, dc), ("dcol",), ("y", dc)], w=[("tmp", p)])
            self.tt(self.autile(q, dc)[:, UOF:UOF + 512], self.ATMP[p],
                    self.AGR[k], ALU.mult,
                    r=[("tmp", p), ("gr", k), ("u", q, dc)], w=[("u", q, dc)])

    def prefX0(self, j):
        s, c = j % 2, j // 2
        bl = []
        for sub in range(NSUB):
            g = c * NSUB + sub
            b = self.x0cnt % 4
            self.x0cnt += 1
            self.dma("sync", self.AX0[b][:], self.x_in[s, g * 128:(g + 1) * 128, :],
                     f"x0{b}", w=[("x0", b)])
            bl.append(b)
        self.rx0[j] = bl

    WOBANK = (0, 1, 6, 7)

    def woutA_chain(self, j, half, d0, d1, stop_end):
        q = j % 2
        for dc in range(d0, d1):
            wsl = dc % 2
            wb = self.WSTR[:, 2048 + wsl * 512:2048 + wsl * 512 + 512]
            self.dma("sync", wb,
                     self.wout_p[dc * 128:(dc + 1) * 128,
                                 half * 512:(half + 1) * 512],
                     f"wo{wsl}",
                     r=[("wout_p", dc // 4)], w=[("wod", wsl)])
            for sub in range(NSUB):
                bk = self.WOBANK[sub]
                self.mm(self.PS[bk][:],
                        self.autile(q, dc)[:, UOF + sub * 128:UOF + (sub + 1) * 128],
                        wb[:, 0:512],
                        start=(dc == 0), stop=(stop_end and dc == d1 - 1),
                        r=[("wod", wsl), ("u", q, dc)], w=[("ps", bk)])

    def woutA_evac(self, j, half):
        for sub in range(NSUB):
            bk = self.WOBANK[sub]
            self.act(self.AXO[sub][:, half * 512:(half + 1) * 512],
                     self.PS[bk][:], AF.Copy,
                     r=[("ps", bk)], w=[("xo", sub), ("ps", bk)])

    def woutA_close(self, j):
        self.woutA_chain(j, 0, 0, 16, True)
        self.woutA_evac(j, 0)
        self.woutA_chain(j, 1, 0, 16, True)
        self.woutA_evac(j, 1)

    def outLN_store(self, j):
        s, c = j % 2, j // 2
        q = j % 2
        st = self.STA
        pf = st["pf"]
        lo = 8 + 4 * q
        if _A_ONLY:
            for sub in range(NSUB):
                g = c * NSUB + sub
                self.dma("gpsimd", self.out[s, g * 128:(g + 1) * 128, :],
                         self.AXO[sub][:], "out", r=[("xo", sub)],
                         w=[("odbg", s, g)])
        for sub in range(NSUB):
            self.ln_stats(st, self.AXO[sub][:], lo + sub, [("xo", sub)])
        self.ln_math(st, lo, NSUB)
        for sub in range(NSUB):
            g = c * NSUB + sub
            b = self.rx0[j][sub]
            for half in range(2):
                hp = half
                self.ts(self.ATMP[hp],
                        self.AXO[sub][:, half * 512:(half + 1) * 512],
                        st["negm"][:, lo + sub:lo + sub + 1],
                        st["rstd"][:, lo + sub:lo + sub + 1], ALU.add, ALU.mult,
                        r=[("xo", sub), (pf, "negm", lo), (pf, "rstd", lo)],
                        w=[("tmp", hp)])
                self.tt(self.AXO[sub][:, half * 512:(half + 1) * 512],
                        self.AX0[b][:, half * 512:(half + 1) * 512],
                        self.ATMP[hp], ALU.add,
                        r=[("x0", b), ("tmp", hp)], w=[("xo", sub)])
            self.dma("gpsimd", self.xres_h[s, g * 128:(g + 1) * 128, :],
                     self.AXO[sub][:], "xw",
                     r=[("xo", sub)], w=[("xrh", s, g)])

    def sweepA(self, j, sh, nxt):
        s = j % 2
        q = j % 2
        d0 = 8 * sh
        for n in range(NST):
            if sh == 0:
                if nxt and j + 1 >= 2:
                    if n == 4:
                        self.frontA_LN(j + 1)
                        self.frontA_dmats(j + 1, (0, 1))
                    elif n == 6:
                        self.frontA_dmats(j + 1, (2, 3))
                    elif n == 8:
                        self.frontA_win(j + 1)
                if n == 11:
                    self.resA(j, 0)
            else:
                if n == 2:
                    self.resA(j, 8)
                if nxt:
                    if n == 4:
                        self.frontB_xdbl(j + 1)
                    elif n == 6:
                        self.frontB_delta(j + 1, 0, 8)
                if n == 8:
                    self.prefX0(j)
            nb = n % 2
            self.mm(self.PS[4][:], self.SELB[64:96, n * 128:(n + 1) * 128],
                    self.AXDB[q][64:96, :], start=True, stop=True,
                    r=[("xdb", q), ("selb",)], w=[("ps", 4)])
            self.act(self.ABBC[nb], self.PS[4][:], AF.Copy,
                     r=[("ps", 4)], w=[("bbc", nb), ("ps", 4)])
            self.mm(self.PS[5][:], self.SELC[64:96, n * 128:(n + 1) * 128],
                    self.AXDB[q][64:96, :], start=True, stop=True,
                    r=[("xdb", q), ("selc",)], w=[("ps", 5)])
            self.vcopy(self.ACBC[nb], self.PS[5][:],
                       r=[("ps", 5)], w=[("cbc", nb), ("ps", 5)])
            for dc in range(d0, d0 + 8):
                p = dc % 2
                a2 = self.a2c % 4
                self.a2c += 1
                col = s * 256 + dc * 16 + n
                acol = dc * 16 + n
                self.act(self.AA2[a2], self.adtile(dc), AF.Exp,
                         scale=self.ANEG[:, acol:acol + 1],
                         r=[("del", dc), ("aneg",)], w=[("a2", a2)])
                self.tt(self.AB2[p], self.awtile(dc), self.ABBC[nb], ALU.mult,
                        r=[("w", dc), ("bbc", nb)], w=[("b2", p)])
                self.scan(self.AH2[p], self.AA2[a2], self.AB2[p],
                          self.HST[:, col:col + 1],
                          r=[("a2", a2), ("b2", p), ("hst", s, n, dc)],
                          w=[("h2", p)])
                self.gcopy(self.HST[:, col:col + 1], self.AH2[p][:, 511:512],
                           r=[("h2", p)], w=[("hst", s, n, dc)])
                if n == 0:
                    self.tt(self.aytile(dc), self.AH2[p], self.ACBC[nb], ALU.mult,
                            r=[("h2", p), ("cbc", nb)], w=[("y", dc)])
                else:
                    self.tt(self.AYC[p], self.AH2[p], self.ACBC[nb], ALU.mult,
                            r=[("h2", p), ("cbc", nb)], w=[("yc", p)])
                    self.tt(self.aytile(dc), self.aytile(dc), self.AYC[p], ALU.add,
                            r=[("yc", p), ("y", dc)], w=[("y", dc)])

    def phaseB_merged(self):
        v = self.view
        BXIN = [v(F32, 1024) for _ in range(6)]
        BXFB = [v(BF16, FC * 512) for _ in range(2)]
        BZ = [v(F32, NSUB * 1024) for _ in range(2)]
        BXH = [v(BF16, 1024) for _ in range(2)]
        BT = [v(BF16, 512) for _ in range(2)]
        BG = [v(BF16, 512) for _ in range(2)]
        BTMP = [v(F32, 512) for _ in range(2)]
        st = self.STB
        pf = st["pf"]
        xfb = lambda qb, fc: BXFB[qb][:, fc * 512:(fc + 1) * 512]

        for i in range(2 * NCH):
            s, c = i % 2, i // 2
            qb = i % 2
            lo = 4 * qb
            loz = 8 + 4 * qb
            tb = [(i * NSUB + sub) % 6 for sub in range(NSUB)]
            for sub in range(NSUB):
                g = c * NSUB + sub
                b = tb[sub]
                self.dma("sync", BXIN[b][:], self.xres_h[s, g * 128:(g + 1) * 128, :],
                         f"bx{b % 3}", r=[("xrh", s, g)], w=[("bxin", b)])
                self.ln_stats(st, BXIN[b][:], lo + sub, [("bxin", b)])
            self.ln_math(st, lo, NSUB)
            for sub in range(NSUB):
                b = tb[sub]
                rx = sub % 2
                self.ts(BXH[rx], BXIN[b][:],
                        st["negm"][:, lo + sub:lo + sub + 1],
                        st["rstd"][:, lo + sub:lo + sub + 1], ALU.add, ALU.mult,
                        r=[("bxin", b), (pf, "negm", lo), (pf, "rstd", lo)],
                        w=[("bxh", rx)])
                for fc in range(FC):
                    self.dmat(xfb(qb, fc)[:, sub * 128:(sub + 1) * 128],
                              BXH[rx][:, fc * 128:(fc + 1) * 128],
                              r=[("bxh", rx)], w=[("xfb", qb, fc, sub)])
            xfbr = [("xfb", qb, fc, sub) for fc in range(FC) for sub in range(NSUB)]

            pt = qb
            for fc in range(FC):
                self.mm(self.PS[pt][:], self.UTS[:, fc * 128:(fc + 1) * 128],
                        xfb(qb, fc), start=(fc == 0), stop=(fc == FC - 1),
                        r=[("uts",)] + xfbr, w=[("ps", pt)])
            self.vcopy(BT[qb], self.PS[pt][:],
                       r=[("ps", pt)], w=[("texp", qb), ("ps", pt)])
            pg = 2 + qb
            for sub in range(NSUB):
                for fc in range(FC):
                    self.mm(self.PS[pg][:, 0:8],
                            xfb(qb, fc)[:, sub * 128:(sub + 1) * 128],
                            self.GATET[:, fc * 8:(fc + 1) * 8],
                            start=(fc == 0), stop=(fc == FC - 1),
                            r=[("gatet",)] + xfbr, w=[("ps", pg)])
                self.rmax(self.NM[:, qb:qb + 1], self.PS[pg][:, 0:8], True,
                          r=[("ps", pg)], w=[("nm", qb)])
                self.act(self.GE[:, 8 * qb:8 * qb + 8], self.PS[pg][:, 0:8], AF.Exp,
                         bias=self.NM[:, qb:qb + 1],
                         r=[("ps", pg), ("nm", qb)], w=[("ge", qb), ("ps", pg)])
                self.rsumf(self.GS[:, 2 * qb:2 * qb + 1], self.GE[:, 8 * qb:8 * qb + 8],
                           r=[("ge", qb)], w=[("gs", qb)])
                self.ts(self.GS[:, 2 * qb + 1:2 * qb + 2], self.GS[:, 2 * qb:2 * qb + 1],
                        16.0, None, ALU.mult, None,
                        r=[("gs", qb)], w=[("gs16", qb)])
                self.recip(self.GS[:, 2 * qb:2 * qb + 1], self.GS[:, 2 * qb + 1:2 * qb + 2],
                           r=[("gs16", qb)], w=[("gs", qb)])
                self.ts(self.GSUB[:, 8 * qb:8 * qb + 8], self.GE[:, 8 * qb:8 * qb + 8],
                        self.GS[:, 2 * qb:2 * qb + 1], None, ALU.mult,
                        None, r=[("ge", qb), ("gs", qb)], w=[("gsub", qb)])
                pgt = self.PS[pg].ap().bitcast(BF16)[0:8, 0:128]
                self.tpose(pgt, self.GSUB[:, 8 * qb:8 * qb + 8],
                           r=[("gsub", qb), ("iden",)], w=[("ps", pg)])
                self.act(self.GFM[:, qb * LC + sub * 128:qb * LC + (sub + 1) * 128],
                         pgt, AF.Copy,
                         r=[("ps", pg)], w=[("gfm", qb, sub), ("ps", pg)])
            self.mm(self.PS[pt][:], self.REP8[:], self.GFM[:, qb * LC:(qb + 1) * LC],
                    start=True, stop=True,
                    r=[("rep8",)] + [("gfm", qb, ii) for ii in range(NSUB)],
                    w=[("ps", pt)])
            self.vcopy(BG[qb], self.PS[pt][:], r=[("ps", pt)], w=[("bg", qb), ("ps", pt)])
            self.tt(BT[qb], BT[qb], BG[qb], ALU.mult,
                    r=[("texp", qb), ("bg", qb)], w=[("texp", qb)])

            for half in range(2):
                for fc in range(FC):
                    p = fc % 2
                    wb = self.WSTR[:, 3072 + p * 512:3072 + p * 512 + 512]
                    self.dma("sync", wb,
                             self.w0t_p[fc * 128:(fc + 1) * 128,
                                        half * 512:(half + 1) * 512],
                             f"wB{p}",
                             r=[("w0t_p", fc // 2)], w=[("wbB", p)])
                    for sub in range(NSUB):
                        self.mm(self.PS[4 + sub][:],
                                xfb(qb, fc)[:, sub * 128:(sub + 1) * 128],
                                wb[:, 0:512],
                                start=(fc == 0), stop=False,
                                r=[("wbB", p)] + xfbr, w=[("ps", 4 + sub)])
                for sub in range(NSUB):
                    self.mm(self.PS[4 + sub][:],
                            BT[qb][:, sub * 128:(sub + 1) * 128],
                            self.VSB[:, half * 512:(half + 1) * 512],
                            start=False, stop=True,
                            r=[("texp", qb), ("vsb",)], w=[("ps", 4 + sub)])
                for sub in range(NSUB):
                    self.act(BZ[qb][:, sub * 1024 + half * 512:
                                    sub * 1024 + (half + 1) * 512],
                             self.PS[4 + sub][:], AF.Copy,
                             r=[("ps", 4 + sub)],
                             w=[("bz", qb, sub), ("ps", 4 + sub)])
            for sub in range(NSUB):
                self.ln_stats(st, BZ[qb][:, sub * 1024:(sub + 1) * 1024], loz + sub,
                              [("bz", qb, sub)])
            self.ln_math(st, loz, NSUB)
            for sub in range(NSUB):
                g = c * NSUB + sub
                b = tb[sub]
                for half in range(2):
                    hp = half
                    self.ts(BTMP[hp], BZ[qb][:, sub * 1024 + half * 512:
                                              sub * 1024 + (half + 1) * 512],
                            st["negm"][:, loz + sub:loz + sub + 1],
                            st["rstd"][:, loz + sub:loz + sub + 1],
                            ALU.add, ALU.mult,
                            r=[("bz", qb, sub), (pf, "negm", loz), (pf, "rstd", loz)],
                            w=[("btmp", hp)])
                    self.tt(BXIN[b][:, half * 512:(half + 1) * 512],
                            BXIN[b][:, half * 512:(half + 1) * 512],
                            BTMP[hp], ALU.add,
                            r=[("btmp", hp)], w=[("bxin", b)])
                self.dma("gpsimd", self.xres_h[s, g * 128:(g + 1) * 128, :],
                         BXIN[b][:], "xw",
                         r=[("bxin", b)], w=[("xrh", s, g)])

    def phaseC(self, s):
        pf = ("C", s)
        st = self.STA
        spf = st["pf"]
        v = self.view
        XHT = v(BF16, TS * D)
        XFC = v(BF16, FC * SEQ)
        ES = v(BF16, TS * 512)
        AO = [v(F32, D) for _ in range(2)]
        XT = [v(BF16, D) for _ in range(2)]
        TMP = [v(F32, 512) for _ in range(2)]
        CXI = [v(F32, 1024) for _ in range(3)]
        CXR = [v(F32, 1024) for _ in range(2)]
        xht = lambda g: XHT[:, g * D:(g + 1) * D]
        xfc = lambda fc: XFC[:, fc * SEQ:(fc + 1) * SEQ]
        es = lambda kc: ES[:, kc * 512:(kc + 1) * 512]
        cx = [0]

        for g in range(TS):
            b = cx[0] % 3
            cx[0] += 1
            self.dma("sync", CXI[b][:], self.xres_h[s, g * 128:(g + 1) * 128, :],
                     f"cx{b}", r=[("xrh", s, g)], w=[("cxi", b)])
            self.ln_stats(st, CXI[b][:], g, [("cxi", b)])
        self.ln_math(st, 0, TS, want_r=True)
        for g in range(TS):
            b = cx[0] % 3
            cx[0] += 1
            p = g % 2
            self.dma("sync", CXI[b][:], self.xres_h[s, g * 128:(g + 1) * 128, :],
                     f"cx{b}", r=[("xrh", s, g)], w=[("cxi", b)])
            self.ts(xht(g), CXI[b][:],
                    st["negm"][:, g:g + 1], st["rstd"][:, g:g + 1],
                    ALU.add, ALU.mult,
                    r=[("cxi", b), (spf, "negm", 0), (spf, "rstd", 0)],
                    w=[(pf, "xht", g)])
            self.ts(XT[p], CXI[b][:],
                    st["negm"][:, g:g + 1], st["aux2"][:, g:g + 1],
                    ALU.add, ALU.mult,
                    r=[("cxi", b), (spf, "negm", 0), (spf, "aux2", 0)],
                    w=[(pf, "xt", p)])
            for fc in range(FC):
                self.dmat(xfc(fc)[:, g * 128:(g + 1) * 128],
                          XT[p][:, fc * 128:(fc + 1) * 128],
                          r=[(pf, "xt", p)], w=[(pf, "xfc", fc, g)])

        for qb in range(NCH):
            q0 = qb * 512
            xfcq = [(pf, "xfc", fc, qb * 4 + i) for fc in range(FC) for i in range(4)]
            for kc in range(TS):
                pq = kc % 2
                for fc in range(FC):
                    self.mm(self.PS[pq][:], xfc(fc)[:, kc * 128:(kc + 1) * 128],
                            xfc(fc)[:, q0:q0 + 512],
                            start=(fc == 0), stop=(fc == FC - 1),
                            r=[(pf, "xfc", fc, kc)] + xfcq, w=[("ps", pq)])
                self.act(es(kc), self.PS[pq][:], AF.Exp,
                         r=[("ps", pq)], w=[(pf, "es", kc), ("ps", pq)])
            for qs in range(NSUB):
                gq = qb * NSUB + qs
                i0, i1 = 2 + 2 * (qs % 2), 3 + 2 * (qs % 2)
                for kc in range(TS):
                    l = es(kc)[:, qs * 128:(qs + 1) * 128]
                    self.mm(self.PS[i0][:], l, xht(kc)[:, 0:512],
                            start=(kc == 0), stop=(kc == TS - 1),
                            r=[(pf, "es", kc), (pf, "xht", kc)], w=[("ps", i0)])
                    self.mm(self.PS[i1][:], l, xht(kc)[:, 512:1024],
                            start=(kc == 0), stop=(kc == TS - 1),
                            r=[(pf, "es", kc), (pf, "xht", kc)], w=[("ps", i1)])
                    self.mm(self.PS[6][:, 0:1], l, self.ONES1[:],
                            start=(kc == 0), stop=(kc == TS - 1),
                            r=[(pf, "es", kc), ("ones1",)], w=[("ps", 6)])
                self.recip(self.RINV[:, qs:qs + 1], self.PS[6][:, 0:1],
                           r=[("ps", 6)], w=[(pf, "rinv", qs), ("ps", 6)])
                pa = qs % 2
                for half, pi in ((0, i0), (1, i1)):
                    self.stt(AO[pa][:, half * 512:(half + 1) * 512],
                             self.PS[pi][:], self.RINV[:, qs:qs + 1],
                             xht(gq)[:, half * 512:(half + 1) * 512],
                             ALU.mult, ALU.add,
                             r=[("ps", pi), (pf, "rinv", qs), (pf, "xht", gq)],
                             w=[(pf, "ao", pa), ("ps", pi)])
                self.ln_stats(st, AO[pa][:], qs, [(pf, "ao", pa)])
                self.ln_math(st, qs, 1)
                b2 = cx[0] % 2
                cx[0] += 1
                self.dma("sync", CXR[b2][:],
                         self.xres_h[s, gq * 128:(gq + 1) * 128, :],
                         f"cr{b2}", r=[("xrh", s, gq)], w=[("cxr", b2)])
                for half in range(2):
                    self.ts(TMP[half], AO[pa][:, half * 512:(half + 1) * 512],
                            st["negm"][:, qs:qs + 1], st["rstd"][:, qs:qs + 1],
                            ALU.add, ALU.mult,
                            r=[(pf, "ao", pa), (spf, "negm", qs), (spf, "rstd", qs)],
                            w=[(pf, "tmp", half)])
                    self.tt(CXR[b2][:, half * 512:(half + 1) * 512],
                            CXR[b2][:, half * 512:(half + 1) * 512],
                            TMP[half], ALU.add,
                            r=[(pf, "tmp", half)], w=[("cxr", b2)])
                self.dma("gpsimd", self.xres_h[s, gq * 128:(gq + 1) * 128, :],
                         CXR[b2][:], "xw",
                         r=[("cxr", b2)], w=[("xrh", s, gq)])

    def phaseD(self, s):
        pf = ("D", s)
        st = self.STA
        spf = st["pf"]
        v = self.view
        XFD = v(BF16, FC * 544)
        XR = v(BF16, FC * 512)
        XK = v(BF16, FC * 512)
        KACT = v(BF16, FC * 512)
        RT = v(BF16, NSUB * D)
        VR = v(F32, NSUB * D)
        WS2 = v(BF16, FC * 512)
        XH = [v(BF16, D) for _ in range(2)]
        DD = v(BF16, 512)
        KR = v(BF16, 512)
        XO = [v(F32, D) for _ in range(2)]
        TMP = [v(F32, 512) for _ in range(2)]
        DXI = [v(F32, 1024) for _ in range(6)]
        xfd3 = XFD.rearrange("p (fc r) -> p fc r", fc=FC)
        xfd = lambda fc: XFD[:, fc * 544:(fc + 1) * 544]
        xr = lambda fc: XR[:, fc * 512:(fc + 1) * 512]
        xk = lambda fc: XK[:, fc * 512:(fc + 1) * 512]
        ka = lambda dcb: KACT[:, dcb * 512:(dcb + 1) * 512]
        dx = [0]

        self.vmemset(xfd3[:, :, 15:16], 0.0, w=[(pf, "bnd")])

        for c in range(NCH):
            xfdr = [(pf, "xfd", fc, sub) for fc in range(FC) for sub in range(NSUB)]
            if c > 0:
                self.vcopy(xfd3[:, :, 15:16], xfd3[:, :, 527:528],
                           r=xfdr, w=[(pf, "bnd")])
            tb = [(c * NSUB + sub) % 6 for sub in range(NSUB)]
            for sub in range(NSUB):
                g = c * NSUB + sub
                b = tb[sub]
                self.dma("sync", DXI[b][:], self.xres_h[s, g * 128:(g + 1) * 128, :],
                         f"dx{b % 3}", r=[("xrh", s, g)], w=[("dxi", b)])
                self.ln_stats(st, DXI[b][:], sub, [("dxi", b)])
            self.ln_math(st, 0, NSUB)
            for sub in range(NSUB):
                g = c * NSUB + sub
                b = tb[sub]
                p = sub % 2
                self.ts(XH[p], DXI[b][:],
                        st["negm"][:, sub:sub + 1], st["rstd"][:, sub:sub + 1],
                        ALU.add, ALU.mult,
                        r=[("dxi", b), (spf, "negm", 0), (spf, "rstd", 0)],
                        w=[(pf, "xh", p)])
                for fc in range(FC):
                    self.dmat(xfd(fc)[:, 16 + sub * 128:16 + (sub + 1) * 128],
                              XH[p][:, fc * 128:(fc + 1) * 128],
                              r=[(pf, "xh", p), (pf, "bnd")],
                              w=[(pf, "xfd", fc, sub)])
            for fc in range(FC):
                cur = xfd(fc)[:, 16:528]
                prv = xfd(fc)[:, 15:527]
                self.tt(DD, cur, prv, ALU.subtract,
                        r=xfdr + [(pf, "bnd")], w=[(pf, "dd")])
                self.stt(xr(fc), DD, self.MUR[:, fc:fc + 1], prv, ALU.mult, ALU.add,
                         r=[(pf, "dd"), ("mur",)] + xfdr + [(pf, "bnd")],
                         w=[(pf, "xr", fc)])
                self.stt(xk(fc), DD, self.MUK[:, fc:fc + 1], prv, ALU.mult, ALU.add,
                         r=[(pf, "dd"), ("muk",)] + xfdr + [(pf, "bnd")],
                         w=[(pf, "xk", fc)])
            xrr = [(pf, "xr", fc) for fc in range(FC)]
            xkr = [(pf, "xk", fc) for fc in range(FC)]
            for dcb in range(FC):
                p = dcb % 2
                wb = self.WSTR[:, p * 1024:p * 1024 + 1024]
                self.dma("sync", wb, self.wkt_p[dcb], f"wb{p}",
                         r=[("wkt_p", j) for j in range(4)], w=[("wbd", p)])
                for fc in range(FC):
                    self.mm(self.PS[p][:], wb[:, fc * 128:(fc + 1) * 128], xk(fc),
                            start=(fc == 0), stop=(fc == FC - 1),
                            r=[("wbd", p)] + xkr, w=[("ps", p)])
                self.act(KR, self.PS[p][:], AF.Relu,
                         r=[("ps", p)], w=[(pf, "kr"), ("ps", p)])
                self.tt(ka(dcb), KR, KR, ALU.mult, r=[(pf, "kr")], w=[(pf, "ka", dcb)])
            for half in range(2):
                for fc in range(FC):
                    self.dma("sync", WS2[:, fc * 512:(fc + 1) * 512],
                             self.wrt_p[fc * 128:(fc + 1) * 128,
                                        half * 512:(half + 1) * 512],
                             "ws2",
                             r=[("wrt_p", fc // 2)], w=[(pf, "ws2d", fc)])
                for sub in range(NSUB):
                    for fc in range(FC):
                        self.mm(self.PS[2 + sub][:],
                                xr(fc)[:, sub * 128:(sub + 1) * 128],
                                WS2[:, fc * 512:(fc + 1) * 512],
                                start=(fc == 0), stop=(fc == FC - 1),
                                r=[(pf, "ws2d", fc)] + xrr, w=[("ps", 2 + sub)])
                    self.act(RT[:, sub * D + half * 512:sub * D + (half + 1) * 512],
                             self.PS[2 + sub][:], AF.Sigmoid,
                             r=[("ps", 2 + sub)],
                             w=[(pf, "rt", sub), ("ps", 2 + sub)])
            for half in range(2):
                for dcb in range(FC):
                    self.dma("sync", WS2[:, dcb * 512:(dcb + 1) * 512],
                             self.wvt_p[dcb * 128:(dcb + 1) * 128,
                                        half * 512:(half + 1) * 512],
                             "ws2",
                             r=[("wvt_p", dcb // 2)], w=[(pf, "ws2d", dcb)])
                for sub in range(NSUB):
                    for dcb in range(FC):
                        self.mm(self.PS[2 + sub][:],
                                ka(dcb)[:, sub * 128:(sub + 1) * 128],
                                WS2[:, dcb * 512:(dcb + 1) * 512],
                                start=(dcb == 0), stop=(dcb == FC - 1),
                                r=[(pf, "ws2d", dcb), (pf, "ka", dcb)],
                                w=[("ps", 2 + sub)])
                    self.tt(VR[:, sub * D + half * 512:sub * D + (half + 1) * 512],
                            self.PS[2 + sub][:],
                            RT[:, sub * D + half * 512:sub * D + (half + 1) * 512],
                            ALU.mult,
                            r=[("ps", 2 + sub), (pf, "rt", sub)],
                            w=[(pf, "vr", sub), ("ps", 2 + sub)])
            for sub in range(NSUB):
                self.ln_stats(st, VR[:, sub * D:(sub + 1) * D], sub,
                              [(pf, "vr", sub)])
            self.ln_math(st, 0, NSUB)
            for sub in range(NSUB):
                g = c * NSUB + sub
                b = tb[sub]
                p = sub % 2
                for half in range(2):
                    self.ts(TMP[half],
                            VR[:, sub * D + half * 512:sub * D + (half + 1) * 512],
                            st["negm"][:, sub:sub + 1], st["rstd"][:, sub:sub + 1],
                            ALU.add, ALU.mult,
                            r=[(pf, "vr", sub), (spf, "negm", 0), (spf, "rstd", 0)],
                            w=[(pf, "tmp", half)])
                    self.tt(XO[p][:, half * 512:(half + 1) * 512],
                            DXI[b][:, half * 512:(half + 1) * 512],
                            TMP[half], ALU.add,
                            r=[(pf, "tmp", half), ("dxi", b)], w=[(pf, "xo", p)])
                self.dma("sync", self.out[s, g * 128:(g + 1) * 128, :], XO[p][:],
                         "out", r=[(pf, "xo", p)], w=[("out", s, g)])


def _layouts(i):
    def col16(v):
        return np.ascontiguousarray(v.reshape(16, 128).T)
    out = {}
    out["conv_w_r"] = np.ascontiguousarray(
        i["L0_attn_conv_w"].reshape(16, 128, 4).transpose(1, 0, 2).reshape(128, 64))
    out["conv_b_r"] = col16(i["L0_attn_conv_b"])
    out["b_dt_r"] = col16(i["L0_attn_b_dt"])
    out["d_r"] = col16(i["L0_attn_D"])
    out["a_log_r"] = np.ascontiguousarray(
        i["L0_attn_A_log"].reshape(16, 128, 16).transpose(1, 0, 2).reshape(128, 256))
    out["w_in"] = np.ascontiguousarray(
        i["L0_attn_W_in"].reshape(8, 128, 32, 128).transpose(2, 1, 0, 3)
        .reshape(32, 128, 1024))
    out["w_x"] = np.ascontiguousarray(
        i["L0_attn_W_x"].reshape(16, 128, 96).transpose(1, 0, 2).reshape(128, 16 * 96))
    out["w_dt"] = np.ascontiguousarray(i["L0_attn_W_dt"])
    out["w_out"] = np.ascontiguousarray(i["L0_attn_W_out"])
    out["w0t"] = np.ascontiguousarray(i["L0_ffn_weight"].T)
    out["gatet"] = np.ascontiguousarray(
        i["L0_ffn_gate_w"].T.reshape(8, 128, 8).transpose(1, 0, 2).reshape(128, 64))
    out["ut"] = np.ascontiguousarray(
        i["L0_ffn_lora_U"].reshape(128, 1024).T.reshape(8, 128, 128)
        .transpose(1, 0, 2).reshape(128, 8 * 128))
    out["v_r"] = np.ascontiguousarray(
        i["L0_ffn_lora_V"].transpose(0, 2, 1).reshape(128, 1024))
    out["wrt"] = np.ascontiguousarray(i["L1_ffn_W_r"].T)
    out["wkt"] = np.ascontiguousarray(
        i["L1_ffn_W_k"].T.reshape(8, 128, 8, 128).transpose(2, 1, 0, 3)
        .reshape(8, 128, 1024))
    out["wvt"] = np.ascontiguousarray(i["L1_ffn_W_v"].T)
    out["mur_r"] = np.ascontiguousarray(i["L1_ffn_mu_r"].reshape(8, 128).T)
    out["muk_r"] = np.ascontiguousarray(i["L1_ffn_mu_k"].reshape(8, 128).T)
    bf = ml_dtypes.bfloat16
    selb = np.zeros((32, NST * 128), np.float32)
    selc = np.zeros((32, NST * 128), np.float32)
    for n in range(NST):
        selb[n, n * 128:(n + 1) * 128] = 1.0
        selc[16 + n, n * 128:(n + 1) * 128] = 1.0
    out["selb"] = selb.astype(bf)
    out["selc"] = selc.astype(bf)
    rep = np.zeros((8, 128), np.float32)
    for e in range(8):
        rep[e, e * 16:(e + 1) * 16] = 1.0
    out["rep8"] = rep.astype(bf)
    out["ones1"] = np.ones((128, 1), np.float32).astype(bf)
    out["iden"] = np.eye(128, dtype=np.float32).astype(bf)
    return out


_NC_CACHE = None


_BUILDER = None


def _get_nc():
    global _NC_CACHE, _BUILDER
    if _NC_CACHE is None:
        _BUILDER = Builder()
        _NC_CACHE = _BUILDER.build()
    return _NC_CACHE


def kernel(**inputs):
    x = np.ascontiguousarray(np.asarray(inputs["L0_attn_x"], dtype=np.float32))
    shared = _layouts({k: np.asarray(v, dtype=np.float32)
                       for k, v in inputs.items() if k != "L0_attn_x"})
    nc = _get_nc()
    in_maps = []
    for i in range(8):
        m = dict(shared)
        m["x"] = np.ascontiguousarray(x[2 * i:2 * i + 2])
        in_maps.append(m)
    res = run_bass_kernel_spmd(nc, in_maps, core_ids=list(range(8)))
    out = np.stack([r["out"] for r in res.results], axis=0)
    return out.reshape(16, SEQ, D).astype(np.float32)


# version 11
# speedup vs baseline: 1.3756x; 1.0565x over previous
import sys
import numpy as np
import ml_dtypes
from contextlib import ExitStack

import concourse.bass as bass
import concourse.mybir as mybir
from concourse.bass_utils import run_bass_kernel_spmd

F32 = mybir.dt.float32
BF16 = mybir.dt.bfloat16
FP16 = mybir.dt.float16
AF = mybir.ActivationFunctionType
ALU = mybir.AluOpType
AXX = mybir.AxisListType.X

D = 1024
DI = 2048
SEQ = 2048
NSEQ = 2
NST = 16
DTR = 64
LC = 512
NCH = SEQ // LC
NSUB = LC // 128
FC = D // 128
DC = DI // 128
TS = SEQ // 128
EPS = 1e-5
UOF = 8
UW = UOF + 512

ENGS = ("tensor", "vector", "scalar", "gpsimd", "sync")
_A_ONLY = False
RELAXED = False


class Planner:

    def __init__(self):
        self.ops = []
        self.bufs = {}
        self.groups = {}
        self.nbar = 0
        self.last_eng = {}

    def _b(self, k):
        b = self.bufs.get(k)
        if b is None:
            b = {"w": None, "r": []}
            self.bufs[k] = b
        return b

    def _g(self, name):
        g = self.groups.get(name)
        if g is None:
            g = {"gen": 0, "count": 0, "bounds": [], "pending": set(), "last": None}
            self.groups[name] = g
        return g

    def add(self, eng, fn, r=(), w=(), grp=None, extra=(), bar_inc=False,
            bar_wait=None, async_out=False):
        i = len(self.ops)
        deps = set(extra)
        sdeps = set()
        for k in r:
            b = self._b(k)
            if b["w"] is not None:
                deps.add(b["w"])
        for k in w:
            b = self._b(k)
            if b["w"] is not None:
                sdeps.add(b["w"])
            sdeps.update(b["r"])
        deps.discard(i)
        sdeps.discard(i)
        deps -= sdeps
        gw = {}
        eng_deps = set()
        for j in deps | sdeps:
            oj = self.ops[j]
            if oj["grp"] is not None:
                gname, gen = oj["grp"]
                gw[gname] = max(gw.get(gname, -1), gen)
                gg = self._g(gname)
                if gen == gg["gen"]:
                    gg["pending"].add(i)
            else:
                eng_deps.add(j)
        gtag = None
        if grp is not None:
            g = self._g(grp)
            pend = {j for j in g["pending"] if j != i}
            if pend:
                g["bounds"].append(g["count"])
                g["gen"] += 1
                eng_deps.update(j for j in pend if self.ops[j]["grp"] is None)
                for j in pend:
                    oj = self.ops[j]
                    if oj["grp"] is not None:
                        gn, gg = oj["grp"]
                        gw[gn] = max(gw.get(gn, -1), gg)
                g["pending"] = set()
            done = g["bounds"][-1] if g["bounds"] else 0
            if gw.get(grp, -1) == g["gen"] and g["count"] > done:
                g["bounds"].append(g["count"])
                g["gen"] += 1
            g["count"] += 1
            g["last"] = i
            gtag = (grp, g["gen"])
        self.ops.append({"eng": eng, "fn": fn, "grp": gtag, "deps": eng_deps,
                         "gw": gw, "async_out": async_out,
                         "bar_inc": bar_inc, "bar_wait": bar_wait})
        for k in r:
            self._b(k)["r"].append(i)
        for k in w:
            b = self._b(k)
            b["w"] = i
            b["r"] = []
        if gtag is None and not bar_inc and bar_wait is None:
            self.last_eng[eng] = i
        return i

    def barrier(self):
        lasts = [i for i in self.last_eng.values()]
        last_dmas = [g["last"] for g in self.groups.values() if g["last"] is not None]
        for g in self.groups.values():
            done = g["bounds"][-1] if g["bounds"] else 0
            if g["count"] > done:
                g["bounds"].append(g["count"])
                g["gen"] += 1
        self.add("sync", lambda eng: eng.nop(), extra=lasts + last_dmas,
                 bar_inc=True)
        self.nbar += 1
        for e in ENGS:
            if e != "sync":
                self.add(e, lambda eng: eng.nop(), bar_wait=self.nbar)

    def finalize(self):
        n = len(self.ops)
        for g in self.groups.values():
            g["bounds"].append(g["count"])
        for o in self.ops:
            if o["grp"] is None and o["eng"] == "tensor":
                o["deps"] = {j for j in o["deps"]
                             if self.ops[j]["eng"] != "tensor"
                             or self.ops[j]["grp"] is not None
                             or self.ops[j]["async_out"]}
        if RELAXED:
            for o in self.ops:
                if o["grp"] is None:
                    o["deps"] = {j for j in o["deps"]
                                 if self.ops[j]["eng"] != o["eng"]
                                 or self.ops[j]["grp"] is not None
                                 or self.ops[j]["async_out"]}
            need = [False] * n
            for o in self.ops:
                for j in o["deps"]:
                    need[j] = True
        else:
            need = [o["grp"] is None and not o["bar_inc"] and o["bar_wait"] is None
                    for o in self.ops]
        cum = {}
        val = [0] * n
        for i, o in enumerate(self.ops):
            if o["grp"] is not None:
                continue
            if need[i]:
                s = "e_" + o["eng"]
                cum[s] = cum.get(s, 0) + 1
                val[i] = cum[s]

        def gen_threshold(gname, gen):
            b = self.groups[gname]["bounds"]
            k = min(gen, len(b) - 1)
            return 16 * b[k]

        sched = {k: [] for k in ENGS}
        floor = {e: {} for e in ENGS}
        for i, o in enumerate(self.ops):
            wmap = {}
            for j in o["deps"]:
                if val[j] > 0:
                    s = "e_" + self.ops[j]["eng"]
                    wmap[s] = max(wmap.get(s, 0), val[j])
            for gname, gen in o["gw"].items():
                s = "g_" + gname
                wmap[s] = max(wmap.get(s, 0), gen_threshold(gname, gen))
            if o["bar_wait"] is not None:
                wmap["bar"] = o["bar_wait"]
            if o["bar_inc"]:
                inc = ("bar", 1)
                nbc = self.bar_cnt = getattr(self, "bar_cnt", 0) + 1
                if nbc > 1:
                    wmap["bar"] = max(wmap.get("bar", 0), nbc - 1)
            elif o["grp"] is not None:
                inc = ("g_" + o["grp"][0], 16)
            elif need[i]:
                inc = ("e_" + o["eng"], 1)
                pass
            else:
                inc = None
            fl = floor[o["eng"]]
            wl = []
            for s, v in sorted(wmap.items()):
                if fl.get(s, -1) < v:
                    fl[s] = v
                    wl.append((s, v))
            sched[o["eng"]].append((o["fn"], wl, inc))
        return sched, cum


class Builder:
    def __init__(self):
        self.nc = bass.Bass("TRN2", target_bir_lowering=False,
                            detect_race_conditions=not RELAXED)
        self.pl = Planner()
        self.ctx = ExitStack()
        self.arena_off = 0

    def sb(self, name, shape, dt):
        return self.ctx.enter_context(self.nc.sbuf_tensor(name, shape, dt))

    def view(self, dt, cols):
        esz = 2 if dt in (BF16, FP16) else 4
        off = (self.arena_off + 63) & ~63
        self.arena_off = off + cols * esz
        assert self.arena_off <= self.ARENA_BYTES, \
            f"arena overflow: {self.arena_off} > {self.ARENA_BYTES}"
        base = self.arena.ap() if dt == BF16 else self.arena.ap().bitcast(dt)
        return base[:, off // esz: off // esz + cols]

    def reset_arena(self):
        self.arena_off = 0

    def dma(self, q, out, in_, grp, r=(), w=()):
        self.pl.add(q, lambda e, o=out, i=in_: e.dma_start(out=o, in_=i),
                    r=r, w=w, grp=grp)

    def dmat(self, out, in_, r=(), w=()):
        self.pl.add("scalar",
                    lambda e, o=out, i=in_: e.dma_start_transpose(out=o, in_=i),
                    r=r, w=w, grp="tp")

    def mm(self, out, lhsT, rhs, start, stop, r=(), w=()):
        self.pl.add("tensor",
                    lambda e, o=out, l=lhsT, rr=rhs, s1=start, s2=stop:
                    e.matmul(o, l, rr, start=s1, stop=s2), r=r, w=w)

    def tpose(self, out, in_, r=(), w=()):
        self.pl.add("tensor",
                    lambda e, o=out, i=in_, d=self.IDEN[:]: e.transpose(o, i, d),
                    r=r, w=w)

    def act(self, out, in_, func, bias=0.0, scale=1.0, accum=None, r=(), w=()):
        self.pl.add("scalar",
                    lambda e, o=out, i=in_, f=func, b=bias, s=scale, a=accum:
                    e.activation(out=o, in_=i, func=f, bias=b, scale=s, accum_out=a),
                    r=r, w=w, async_out=accum is not None)

    def tt(self, out, a, b, op, r=(), w=()):
        self.pl.add("vector",
                    lambda e, o=out, x=a, y=b, p=op:
                    e.tensor_tensor(out=o, in0=x, in1=y, op=p), r=r, w=w)

    def ts(self, out, a, s1, s2, op0, op1, accum=None, r=(), w=()):
        if s2 is None and accum is None:
            self.pl.add("vector",
                        lambda e, o=out, x=a, u=s1, p0=op0:
                        e.tensor_scalar(out=o, in0=x, scalar1=u, scalar2=None, op0=p0),
                        r=r, w=w)
        else:
            self.pl.add("vector",
                        lambda e, o=out, x=a, u=s1, v=s2, p0=op0, p1=op1, ac=accum:
                        e.tensor_scalar(out=o, in0=x, scalar1=u, scalar2=v,
                                        op0=p0, op1=p1, accum_out=ac), r=r, w=w,
                        async_out=accum is not None)

    def stt(self, out, in0, scal, in1, op0, op1, r=(), w=()):
        self.pl.add("vector",
                    lambda e, o=out, x=in0, u=scal, y=in1, p0=op0, p1=op1:
                    e.scalar_tensor_tensor(out=o, in0=x, scalar=u, in1=y,
                                           op0=p0, op1=p1), r=r, w=w)

    def scan(self, out, a, b, init, r=(), w=()):
        self.pl.add("vector",
                    lambda e, o=out, x=a, y=b, h0=init:
                    e.tensor_tensor_scan(out=o, data0=x, data1=y, initial=h0,
                                         op0=ALU.mult, op1=ALU.add), r=r, w=w)

    def vcopy(self, out, in_, r=(), w=()):
        self.pl.add("vector", lambda e, o=out, i=in_: e.tensor_copy(out=o, in_=i),
                    r=r, w=w)

    def gcopy(self, out, in_, r=(), w=()):
        self.pl.add("gpsimd", lambda e, o=out, i=in_: e.tensor_copy(out=o, in_=i),
                    r=r, w=w)

    def vmemset(self, ap, val, r=(), w=()):
        self.pl.add("vector", lambda e, a=ap, v=val: e.memset(a, v), r=r, w=w)

    def rmax(self, out, in_, negate, r=(), w=()):
        self.pl.add("vector",
                    lambda e, o=out, i=in_, ng=negate:
                    e.reduce_max(out=o, in_=i, axis=AXX, negate=ng), r=r, w=w)

    def rsumf(self, out, in_, r=(), w=()):
        self.pl.add("vector",
                    lambda e, o=out, i=in_: e.reduce_sum(out=o, in_=i, axis=AXX),
                    r=r, w=w)

    def recip(self, out, in_, r=(), w=()):
        self.pl.add("vector", lambda e, o=out, i=in_: e.reciprocal(out=o, in_=i),
                    r=r, w=w)

    def build(self):
        nc = self.nc
        self.x_in = nc.dram_tensor("x", [NSEQ, SEQ, D], F32, kind="ExternalInput").ap()
        self.out = nc.dram_tensor("out", [NSEQ, SEQ, D], F32, kind="ExternalOutput").ap()
        f32in = lambda n, sh: nc.dram_tensor(n, sh, F32, kind="ExternalInput").ap()
        bf16in = lambda n, sh: nc.dram_tensor(n, sh, BF16, kind="ExternalInput").ap()
        self.w_in = f32in("w_in", [2 * DC, 128, 8 * 128])
        self.conv_w_r = f32in("conv_w_r", [128, 64])
        self.conv_b_r = f32in("conv_b_r", [128, 16])
        self.b_dt_r = f32in("b_dt_r", [128, 16])
        self.d_r = f32in("d_r", [128, 16])
        self.a_log_r = f32in("a_log_r", [128, 256])
        self.w_x = f32in("w_x", [128, DC * 96])
        self.w_dt = f32in("w_dt", [DTR, DI])
        self.w_out = f32in("w_out", [DI, D])
        self.w0t = f32in("w0t", [D, D])
        self.gatet = f32in("gatet", [128, FC * 8])
        self.ut = f32in("ut", [128, FC * 128])
        self.v_r = f32in("v_r", [128, D])
        self.wrt = f32in("wrt", [D, D])
        self.wkt = f32in("wkt", [FC, 128, FC * 128])
        self.wvt = f32in("wvt", [D, D])
        self.mur_r = f32in("mur_r", [128, 8])
        self.muk_r = f32in("muk_r", [128, 8])
        self.selb = bf16in("selb", [32, NST * 128])
        self.selc = bf16in("selc", [32, NST * 128])
        self.rep8 = bf16in("rep8", [8, 128])
        self.ones1 = bf16in("ones1", [128, 1])
        self.iden = bf16in("iden", [128, 128])

        self.win_p = nc.dram_tensor("win_p", [2 * DC, 128, 8 * 128], BF16).ap()
        self.wout_p = nc.dram_tensor("wout_p", [DI, D], BF16).ap()
        self.w0t_p = nc.dram_tensor("w0t_p", [D, D], BF16).ap()
        self.wrt_p = nc.dram_tensor("wrt_p", [D, D], BF16).ap()
        self.wkt_p = nc.dram_tensor("wkt_p", [FC, 128, FC * 128], BF16).ap()
        self.wvt_p = nc.dram_tensor("wvt_p", [D, D], BF16).ap()
        self.xres_h = nc.dram_tensor("xres_h", [NSEQ, SEQ, D], F32).ap()

        self.ARENA_BYTES = 172544
        self.arena = self.sb("arena", [128, self.ARENA_BYTES // 2], BF16)
        self.WSTR = self.sb("wstr", [128, 4 * 1024], BF16)
        self.WX = self.sb("wxr", [128, DC * 96], BF16)
        self.WDT = self.sb("wdtr", [DTR, DI], BF16)
        self.UTS = self.sb("uts", [128, FC * 128], BF16)
        self.VSB = self.sb("vsb", [128, D], BF16)
        self.GATET = self.sb("gatetr", [128, FC * 8], BF16)
        self.SELB = self.sb("selbr", [128, NST * 128], BF16)
        self.SELC = self.sb("selcr", [128, NST * 128], BF16)
        self.REP8 = self.sb("rep8r", [8, 128], BF16)
        self.ONES1 = self.sb("ones1r", [128, 1], BF16)
        self.IDEN = self.sb("idenr", [128, 128], BF16)
        self.CONVW = self.sb("convw", [128, 64], F32)
        self.CONVB = self.sb("convb", [128, 16], F32)
        self.BDT = self.sb("bdt", [128, 16], F32)
        self.DCOL = self.sb("dcol", [128, 16], F32)
        self.ANEG = self.sb("aneg", [128, 256], F32)
        self.MUR = self.sb("mur", [128, 8], F32)
        self.MUK = self.sb("muk", [128, 8], F32)
        self.HST = self.sb("hst", [128, 512], F32)
        self.DV = self.sb("dv", [128, 1024], BF16)
        self.DA = self.sb("da", [128, 1024], BF16)
        def stats_set(tag):
            t = lambda nm, c: self.sb(tag + nm, [128, c], F32)
            return {"pf": tag, "s1": t("s1", 16), "s2": t("s2", 16),
                    "negm": t("nm", 16), "rstd": t("rs", 16),
                    "sa": t("sa", 16), "sb": t("sb", 16),
                    "sv": t("sv", 16), "sw": t("sw", 16),
                    "aux": t("ax", 16), "aux2": t("a2", 16),
                    "dv": self.DV, "da": self.DA}
        self.STA = stats_set("LA")
        self.STB = stats_set("LB")
        self.RINV = self.sb("rinv", [128, 4], F32)
        self.NM = self.sb("nm", [128, 2], F32)
        self.GE = self.sb("ge", [128, 16], F32)
        self.GS = self.sb("gs", [128, 4], F32)
        self.GSUB = self.sb("gsub", [128, 16], BF16)
        self.GFM = self.sb("gfm", [8, 2 * LC], BF16)
        self.FCONST = self.sb("fconst", [128, 2], F32)
        self.nc.const_aps.register(F32, EPS, self.FCONST[:, 0:1])
        self.nc.const_aps.register(F32, 1.0, self.FCONST[:, 1:2])
        self.PS = [self.ctx.enter_context(self.nc.psum_tensor(f"ps{i}", [128, 512], F32))
                   for i in range(8)]

        self.prep()
        self.pl.barrier()
        self.reset_arena()
        self.phaseA_merged()
        if not _A_ONLY:
            self.pl.barrier()
            self.reset_arena()
            self.phaseB_merged()
            for s in range(NSEQ):
                self.pl.barrier()
                self.reset_arena()
                self.phaseC(s)
                self.pl.barrier()
                self.reset_arena()
                self.phaseD(s)

        self.pl.add("sync", lambda e: e.nop(),
                    r=[("out", s, g) for s in range(NSEQ) for g in range(TS)])

        sched, cum = self.pl.finalize()
        sems = {}
        names = ["e_tensor", "e_vector", "e_scalar", "e_gpsimd", "e_sync", "bar"]
        names += ["g_" + g for g in self.pl.groups]
        for name in names:
            sems[name] = self.ctx.enter_context(nc.semaphore(name))

        def body(eng_obj, items):
            for fn, wl, inc in items:
                for sname, v in wl:
                    eng_obj.wait_ge(sems[sname], v)
                inst = fn(eng_obj)
                if inc is not None:
                    inst.then_inc(sems[inc[0]], inc[1])

        with nc.Block() as blk:
            blk.tensor(lambda e: body(e, sched["tensor"]))
            blk.vector(lambda e: body(e, sched["vector"]))
            blk.scalar(lambda e: body(e, sched["scalar"]))
            blk.gpsimd(lambda e: body(e, sched["gpsimd"]))
            blk.sync(lambda e: body(e, sched["sync"]))
        self.ctx.close()
        return nc

    def prep(self):
        self.vmemset(self.FCONST[:, 0:1], EPS, w=[("fconst",)])
        self.vmemset(self.FCONST[:, 1:2], 1.0, w=[("fconst",)])
        for dst, src, key in [(self.win_p, self.w_in, "win_p"),
                              (self.wout_p, self.w_out, "wout_p"),
                              (self.w0t_p, self.w0t, "w0t_p"),
                              (self.wrt_p, self.wrt, "wrt_p"),
                              (self.wkt_p, self.wkt, "wkt_p"),
                              (self.wvt_p, self.wvt, "wvt_p")]:
            rows = src.shape[0]
            q = rows // 4
            for j in range(4):
                self.dma("gpsimd", dst[j * q:(j + 1) * q], src[j * q:(j + 1) * q],
                         "p_" + key, w=[(key, j)])
        self.dma("gpsimd", self.WX[:], self.w_x[:], "pres", w=[("wx",)])
        self.dma("gpsimd", self.WDT[:], self.w_dt[:], "pres", w=[("wdt",)])
        self.dma("gpsimd", self.UTS[:], self.ut[:], "pres", w=[("uts",)])
        self.dma("gpsimd", self.VSB[:], self.v_r[:], "pres", w=[("vsb",)])
        self.dma("gpsimd", self.GATET[:], self.gatet[:], "pres", w=[("gatet",)])
        for dst, src, key in [(self.CONVW, self.conv_w_r, "convw"),
                              (self.CONVB, self.conv_b_r, "convb"),
                              (self.BDT, self.b_dt_r, "bdt"),
                              (self.DCOL, self.d_r, "dcol"),
                              (self.MUR, self.mur_r, "mur"),
                              (self.MUK, self.muk_r, "muk")]:
            self.dma("sync", dst[:], src[:], "psml", w=[(key,)])
        self.dma("sync", self.SELB[64:96, :], self.selb[:], "pcon", w=[("selb",)])
        self.dma("sync", self.SELC[64:96, :], self.selc[:], "pcon", w=[("selc",)])
        self.dma("sync", self.REP8[:], self.rep8[:], "pcon", w=[("rep8",)])
        self.dma("sync", self.ONES1[:], self.ones1[:], "pcon", w=[("ones1",)])
        self.dma("sync", self.IDEN[:], self.iden[:], "pcon", w=[("iden",)])
        self.dma("sync", self.ANEG[:], self.a_log_r[:], "pan", w=[("aneg",)])
        self.act(self.ANEG[:], self.ANEG[:], AF.Exp, r=[("aneg",)], w=[("aneg",)])
        self.ts(self.ANEG[:], self.ANEG[:], -1.0, None, ALU.mult, None,
                r=[("aneg",)], w=[("aneg",)])

    def ln_stats(self, st, src_ap, col, key_r):
        pf = st["pf"]
        self.ts(st["dv"][:, 0:src_ap.shape[1]], src_ap, 1.0, 0.0, ALU.mult, ALU.add,
                accum=st["s1"][:, col:col + 1],
                r=key_r, w=[(pf, "s1", col), (pf, "dumpv")])
        self.act(st["da"][:, 0:src_ap.shape[1]], src_ap, AF.Square,
                 accum=st["s2"][:, col:col + 1],
                 r=key_r, w=[(pf, "s2", col), (pf, "dumpa")])

    def ln_math(self, st, lo, n, want_r=False):
        pf = st["pf"]
        sl = slice(lo, lo + n)
        rs1 = [(pf, "s1", c) for c in range(lo, lo + n)]
        rs2 = [(pf, "s2", c) for c in range(lo, lo + n)]
        self.ts(st["negm"][:, sl], st["s1"][:, sl], -1.0 / D, None, ALU.mult, None,
                r=rs1, w=[(pf, "negm", lo)])
        self.ts(st["sa"][:, sl], st["s1"][:, sl], 1.0 / D, None, ALU.mult, None,
                r=rs1, w=[(pf, "sa", lo)])
        self.tt(st["sb"][:, sl], st["sa"][:, sl], st["sa"][:, sl], ALU.mult,
                r=[(pf, "sa", lo)], w=[(pf, "sb", lo)])
        self.stt(st["sv"][:, sl], st["s2"][:, sl], 1.0 / D, st["sb"][:, sl],
                 ALU.mult, ALU.subtract, r=rs2 + [(pf, "sb", lo)], w=[(pf, "sv", lo)])
        self.act(st["sw"][:, sl], st["sv"][:, sl], AF.Ln, bias=EPS,
                 r=[(pf, "sv", lo)], w=[(pf, "sw", lo)])
        self.act(st["rstd"][:, sl], st["sw"][:, sl], AF.Exp, scale=-0.5,
                 r=[(pf, "sw", lo)], w=[(pf, "rstd", lo)])
        if want_r:
            self.tt(st["aux"][:, sl], st["rstd"][:, sl], st["rstd"][:, sl], ALU.mult,
                    r=[(pf, "rstd", lo)], w=[(pf, "aux", lo)])
            self.stt(st["aux"][:, sl], st["sv"][:, sl], float(D), st["aux"][:, sl],
                     ALU.mult, ALU.mult,
                     r=[(pf, "sv", lo), (pf, "aux", lo)], w=[(pf, "aux", lo)])
            self.ts(st["aux"][:, sl], st["aux"][:, sl], 1e-20, None, ALU.max, None,
                    r=[(pf, "aux", lo)], w=[(pf, "aux", lo)])
            self.act(st["aux"][:, sl], st["aux"][:, sl], AF.Ln,
                     r=[(pf, "aux", lo)], w=[(pf, "aux", lo)])
            self.act(st["aux"][:, sl], st["aux"][:, sl], AF.Exp, scale=-0.5,
                     r=[(pf, "aux", lo)], w=[(pf, "aux", lo)])
            self.tt(st["aux2"][:, sl], st["rstd"][:, sl], st["aux"][:, sl], ALU.mult,
                    r=[(pf, "rstd", lo), (pf, "aux", lo)], w=[(pf, "aux2", lo)])

    def phaseA_merged(self):
        v = self.view
        self.AU = [v(BF16, DC * UW) for _ in range(2)]
        self.ADEL = v(BF16, DC * 512)
        self.AW = v(BF16, DC * 512)
        self.AY = v(FP16, DC * 512)
        self.AXFA = [v(BF16, FC * 512) for _ in range(2)]
        self.AX0 = [v(F32, 1024) for _ in range(4)]
        self.AXH = [v(BF16, 1024) for _ in range(4)]
        self.AXO = [v(F32, 1024) for _ in range(4)]
        self.ADR = v(BF16, 512)
        self.AXDB = [v(BF16, 512) for _ in range(2)]
        self.AE1 = v(F32, 512)
        self.ABBC = [v(BF16, 512) for _ in range(2)]
        self.ACBC = [v(BF16, 512) for _ in range(2)]
        self.AA2 = [v(FP16, 512) for _ in range(4)]
        self.AB2 = [v(BF16, 512) for _ in range(2)]
        self.AH2 = [v(BF16, 512) for _ in range(2)]
        self.AYC = [v(BF16, 512) for _ in range(2)]
        self.AACC = v(BF16, 512)
        self.AGR = [v(BF16, 512) for _ in range(8)]
        self.ATMP = [v(F32, 512) for _ in range(2)]
        self.x0cnt = 0
        self.xhcnt = 0
        self.a2c = 0
        self.rx0 = {}
        NSTEP = 2 * NCH

        self.vmemset(self.HST[:], 0.0,
                     w=[("hst", s, n, dc) for s in range(2)
                        for n in range(NST) for dc in range(DC)])
        for q in range(2):
            u3 = self.AU[q].rearrange("p (dc r) -> p dc r", dc=DC)
            self.vmemset(u3[:, :, UOF - 3:UOF], 0.0,
                         w=[("tl", q, dc) for dc in range(DC)])

        for j0 in (0, 1):
            self.frontA_LN(j0)
            self.frontA_dmats(j0, (0, 1))
            self.frontA_dmats(j0, (2, 3))
            self.frontA_win(j0)
            self.frontA_conv(j0)
        self.frontB_xdbl(0)
        self.frontB_delta(0, 0, 16)
        for j in range(NSTEP):
            nxt = j + 1 < NSTEP
            self.sweepA(j, 0, nxt)
            self.gateA(j, 0)
            if nxt and j + 1 >= 2:
                self.frontA_conv(j + 1)
            self.sweepA(j, 1, nxt)
            self.gateA(j, 8)
            if nxt:
                self.frontB_delta(j + 1, 8, 16)
            self.woutA_close(j)
            self.outLN_store(j)

    def autile(self, q, dc):
        return self.AU[q][:, dc * UW: dc * UW + UW]

    def adtile(self, dc):
        return self.ADEL[:, dc * 512:(dc + 1) * 512]

    def awtile(self, dc):
        return self.AW[:, dc * 512:(dc + 1) * 512]

    def aytile(self, dc):
        return self.AY[:, dc * 512:(dc + 1) * 512]

    def axfa(self, q, fc):
        return self.AXFA[q][:, fc * 512:(fc + 1) * 512]

    def frontA_LN(self, j):
        s, c = j % 2, j // 2
        q = j % 2
        lo = 4 * q
        st = self.STA
        pf = st["pf"]
        for sub in range(NSUB):
            g = c * NSUB + sub
            b = self.x0cnt % 4
            self.x0cnt += 1
            self.dma("sync", self.AX0[b][:], self.x_in[s, g * 128:(g + 1) * 128, :],
                     f"x0{b}", w=[("x0", b)])
            self.ln_stats(st, self.AX0[b][:], lo + sub, [("x0", b)])
        self.ln_math(st, lo, NSUB)
        for sub in range(NSUB):
            g = c * NSUB + sub
            b = self.x0cnt % 4
            self.x0cnt += 1
            rx = sub
            self.dma("sync", self.AX0[b][:], self.x_in[s, g * 128:(g + 1) * 128, :],
                     f"x0{b}", w=[("x0", b)])
            self.ts(self.AXH[rx], self.AX0[b][:],
                    st["negm"][:, lo + sub:lo + sub + 1],
                    st["rstd"][:, lo + sub:lo + sub + 1], ALU.add, ALU.mult,
                    r=[("x0", b), (pf, "negm", lo), (pf, "rstd", lo)],
                    w=[("xh", rx)])

    def frontA_dmats(self, j, subs):
        q = j % 2
        for sub in subs:
            rx = sub
            for fc in range(FC):
                self.dmat(self.axfa(q, fc)[:, sub * 128:(sub + 1) * 128],
                          self.AXH[rx][:, fc * 128:(fc + 1) * 128],
                          r=[("xh", rx)], w=[("xfa", q, fc, sub)])

    def frontA_win(self, j):
        q = j % 2
        xfar = [("xfa", q, fc, sub) for fc in range(FC) for sub in range(NSUB)]
        for oc in range(DC):
            p = oc % 2
            wb = self.WSTR[:, p * 1024:p * 1024 + 1024]
            self.dma("sync", wb, self.win_p[oc], f"wb{p}",
                     r=[("win_p", jj) for jj in range(4)], w=[("wbd", p)])
            pb = self.PS[p]
            for fc in range(FC):
                self.mm(pb[:], wb[:, fc * 128:(fc + 1) * 128], self.axfa(q, fc),
                        start=(fc == 0), stop=(fc == FC - 1),
                        r=[("wbd", p)] + xfar, w=[("ps", p)])
            self.act(self.autile(q, oc)[:, UOF:UOF + 512], pb[:], AF.Copy,
                     r=[("ps", p)], w=[("u", q, oc), ("ps", p)])

    def frontA_conv(self, j):
        q = j % 2
        for dc in range(DC):
            ut = self.autile(q, dc)
            self.ts(self.AACC, ut[:, UOF - 3:UOF + 509],
                    self.CONVW[:, dc * 4:dc * 4 + 1], None, ALU.mult, None,
                    r=[("u", q, dc), ("tl", q, dc), ("convw",)], w=[("acc",)])
            for k in range(1, 4):
                self.stt(self.AACC, ut[:, UOF - 3 + k:UOF + 509 + k],
                         self.CONVW[:, dc * 4 + k:dc * 4 + k + 1], self.AACC,
                         ALU.mult, ALU.add,
                         r=[("u", q, dc), ("tl", q, dc), ("acc",), ("convw",)],
                         w=[("acc",)])
            self.gcopy(ut[:, UOF - 3:UOF], ut[:, UOF + 509:UOF + 512],
                       r=[("u", q, dc)], w=[("tl", q, dc)])
            self.act(ut[:, UOF:UOF + 512], self.AACC, AF.Silu,
                     bias=self.CONVB[:, dc:dc + 1],
                     r=[("acc",), ("convb",)], w=[("u", q, dc)])

    def frontB_xdbl(self, j):
        q = j % 2
        px = self.PS[2]
        for kc in range(DC):
            self.mm(px[0:96, :], self.WX[:, kc * 96:(kc + 1) * 96],
                    self.autile(q, kc)[:, UOF:UOF + 512],
                    start=(kc == 0), stop=(kc == DC - 1),
                    r=[("u", q, kc), ("wx",)], w=[("ps", 2)])
        self.act(self.ADR[0:64, :], px[0:64, :], AF.Copy,
                 r=[("ps", 2)], w=[("dr",)])
        self.act(self.AXDB[q][64:96, :], px[64:96, :], AF.Copy,
                 r=[("ps", 2)], w=[("xdb", q), ("ps", 2)])

    def frontB_delta(self, j, d0, d1):
        q = j % 2
        for dc in range(d0, d1):
            pi = 3 if dc % 2 == 0 else 2
            pd = self.PS[pi]
            self.mm(pd[:], self.WDT[:, dc * 128:(dc + 1) * 128], self.ADR[0:64, :],
                    start=True, stop=True,
                    r=[("dr",), ("wdt",)], w=[("ps", pi)])
            self.act(self.AE1, pd[:], AF.Exp, bias=self.BDT[:, dc:dc + 1],
                     r=[("ps", pi), ("bdt",)], w=[("e1",), ("ps", pi)])
            self.act(self.adtile(dc), self.AE1, AF.Ln, bias=1.0,
                     r=[("e1",)], w=[("del", dc)])
            self.tt(self.awtile(dc), self.adtile(dc),
                    self.autile(q, dc)[:, UOF:UOF + 512], ALU.mult,
                    r=[("del", dc), ("u", q, dc)], w=[("w", dc)])

    def resA(self, j, base):
        q = j % 2
        xfar = [("xfa", q, fc, sub) for fc in range(FC) for sub in range(NSUB)]
        for k in range(8):
            oc = DC + base + k
            p = oc % 2
            wb = self.WSTR[:, p * 1024:p * 1024 + 1024]
            self.dma("sync", wb, self.win_p[oc], f"wb{p}",
                     r=[("win_p", jj) for jj in range(4)], w=[("wbd", p)])
            pb = self.PS[p]
            for fc in range(FC):
                self.mm(pb[:], wb[:, fc * 128:(fc + 1) * 128], self.axfa(q, fc),
                        start=(fc == 0), stop=(fc == FC - 1),
                        r=[("wbd", p)] + xfar, w=[("ps", p)])
            self.act(self.AGR[k], pb[:], AF.Silu,
                     r=[("ps", p)], w=[("gr", k), ("ps", p)])

    def gateA(self, j, base):
        q = j % 2
        for k in range(8):
            dc = base + k
            p = dc % 2
            self.stt(self.ATMP[p], self.autile(q, dc)[:, UOF:UOF + 512],
                     self.DCOL[:, dc:dc + 1], self.aytile(dc), ALU.mult, ALU.add,
                     r=[("u", q, dc), ("dcol",), ("y", dc)], w=[("tmp", p)])
            self.tt(self.autile(q, dc)[:, UOF:UOF + 512], self.ATMP[p],
                    self.AGR[k], ALU.mult,
                    r=[("tmp", p), ("gr", k), ("u", q, dc)], w=[("u", q, dc)])

    def prefX0(self, j):
        s, c = j % 2, j // 2
        bl = []
        for sub in range(NSUB):
            g = c * NSUB + sub
            b = self.x0cnt % 4
            self.x0cnt += 1
            self.dma("sync", self.AX0[b][:], self.x_in[s, g * 128:(g + 1) * 128, :],
                     f"x0{b}", w=[("x0", b)])
            bl.append(b)
        self.rx0[j] = bl

    WOBANK = (0, 1, 6, 7)

    def woutA_chain(self, j, half, d0, d1, stop_end):
        q = j % 2
        for dc in range(d0, d1):
            wsl = dc % 2
            wb = self.WSTR[:, 2048 + wsl * 512:2048 + wsl * 512 + 512]
            self.dma("sync", wb,
                     self.wout_p[dc * 128:(dc + 1) * 128,
                                 half * 512:(half + 1) * 512],
                     f"wo{wsl}",
                     r=[("wout_p", dc // 4)], w=[("wod", wsl)])
            for sub in range(NSUB):
                bk = self.WOBANK[sub]
                self.mm(self.PS[bk][:],
                        self.autile(q, dc)[:, UOF + sub * 128:UOF + (sub + 1) * 128],
                        wb[:, 0:512],
                        start=(dc == 0), stop=(stop_end and dc == d1 - 1),
                        r=[("wod", wsl), ("u", q, dc)], w=[("ps", bk)])

    def woutA_evac(self, j, half):
        for sub in range(NSUB):
            bk = self.WOBANK[sub]
            self.act(self.AXO[sub][:, half * 512:(half + 1) * 512],
                     self.PS[bk][:], AF.Copy,
                     r=[("ps", bk)], w=[("xo", sub), ("ps", bk)])

    def woutA_close(self, j):
        self.woutA_chain(j, 0, 0, 16, True)
        self.woutA_evac(j, 0)
        self.woutA_chain(j, 1, 0, 16, True)
        self.woutA_evac(j, 1)

    def outLN_store(self, j):
        s, c = j % 2, j // 2
        q = j % 2
        st = self.STA
        pf = st["pf"]
        lo = 8 + 4 * q
        if _A_ONLY:
            for sub in range(NSUB):
                g = c * NSUB + sub
                self.dma("gpsimd", self.out[s, g * 128:(g + 1) * 128, :],
                         self.AXO[sub][:], "out", r=[("xo", sub)],
                         w=[("odbg", s, g)])
        for sub in range(NSUB):
            self.ln_stats(st, self.AXO[sub][:], lo + sub, [("xo", sub)])
        self.ln_math(st, lo, NSUB)
        for sub in range(NSUB):
            g = c * NSUB + sub
            b = self.rx0[j][sub]
            for half in range(2):
                hp = half
                self.ts(self.ATMP[hp],
                        self.AXO[sub][:, half * 512:(half + 1) * 512],
                        st["negm"][:, lo + sub:lo + sub + 1],
                        st["rstd"][:, lo + sub:lo + sub + 1], ALU.add, ALU.mult,
                        r=[("xo", sub), (pf, "negm", lo), (pf, "rstd", lo)],
                        w=[("tmp", hp)])
                self.tt(self.AXO[sub][:, half * 512:(half + 1) * 512],
                        self.AX0[b][:, half * 512:(half + 1) * 512],
                        self.ATMP[hp], ALU.add,
                        r=[("x0", b), ("tmp", hp)], w=[("xo", sub)])
            self.dma("gpsimd", self.xres_h[s, g * 128:(g + 1) * 128, :],
                     self.AXO[sub][:], "xw",
                     r=[("xo", sub)], w=[("xrh", s, g)])

    def sweepA(self, j, sh, nxt):
        s = j % 2
        q = j % 2
        d0 = 8 * sh
        for n in range(NST):
            if sh == 0:
                if nxt and j + 1 >= 2:
                    if n == 4:
                        self.frontA_LN(j + 1)
                        self.frontA_dmats(j + 1, (0, 1))
                    elif n == 6:
                        self.frontA_dmats(j + 1, (2, 3))
                    elif n == 8:
                        self.frontA_win(j + 1)
                if n == 11:
                    self.resA(j, 0)
            else:
                if n == 2:
                    self.resA(j, 8)
                if nxt:
                    if n == 4:
                        self.frontB_xdbl(j + 1)
                    elif n == 6:
                        self.frontB_delta(j + 1, 0, 8)
                if n == 8:
                    self.prefX0(j)
            nb = n % 2
            self.mm(self.PS[4][:], self.SELB[64:96, n * 128:(n + 1) * 128],
                    self.AXDB[q][64:96, :], start=True, stop=True,
                    r=[("xdb", q), ("selb",)], w=[("ps", 4)])
            self.act(self.ABBC[nb], self.PS[4][:], AF.Copy,
                     r=[("ps", 4)], w=[("bbc", nb), ("ps", 4)])
            self.mm(self.PS[5][:], self.SELC[64:96, n * 128:(n + 1) * 128],
                    self.AXDB[q][64:96, :], start=True, stop=True,
                    r=[("xdb", q), ("selc",)], w=[("ps", 5)])
            self.vcopy(self.ACBC[nb], self.PS[5][:],
                       r=[("ps", 5)], w=[("cbc", nb), ("ps", 5)])
            for dc in range(d0, d0 + 8):
                p = dc % 2
                a2 = self.a2c % 4
                self.a2c += 1
                col = s * 256 + dc * 16 + n
                acol = dc * 16 + n
                self.act(self.AA2[a2], self.adtile(dc), AF.Exp,
                         scale=self.ANEG[:, acol:acol + 1],
                         r=[("del", dc), ("aneg",)], w=[("a2", a2)])
                self.tt(self.AB2[p], self.awtile(dc), self.ABBC[nb], ALU.mult,
                        r=[("w", dc), ("bbc", nb)], w=[("b2", p)])
                self.scan(self.AH2[p], self.AA2[a2], self.AB2[p],
                          self.HST[:, col:col + 1],
                          r=[("a2", a2), ("b2", p), ("hst", s, n, dc)],
                          w=[("h2", p)])
                self.gcopy(self.HST[:, col:col + 1], self.AH2[p][:, 511:512],
                           r=[("h2", p)], w=[("hst", s, n, dc)])
                if n == 0:
                    self.tt(self.aytile(dc), self.AH2[p], self.ACBC[nb], ALU.mult,
                            r=[("h2", p), ("cbc", nb)], w=[("y", dc)])
                else:
                    self.tt(self.AYC[p], self.AH2[p], self.ACBC[nb], ALU.mult,
                            r=[("h2", p), ("cbc", nb)], w=[("yc", p)])
                    self.tt(self.aytile(dc), self.aytile(dc), self.AYC[p], ALU.add,
                            r=[("yc", p), ("y", dc)], w=[("y", dc)])

    def phaseB_merged(self):
        v = self.view
        BXIN = [v(F32, 1024) for _ in range(6)]
        BXFB = [v(BF16, FC * 512) for _ in range(2)]
        BZ = [v(F32, NSUB * 1024) for _ in range(2)]
        BXH = [v(BF16, 1024) for _ in range(2)]
        BT = [v(BF16, 512) for _ in range(2)]
        BG = [v(BF16, 512) for _ in range(2)]
        BTMP = [v(F32, 512) for _ in range(2)]
        st = self.STB
        pf = st["pf"]
        xfb = lambda qb, fc: BXFB[qb][:, fc * 512:(fc + 1) * 512]

        for i in range(2 * NCH):
            s, c = i % 2, i // 2
            qb = i % 2
            lo = 4 * qb
            loz = 8 + 4 * qb
            tb = [(i * NSUB + sub) % 6 for sub in range(NSUB)]
            for sub in range(NSUB):
                g = c * NSUB + sub
                b = tb[sub]
                self.dma("sync", BXIN[b][:], self.xres_h[s, g * 128:(g + 1) * 128, :],
                         f"bx{b % 3}", r=[("xrh", s, g)], w=[("bxin", b)])
                self.ln_stats(st, BXIN[b][:], lo + sub, [("bxin", b)])
            self.ln_math(st, lo, NSUB)
            for sub in range(NSUB):
                b = tb[sub]
                rx = sub % 2
                self.ts(BXH[rx], BXIN[b][:],
                        st["negm"][:, lo + sub:lo + sub + 1],
                        st["rstd"][:, lo + sub:lo + sub + 1], ALU.add, ALU.mult,
                        r=[("bxin", b), (pf, "negm", lo), (pf, "rstd", lo)],
                        w=[("bxh", rx)])
                for fc in range(FC):
                    self.dmat(xfb(qb, fc)[:, sub * 128:(sub + 1) * 128],
                              BXH[rx][:, fc * 128:(fc + 1) * 128],
                              r=[("bxh", rx)], w=[("xfb", qb, fc, sub)])
            xfbr = [("xfb", qb, fc, sub) for fc in range(FC) for sub in range(NSUB)]

            pt = qb
            for fc in range(FC):
                self.mm(self.PS[pt][:], self.UTS[:, fc * 128:(fc + 1) * 128],
                        xfb(qb, fc), start=(fc == 0), stop=(fc == FC - 1),
                        r=[("uts",)] + xfbr, w=[("ps", pt)])
            self.vcopy(BT[qb], self.PS[pt][:],
                       r=[("ps", pt)], w=[("texp", qb), ("ps", pt)])
            pg = 2 + qb
            for sub in range(NSUB):
                for fc in range(FC):
                    self.mm(self.PS[pg][:, 0:8],
                            xfb(qb, fc)[:, sub * 128:(sub + 1) * 128],
                            self.GATET[:, fc * 8:(fc + 1) * 8],
                            start=(fc == 0), stop=(fc == FC - 1),
                            r=[("gatet",)] + xfbr, w=[("ps", pg)])
                self.rmax(self.NM[:, qb:qb + 1], self.PS[pg][:, 0:8], True,
                          r=[("ps", pg)], w=[("nm", qb)])
                self.act(self.GE[:, 8 * qb:8 * qb + 8], self.PS[pg][:, 0:8], AF.Exp,
                         bias=self.NM[:, qb:qb + 1],
                         r=[("ps", pg), ("nm", qb)], w=[("ge", qb), ("ps", pg)])
                self.rsumf(self.GS[:, 2 * qb:2 * qb + 1], self.GE[:, 8 * qb:8 * qb + 8],
                           r=[("ge", qb)], w=[("gs", qb)])
                self.ts(self.GS[:, 2 * qb + 1:2 * qb + 2], self.GS[:, 2 * qb:2 * qb + 1],
                        16.0, None, ALU.mult, None,
                        r=[("gs", qb)], w=[("gs16", qb)])
                self.recip(self.GS[:, 2 * qb:2 * qb + 1], self.GS[:, 2 * qb + 1:2 * qb + 2],
                           r=[("gs16", qb)], w=[("gs", qb)])
                self.ts(self.GSUB[:, 8 * qb:8 * qb + 8], self.GE[:, 8 * qb:8 * qb + 8],
                        self.GS[:, 2 * qb:2 * qb + 1], None, ALU.mult,
                        None, r=[("ge", qb), ("gs", qb)], w=[("gsub", qb)])
                pgt = self.PS[pg].ap().bitcast(BF16)[0:8, 0:128]
                self.tpose(pgt, self.GSUB[:, 8 * qb:8 * qb + 8],
                           r=[("gsub", qb), ("iden",)], w=[("ps", pg)])
                self.act(self.GFM[:, qb * LC + sub * 128:qb * LC + (sub + 1) * 128],
                         pgt, AF.Copy,
                         r=[("ps", pg)], w=[("gfm", qb, sub), ("ps", pg)])
            self.mm(self.PS[pt][:], self.REP8[:], self.GFM[:, qb * LC:(qb + 1) * LC],
                    start=True, stop=True,
                    r=[("rep8",)] + [("gfm", qb, ii) for ii in range(NSUB)],
                    w=[("ps", pt)])
            self.vcopy(BG[qb], self.PS[pt][:], r=[("ps", pt)], w=[("bg", qb), ("ps", pt)])
            self.tt(BT[qb], BT[qb], BG[qb], ALU.mult,
                    r=[("texp", qb), ("bg", qb)], w=[("texp", qb)])

            for half in range(2):
                for fc in range(FC):
                    p = fc % 2
                    wb = self.WSTR[:, 3072 + p * 512:3072 + p * 512 + 512]
                    self.dma("sync", wb,
                             self.w0t_p[fc * 128:(fc + 1) * 128,
                                        half * 512:(half + 1) * 512],
                             f"wB{p}",
                             r=[("w0t_p", fc // 2)], w=[("wbB", p)])
                    for sub in range(NSUB):
                        self.mm(self.PS[4 + sub][:],
                                xfb(qb, fc)[:, sub * 128:(sub + 1) * 128],
                                wb[:, 0:512],
                                start=(fc == 0), stop=False,
                                r=[("wbB", p)] + xfbr, w=[("ps", 4 + sub)])
                for sub in range(NSUB):
                    self.mm(self.PS[4 + sub][:],
                            BT[qb][:, sub * 128:(sub + 1) * 128],
                            self.VSB[:, half * 512:(half + 1) * 512],
                            start=False, stop=True,
                            r=[("texp", qb), ("vsb",)], w=[("ps", 4 + sub)])
                for sub in range(NSUB):
                    self.act(BZ[qb][:, sub * 1024 + half * 512:
                                    sub * 1024 + (half + 1) * 512],
                             self.PS[4 + sub][:], AF.Copy,
                             r=[("ps", 4 + sub)],
                             w=[("bz", qb, sub), ("ps", 4 + sub)])
            for sub in range(NSUB):
                self.ln_stats(st, BZ[qb][:, sub * 1024:(sub + 1) * 1024], loz + sub,
                              [("bz", qb, sub)])
            self.ln_math(st, loz, NSUB)
            for sub in range(NSUB):
                g = c * NSUB + sub
                b = tb[sub]
                for half in range(2):
                    hp = half
                    self.ts(BTMP[hp], BZ[qb][:, sub * 1024 + half * 512:
                                              sub * 1024 + (half + 1) * 512],
                            st["negm"][:, loz + sub:loz + sub + 1],
                            st["rstd"][:, loz + sub:loz + sub + 1],
                            ALU.add, ALU.mult,
                            r=[("bz", qb, sub), (pf, "negm", loz), (pf, "rstd", loz)],
                            w=[("btmp", hp)])
                    self.tt(BXIN[b][:, half * 512:(half + 1) * 512],
                            BXIN[b][:, half * 512:(half + 1) * 512],
                            BTMP[hp], ALU.add,
                            r=[("btmp", hp)], w=[("bxin", b)])
                self.dma("gpsimd", self.xres_h[s, g * 128:(g + 1) * 128, :],
                         BXIN[b][:], "xw",
                         r=[("bxin", b)], w=[("xrh", s, g)])

    def phaseC(self, s):
        pf = ("C", s)
        st = self.STA
        spf = st["pf"]
        v = self.view
        XHT = v(BF16, TS * D)
        XFC = v(BF16, FC * SEQ)
        ES = v(BF16, TS * 512)
        AO = [v(F32, D) for _ in range(2)]
        XT = [v(BF16, D) for _ in range(2)]
        TMP = [v(F32, 512) for _ in range(2)]
        CXI = [v(F32, 1024) for _ in range(3)]
        CXR = [v(F32, 1024) for _ in range(2)]
        xht = lambda g: XHT[:, g * D:(g + 1) * D]
        xfc = lambda fc: XFC[:, fc * SEQ:(fc + 1) * SEQ]
        es = lambda kc: ES[:, kc * 512:(kc + 1) * 512]
        cx = [0]

        for g in range(TS):
            b = cx[0] % 3
            cx[0] += 1
            self.dma("sync", CXI[b][:], self.xres_h[s, g * 128:(g + 1) * 128, :],
                     f"cx{b}", r=[("xrh", s, g)], w=[("cxi", b)])
            self.ln_stats(st, CXI[b][:], g, [("cxi", b)])
        self.ln_math(st, 0, TS, want_r=True)
        for g in range(TS):
            b = cx[0] % 3
            cx[0] += 1
            p = g % 2
            self.dma("sync", CXI[b][:], self.xres_h[s, g * 128:(g + 1) * 128, :],
                     f"cx{b}", r=[("xrh", s, g)], w=[("cxi", b)])
            self.ts(xht(g), CXI[b][:],
                    st["negm"][:, g:g + 1], st["rstd"][:, g:g + 1],
                    ALU.add, ALU.mult,
                    r=[("cxi", b), (spf, "negm", 0), (spf, "rstd", 0)],
                    w=[(pf, "xht", g)])
            self.ts(XT[p], CXI[b][:],
                    st["negm"][:, g:g + 1], st["aux2"][:, g:g + 1],
                    ALU.add, ALU.mult,
                    r=[("cxi", b), (spf, "negm", 0), (spf, "aux2", 0)],
                    w=[(pf, "xt", p)])
            for fc in range(FC):
                self.dmat(xfc(fc)[:, g * 128:(g + 1) * 128],
                          XT[p][:, fc * 128:(fc + 1) * 128],
                          r=[(pf, "xt", p)], w=[(pf, "xfc", fc, g)])

        for qb in range(NCH):
            q0 = qb * 512
            xfcq = [(pf, "xfc", fc, qb * 4 + i) for fc in range(FC) for i in range(4)]
            for kc in range(TS):
                pq = kc % 2
                for fc in range(FC):
                    self.mm(self.PS[pq][:], xfc(fc)[:, kc * 128:(kc + 1) * 128],
                            xfc(fc)[:, q0:q0 + 512],
                            start=(fc == 0), stop=(fc == FC - 1),
                            r=[(pf, "xfc", fc, kc)] + xfcq, w=[("ps", pq)])
                self.act(es(kc), self.PS[pq][:], AF.Exp,
                         r=[("ps", pq)], w=[(pf, "es", kc), ("ps", pq)])
            for qs in range(NSUB):
                gq = qb * NSUB + qs
                i0, i1 = 2 + 2 * (qs % 2), 3 + 2 * (qs % 2)
                for kc in range(TS):
                    l = es(kc)[:, qs * 128:(qs + 1) * 128]
                    self.mm(self.PS[i0][:], l, xht(kc)[:, 0:512],
                            start=(kc == 0), stop=(kc == TS - 1),
                            r=[(pf, "es", kc), (pf, "xht", kc)], w=[("ps", i0)])
                    self.mm(self.PS[i1][:], l, xht(kc)[:, 512:1024],
                            start=(kc == 0), stop=(kc == TS - 1),
                            r=[(pf, "es", kc), (pf, "xht", kc)], w=[("ps", i1)])
                    self.mm(self.PS[6][:, 0:1], l, self.ONES1[:],
                            start=(kc == 0), stop=(kc == TS - 1),
                            r=[(pf, "es", kc), ("ones1",)], w=[("ps", 6)])
                self.recip(self.RINV[:, qs:qs + 1], self.PS[6][:, 0:1],
                           r=[("ps", 6)], w=[(pf, "rinv", qs), ("ps", 6)])
                pa = qs % 2
                for half, pi in ((0, i0), (1, i1)):
                    self.stt(AO[pa][:, half * 512:(half + 1) * 512],
                             self.PS[pi][:], self.RINV[:, qs:qs + 1],
                             xht(gq)[:, half * 512:(half + 1) * 512],
                             ALU.mult, ALU.add,
                             r=[("ps", pi), (pf, "rinv", qs), (pf, "xht", gq)],
                             w=[(pf, "ao", pa), ("ps", pi)])
                self.ln_stats(st, AO[pa][:], qs, [(pf, "ao", pa)])
                self.ln_math(st, qs, 1)
                b2 = cx[0] % 2
                cx[0] += 1
                self.dma("sync", CXR[b2][:],
                         self.xres_h[s, gq * 128:(gq + 1) * 128, :],
                         f"cr{b2}", r=[("xrh", s, gq)], w=[("cxr", b2)])
                for half in range(2):
                    self.ts(TMP[half], AO[pa][:, half * 512:(half + 1) * 512],
                            st["negm"][:, qs:qs + 1], st["rstd"][:, qs:qs + 1],
                            ALU.add, ALU.mult,
                            r=[(pf, "ao", pa), (spf, "negm", qs), (spf, "rstd", qs)],
                            w=[(pf, "tmp", half)])
                    self.tt(CXR[b2][:, half * 512:(half + 1) * 512],
                            CXR[b2][:, half * 512:(half + 1) * 512],
                            TMP[half], ALU.add,
                            r=[(pf, "tmp", half)], w=[("cxr", b2)])
                self.dma("gpsimd", self.xres_h[s, gq * 128:(gq + 1) * 128, :],
                         CXR[b2][:], "xw",
                         r=[("cxr", b2)], w=[("xrh", s, gq)])

    def phaseD(self, s):
        pf = ("D", s)
        st = self.STA
        spf = st["pf"]
        v = self.view
        XFD = v(BF16, FC * 544)
        XR = v(BF16, FC * 512)
        XK = v(BF16, FC * 512)
        KACT = v(BF16, FC * 512)
        RT = v(BF16, NSUB * D)
        VR = v(F32, NSUB * D)
        WS2 = v(BF16, FC * 512)
        XH = [v(BF16, D) for _ in range(2)]
        DD = v(BF16, 512)
        KR = v(BF16, 512)
        XO = [v(F32, D) for _ in range(2)]
        TMP = [v(F32, 512) for _ in range(2)]
        DXI = [v(F32, 1024) for _ in range(6)]
        xfd3 = XFD.rearrange("p (fc r) -> p fc r", fc=FC)
        xfd = lambda fc: XFD[:, fc * 544:(fc + 1) * 544]
        xr = lambda fc: XR[:, fc * 512:(fc + 1) * 512]
        xk = lambda fc: XK[:, fc * 512:(fc + 1) * 512]
        ka = lambda dcb: KACT[:, dcb * 512:(dcb + 1) * 512]
        dx = [0]

        self.vmemset(xfd3[:, :, 15:16], 0.0, w=[(pf, "bnd")])

        for c in range(NCH):
            xfdr = [(pf, "xfd", fc, sub) for fc in range(FC) for sub in range(NSUB)]
            if c > 0:
                self.vcopy(xfd3[:, :, 15:16], xfd3[:, :, 527:528],
                           r=xfdr, w=[(pf, "bnd")])
            tb = [(c * NSUB + sub) % 6 for sub in range(NSUB)]
            for sub in range(NSUB):
                g = c * NSUB + sub
                b = tb[sub]
                self.dma("sync", DXI[b][:], self.xres_h[s, g * 128:(g + 1) * 128, :],
                         f"dx{b % 3}", r=[("xrh", s, g)], w=[("dxi", b)])
                self.ln_stats(st, DXI[b][:], sub, [("dxi", b)])
            self.ln_math(st, 0, NSUB)
            for sub in range(NSUB):
                g = c * NSUB + sub
                b = tb[sub]
                p = sub % 2
                self.ts(XH[p], DXI[b][:],
                        st["negm"][:, sub:sub + 1], st["rstd"][:, sub:sub + 1],
                        ALU.add, ALU.mult,
                        r=[("dxi", b), (spf, "negm", 0), (spf, "rstd", 0)],
                        w=[(pf, "xh", p)])
                for fc in range(FC):
                    self.dmat(xfd(fc)[:, 16 + sub * 128:16 + (sub + 1) * 128],
                              XH[p][:, fc * 128:(fc + 1) * 128],
                              r=[(pf, "xh", p), (pf, "bnd")],
                              w=[(pf, "xfd", fc, sub)])
            for fc in range(FC):
                cur = xfd(fc)[:, 16:528]
                prv = xfd(fc)[:, 15:527]
                self.tt(DD, cur, prv, ALU.subtract,
                        r=xfdr + [(pf, "bnd")], w=[(pf, "dd")])
                self.stt(xr(fc), DD, self.MUR[:, fc:fc + 1], prv, ALU.mult, ALU.add,
                         r=[(pf, "dd"), ("mur",)] + xfdr + [(pf, "bnd")],
                         w=[(pf, "xr", fc)])
                self.stt(xk(fc), DD, self.MUK[:, fc:fc + 1], prv, ALU.mult, ALU.add,
                         r=[(pf, "dd"), ("muk",)] + xfdr + [(pf, "bnd")],
                         w=[(pf, "xk", fc)])
            xrr = [(pf, "xr", fc) for fc in range(FC)]
            xkr = [(pf, "xk", fc) for fc in range(FC)]
            for dcb in range(FC):
                p = dcb % 2
                wb = self.WSTR[:, p * 1024:p * 1024 + 1024]
                self.dma("sync", wb, self.wkt_p[dcb], f"wb{p}",
                         r=[("wkt_p", j) for j in range(4)], w=[("wbd", p)])
                for fc in range(FC):
                    self.mm(self.PS[p][:], wb[:, fc * 128:(fc + 1) * 128], xk(fc),
                            start=(fc == 0), stop=(fc == FC - 1),
                            r=[("wbd", p)] + xkr, w=[("ps", p)])
                self.act(KR, self.PS[p][:], AF.Relu,
                         r=[("ps", p)], w=[(pf, "kr"), ("ps", p)])
                self.tt(ka(dcb), KR, KR, ALU.mult, r=[(pf, "kr")], w=[(pf, "ka", dcb)])
            for half in range(2):
                for fc in range(FC):
                    self.dma("sync", WS2[:, fc * 512:(fc + 1) * 512],
                             self.wrt_p[fc * 128:(fc + 1) * 128,
                                        half * 512:(half + 1) * 512],
                             "ws2",
                             r=[("wrt_p", fc // 2)], w=[(pf, "ws2d", fc)])
                for sub in range(NSUB):
                    for fc in range(FC):
                        self.mm(self.PS[2 + sub][:],
                                xr(fc)[:, sub * 128:(sub + 1) * 128],
                                WS2[:, fc * 512:(fc + 1) * 512],
                                start=(fc == 0), stop=(fc == FC - 1),
                                r=[(pf, "ws2d", fc)] + xrr, w=[("ps", 2 + sub)])
                    self.act(RT[:, sub * D + half * 512:sub * D + (half + 1) * 512],
                             self.PS[2 + sub][:], AF.Sigmoid,
                             r=[("ps", 2 + sub)],
                             w=[(pf, "rt", sub), ("ps", 2 + sub)])
            for half in range(2):
                for dcb in range(FC):
                    self.dma("sync", WS2[:, dcb * 512:(dcb + 1) * 512],
                             self.wvt_p[dcb * 128:(dcb + 1) * 128,
                                        half * 512:(half + 1) * 512],
                             "ws2",
                             r=[("wvt_p", dcb // 2)], w=[(pf, "ws2d", dcb)])
                for sub in range(NSUB):
                    for dcb in range(FC):
                        self.mm(self.PS[2 + sub][:],
                                ka(dcb)[:, sub * 128:(sub + 1) * 128],
                                WS2[:, dcb * 512:(dcb + 1) * 512],
                                start=(dcb == 0), stop=(dcb == FC - 1),
                                r=[(pf, "ws2d", dcb), (pf, "ka", dcb)],
                                w=[("ps", 2 + sub)])
                    self.tt(VR[:, sub * D + half * 512:sub * D + (half + 1) * 512],
                            self.PS[2 + sub][:],
                            RT[:, sub * D + half * 512:sub * D + (half + 1) * 512],
                            ALU.mult,
                            r=[("ps", 2 + sub), (pf, "rt", sub)],
                            w=[(pf, "vr", sub), ("ps", 2 + sub)])
            for sub in range(NSUB):
                self.ln_stats(st, VR[:, sub * D:(sub + 1) * D], sub,
                              [(pf, "vr", sub)])
            self.ln_math(st, 0, NSUB)
            for sub in range(NSUB):
                g = c * NSUB + sub
                b = tb[sub]
                p = sub % 2
                for half in range(2):
                    self.ts(TMP[half],
                            VR[:, sub * D + half * 512:sub * D + (half + 1) * 512],
                            st["negm"][:, sub:sub + 1], st["rstd"][:, sub:sub + 1],
                            ALU.add, ALU.mult,
                            r=[(pf, "vr", sub), (spf, "negm", 0), (spf, "rstd", 0)],
                            w=[(pf, "tmp", half)])
                    self.tt(XO[p][:, half * 512:(half + 1) * 512],
                            DXI[b][:, half * 512:(half + 1) * 512],
                            TMP[half], ALU.add,
                            r=[(pf, "tmp", half), ("dxi", b)], w=[(pf, "xo", p)])
                self.dma("sync", self.out[s, g * 128:(g + 1) * 128, :], XO[p][:],
                         "out", r=[(pf, "xo", p)], w=[("out", s, g)])


def _layouts(i):
    def col16(v):
        return np.ascontiguousarray(v.reshape(16, 128).T)
    out = {}
    out["conv_w_r"] = np.ascontiguousarray(
        i["L0_attn_conv_w"].reshape(16, 128, 4).transpose(1, 0, 2).reshape(128, 64))
    out["conv_b_r"] = col16(i["L0_attn_conv_b"])
    out["b_dt_r"] = col16(i["L0_attn_b_dt"])
    out["d_r"] = col16(i["L0_attn_D"])
    out["a_log_r"] = np.ascontiguousarray(
        i["L0_attn_A_log"].reshape(16, 128, 16).transpose(1, 0, 2).reshape(128, 256))
    out["w_in"] = np.ascontiguousarray(
        i["L0_attn_W_in"].reshape(8, 128, 32, 128).transpose(2, 1, 0, 3)
        .reshape(32, 128, 1024))
    out["w_x"] = np.ascontiguousarray(
        i["L0_attn_W_x"].reshape(16, 128, 96).transpose(1, 0, 2).reshape(128, 16 * 96))
    out["w_dt"] = np.ascontiguousarray(i["L0_attn_W_dt"])
    out["w_out"] = np.ascontiguousarray(i["L0_attn_W_out"])
    out["w0t"] = np.ascontiguousarray(i["L0_ffn_weight"].T)
    out["gatet"] = np.ascontiguousarray(
        i["L0_ffn_gate_w"].T.reshape(8, 128, 8).transpose(1, 0, 2).reshape(128, 64))
    out["ut"] = np.ascontiguousarray(
        i["L0_ffn_lora_U"].reshape(128, 1024).T.reshape(8, 128, 128)
        .transpose(1, 0, 2).reshape(128, 8 * 128))
    out["v_r"] = np.ascontiguousarray(
        i["L0_ffn_lora_V"].transpose(0, 2, 1).reshape(128, 1024))
    out["wrt"] = np.ascontiguousarray(i["L1_ffn_W_r"].T)
    out["wkt"] = np.ascontiguousarray(
        i["L1_ffn_W_k"].T.reshape(8, 128, 8, 128).transpose(2, 1, 0, 3)
        .reshape(8, 128, 1024))
    out["wvt"] = np.ascontiguousarray(i["L1_ffn_W_v"].T)
    out["mur_r"] = np.ascontiguousarray(i["L1_ffn_mu_r"].reshape(8, 128).T)
    out["muk_r"] = np.ascontiguousarray(i["L1_ffn_mu_k"].reshape(8, 128).T)
    bf = ml_dtypes.bfloat16
    selb = np.zeros((32, NST * 128), np.float32)
    selc = np.zeros((32, NST * 128), np.float32)
    for n in range(NST):
        selb[n, n * 128:(n + 1) * 128] = 1.0
        selc[16 + n, n * 128:(n + 1) * 128] = 1.0
    out["selb"] = selb.astype(bf)
    out["selc"] = selc.astype(bf)
    rep = np.zeros((8, 128), np.float32)
    for e in range(8):
        rep[e, e * 16:(e + 1) * 16] = 1.0
    out["rep8"] = rep.astype(bf)
    out["ones1"] = np.ones((128, 1), np.float32).astype(bf)
    out["iden"] = np.eye(128, dtype=np.float32).astype(bf)
    return out


_NC_CACHE = None


_BUILDER = None


def _get_nc():
    global _NC_CACHE, _BUILDER
    if _NC_CACHE is None:
        _BUILDER = Builder()
        _NC_CACHE = _BUILDER.build()
    return _NC_CACHE


def kernel(**inputs):
    x = np.ascontiguousarray(np.asarray(inputs["L0_attn_x"], dtype=np.float32))
    shared = _layouts({k: np.asarray(v, dtype=np.float32)
                       for k, v in inputs.items() if k != "L0_attn_x"})
    nc = _get_nc()
    in_maps = []
    for i in range(8):
        m = dict(shared)
        m["x"] = np.ascontiguousarray(x[2 * i:2 * i + 2])
        in_maps.append(m)
    res = run_bass_kernel_spmd(nc, in_maps, core_ids=list(range(8)))
    out = np.stack([r["out"] for r in res.results], axis=0)
    return out.reshape(16, SEQ, D).astype(np.float32)


# version 12 (final)
# speedup vs baseline: 1.3900x; 1.0105x over previous
import sys
import numpy as np
import ml_dtypes
from contextlib import ExitStack

import concourse.bass as bass
import concourse.mybir as mybir
from concourse.bass_utils import run_bass_kernel_spmd

F32 = mybir.dt.float32
BF16 = mybir.dt.bfloat16
FP16 = mybir.dt.float16
AF = mybir.ActivationFunctionType
ALU = mybir.AluOpType
AXX = mybir.AxisListType.X

D = 1024
DI = 2048
SEQ = 2048
NSEQ = 2
NST = 16
DTR = 64
LC = 512
NCH = SEQ // LC
NSUB = LC // 128
FC = D // 128
DC = DI // 128
TS = SEQ // 128
EPS = 1e-5
UOF = 8
UW = UOF + 512

ENGS = ("tensor", "vector", "scalar", "gpsimd", "sync")
_A_ONLY = False
RELAXED = False


class Planner:

    def __init__(self):
        self.ops = []
        self.bufs = {}
        self.groups = {}
        self.nbar = 0
        self.last_eng = {}

    def _b(self, k):
        b = self.bufs.get(k)
        if b is None:
            b = {"w": None, "r": []}
            self.bufs[k] = b
        return b

    def _g(self, name):
        g = self.groups.get(name)
        if g is None:
            g = {"gen": 0, "count": 0, "bounds": [], "pending": set(), "last": None}
            self.groups[name] = g
        return g

    def add(self, eng, fn, r=(), w=(), grp=None, extra=(), bar_inc=False,
            bar_wait=None, async_out=False):
        i = len(self.ops)
        deps = set(extra)
        sdeps = set()
        for k in r:
            b = self._b(k)
            if b["w"] is not None:
                deps.add(b["w"])
        for k in w:
            b = self._b(k)
            if b["w"] is not None:
                sdeps.add(b["w"])
            sdeps.update(b["r"])
        deps.discard(i)
        sdeps.discard(i)
        deps -= sdeps
        gw = {}
        eng_deps = set()
        for j in deps | sdeps:
            oj = self.ops[j]
            if oj["grp"] is not None:
                gname, gen = oj["grp"]
                gw[gname] = max(gw.get(gname, -1), gen)
                gg = self._g(gname)
                if gen == gg["gen"]:
                    gg["pending"].add(i)
            else:
                eng_deps.add(j)
        gtag = None
        if grp is not None:
            g = self._g(grp)
            pend = {j for j in g["pending"] if j != i}
            if pend:
                g["bounds"].append(g["count"])
                g["gen"] += 1
                eng_deps.update(j for j in pend if self.ops[j]["grp"] is None)
                for j in pend:
                    oj = self.ops[j]
                    if oj["grp"] is not None:
                        gn, gg = oj["grp"]
                        gw[gn] = max(gw.get(gn, -1), gg)
                g["pending"] = set()
            done = g["bounds"][-1] if g["bounds"] else 0
            if gw.get(grp, -1) == g["gen"] and g["count"] > done:
                g["bounds"].append(g["count"])
                g["gen"] += 1
            g["count"] += 1
            g["last"] = i
            gtag = (grp, g["gen"])
        self.ops.append({"eng": eng, "fn": fn, "grp": gtag, "deps": eng_deps,
                         "gw": gw, "async_out": async_out,
                         "bar_inc": bar_inc, "bar_wait": bar_wait})
        for k in r:
            self._b(k)["r"].append(i)
        for k in w:
            b = self._b(k)
            b["w"] = i
            b["r"] = []
        if gtag is None and not bar_inc and bar_wait is None:
            self.last_eng[eng] = i
        return i

    def barrier(self):
        lasts = [i for i in self.last_eng.values()]
        last_dmas = [g["last"] for g in self.groups.values() if g["last"] is not None]
        for g in self.groups.values():
            done = g["bounds"][-1] if g["bounds"] else 0
            if g["count"] > done:
                g["bounds"].append(g["count"])
                g["gen"] += 1
        self.add("sync", lambda eng: eng.nop(), extra=lasts + last_dmas,
                 bar_inc=True)
        self.nbar += 1
        for e in ENGS:
            if e != "sync":
                self.add(e, lambda eng: eng.nop(), bar_wait=self.nbar)

    def finalize(self):
        n = len(self.ops)
        for g in self.groups.values():
            g["bounds"].append(g["count"])
        for o in self.ops:
            if o["grp"] is None and o["eng"] == "tensor":
                o["deps"] = {j for j in o["deps"]
                             if self.ops[j]["eng"] != "tensor"
                             or self.ops[j]["grp"] is not None
                             or self.ops[j]["async_out"]}
        if RELAXED:
            for o in self.ops:
                if o["grp"] is None:
                    o["deps"] = {j for j in o["deps"]
                                 if self.ops[j]["eng"] != o["eng"]
                                 or self.ops[j]["grp"] is not None
                                 or self.ops[j]["async_out"]}
            need = [False] * n
            for o in self.ops:
                for j in o["deps"]:
                    need[j] = True
        else:
            need = [o["grp"] is None and not o["bar_inc"] and o["bar_wait"] is None
                    for o in self.ops]
        cum = {}
        val = [0] * n
        for i, o in enumerate(self.ops):
            if o["grp"] is not None:
                continue
            if need[i]:
                s = "e_" + o["eng"]
                cum[s] = cum.get(s, 0) + 1
                val[i] = cum[s]

        def gen_threshold(gname, gen):
            b = self.groups[gname]["bounds"]
            k = min(gen, len(b) - 1)
            return 16 * b[k]

        sched = {k: [] for k in ENGS}
        floor = {e: {} for e in ENGS}
        for i, o in enumerate(self.ops):
            wmap = {}
            for j in o["deps"]:
                if val[j] > 0:
                    s = "e_" + self.ops[j]["eng"]
                    wmap[s] = max(wmap.get(s, 0), val[j])
            for gname, gen in o["gw"].items():
                s = "g_" + gname
                wmap[s] = max(wmap.get(s, 0), gen_threshold(gname, gen))
            if o["bar_wait"] is not None:
                wmap["bar"] = o["bar_wait"]
            if o["bar_inc"]:
                inc = ("bar", 1)
                nbc = self.bar_cnt = getattr(self, "bar_cnt", 0) + 1
                if nbc > 1:
                    wmap["bar"] = max(wmap.get("bar", 0), nbc - 1)
            elif o["grp"] is not None:
                inc = ("g_" + o["grp"][0], 16)
            elif need[i]:
                inc = ("e_" + o["eng"], 1)
                pass
            else:
                inc = None
            fl = floor[o["eng"]]
            wl = []
            for s, v in sorted(wmap.items()):
                if fl.get(s, -1) < v:
                    fl[s] = v
                    wl.append((s, v))
            sched[o["eng"]].append((o["fn"], wl, inc))
        return sched, cum


class Builder:
    def __init__(self):
        self.nc = bass.Bass("TRN2", target_bir_lowering=False,
                            detect_race_conditions=not RELAXED)
        self.pl = Planner()
        self.ctx = ExitStack()
        self.arena_off = 0

    def sb(self, name, shape, dt):
        return self.ctx.enter_context(self.nc.sbuf_tensor(name, shape, dt))

    def view(self, dt, cols):
        esz = 2 if dt in (BF16, FP16) else 4
        off = (self.arena_off + 63) & ~63
        self.arena_off = off + cols * esz
        assert self.arena_off <= self.ARENA_BYTES, \
            f"arena overflow: {self.arena_off} > {self.ARENA_BYTES}"
        base = self.arena.ap() if dt == BF16 else self.arena.ap().bitcast(dt)
        return base[:, off // esz: off // esz + cols]

    def reset_arena(self):
        self.arena_off = 0

    def dma(self, q, out, in_, grp, r=(), w=()):
        self.pl.add(q, lambda e, o=out, i=in_: e.dma_start(out=o, in_=i),
                    r=r, w=w, grp=grp)

    def dmat(self, out, in_, r=(), w=()):
        self.pl.add("scalar",
                    lambda e, o=out, i=in_: e.dma_start_transpose(out=o, in_=i),
                    r=r, w=w, grp="tp")

    def mm(self, out, lhsT, rhs, start, stop, r=(), w=()):
        self.pl.add("tensor",
                    lambda e, o=out, l=lhsT, rr=rhs, s1=start, s2=stop:
                    e.matmul(o, l, rr, start=s1, stop=s2), r=r, w=w)

    def tpose(self, out, in_, r=(), w=()):
        self.pl.add("tensor",
                    lambda e, o=out, i=in_, d=self.IDEN[:]: e.transpose(o, i, d),
                    r=r, w=w)

    def act(self, out, in_, func, bias=0.0, scale=1.0, accum=None, r=(), w=()):
        self.pl.add("scalar",
                    lambda e, o=out, i=in_, f=func, b=bias, s=scale, a=accum:
                    e.activation(out=o, in_=i, func=f, bias=b, scale=s, accum_out=a),
                    r=r, w=w, async_out=accum is not None)

    def tt(self, out, a, b, op, r=(), w=()):
        self.pl.add("vector",
                    lambda e, o=out, x=a, y=b, p=op:
                    e.tensor_tensor(out=o, in0=x, in1=y, op=p), r=r, w=w)

    def ts(self, out, a, s1, s2, op0, op1, accum=None, r=(), w=()):
        if s2 is None and accum is None:
            self.pl.add("vector",
                        lambda e, o=out, x=a, u=s1, p0=op0:
                        e.tensor_scalar(out=o, in0=x, scalar1=u, scalar2=None, op0=p0),
                        r=r, w=w)
        else:
            self.pl.add("vector",
                        lambda e, o=out, x=a, u=s1, v=s2, p0=op0, p1=op1, ac=accum:
                        e.tensor_scalar(out=o, in0=x, scalar1=u, scalar2=v,
                                        op0=p0, op1=p1, accum_out=ac), r=r, w=w,
                        async_out=accum is not None)

    def stt(self, out, in0, scal, in1, op0, op1, r=(), w=()):
        self.pl.add("vector",
                    lambda e, o=out, x=in0, u=scal, y=in1, p0=op0, p1=op1:
                    e.scalar_tensor_tensor(out=o, in0=x, scalar=u, in1=y,
                                           op0=p0, op1=p1), r=r, w=w)

    def scan(self, out, a, b, init, r=(), w=()):
        self.pl.add("vector",
                    lambda e, o=out, x=a, y=b, h0=init:
                    e.tensor_tensor_scan(out=o, data0=x, data1=y, initial=h0,
                                         op0=ALU.mult, op1=ALU.add), r=r, w=w)

    def vcopy(self, out, in_, r=(), w=()):
        self.pl.add("vector", lambda e, o=out, i=in_: e.tensor_copy(out=o, in_=i),
                    r=r, w=w)

    def gcopy(self, out, in_, r=(), w=()):
        self.pl.add("gpsimd", lambda e, o=out, i=in_: e.tensor_copy(out=o, in_=i),
                    r=r, w=w)

    def vmemset(self, ap, val, r=(), w=()):
        self.pl.add("vector", lambda e, a=ap, v=val: e.memset(a, v), r=r, w=w)

    def rmax(self, out, in_, negate, r=(), w=()):
        self.pl.add("vector",
                    lambda e, o=out, i=in_, ng=negate:
                    e.reduce_max(out=o, in_=i, axis=AXX, negate=ng), r=r, w=w)

    def rsumf(self, out, in_, r=(), w=()):
        self.pl.add("vector",
                    lambda e, o=out, i=in_: e.reduce_sum(out=o, in_=i, axis=AXX),
                    r=r, w=w)

    def recip(self, out, in_, r=(), w=()):
        self.pl.add("vector", lambda e, o=out, i=in_: e.reciprocal(out=o, in_=i),
                    r=r, w=w)

    def build(self):
        nc = self.nc
        self.x_in = nc.dram_tensor("x", [NSEQ, SEQ, D], F32, kind="ExternalInput").ap()
        self.out = nc.dram_tensor("out", [NSEQ, SEQ, D], F32, kind="ExternalOutput").ap()
        f32in = lambda n, sh: nc.dram_tensor(n, sh, F32, kind="ExternalInput").ap()
        bf16in = lambda n, sh: nc.dram_tensor(n, sh, BF16, kind="ExternalInput").ap()
        self.w_in = f32in("w_in", [2 * DC, 128, 8 * 128])
        self.conv_w_r = f32in("conv_w_r", [128, 64])
        self.conv_b_r = f32in("conv_b_r", [128, 16])
        self.b_dt_r = f32in("b_dt_r", [128, 16])
        self.d_r = f32in("d_r", [128, 16])
        self.a_log_r = f32in("a_log_r", [128, 256])
        self.w_x = f32in("w_x", [128, DC * 96])
        self.w_dt = f32in("w_dt", [DTR, DI])
        self.w_out = f32in("w_out", [DI, D])
        self.w0t = f32in("w0t", [D, D])
        self.gatet = f32in("gatet", [128, FC * 8])
        self.ut = f32in("ut", [128, FC * 128])
        self.v_r = f32in("v_r", [128, D])
        self.wrt = f32in("wrt", [D, D])
        self.wkt = f32in("wkt", [FC, 128, FC * 128])
        self.wvt = f32in("wvt", [D, D])
        self.mur_r = f32in("mur_r", [128, 8])
        self.muk_r = f32in("muk_r", [128, 8])
        self.selb = bf16in("selb", [32, NST * 128])
        self.selc = bf16in("selc", [32, NST * 128])
        self.rep8 = bf16in("rep8", [8, 128])
        self.ones1 = bf16in("ones1", [128, 1])
        self.iden = bf16in("iden", [128, 128])

        self.win_p = nc.dram_tensor("win_p", [2 * DC, 128, 8 * 128], BF16).ap()
        self.wout_p = nc.dram_tensor("wout_p", [DI, D], BF16).ap()
        self.w0t_p = nc.dram_tensor("w0t_p", [D, D], BF16).ap()
        self.wrt_p = nc.dram_tensor("wrt_p", [D, D], BF16).ap()
        self.wkt_p = nc.dram_tensor("wkt_p", [FC, 128, FC * 128], BF16).ap()
        self.wvt_p = nc.dram_tensor("wvt_p", [D, D], BF16).ap()
        self.xres_h = nc.dram_tensor("xres_h", [NSEQ, SEQ, D], F32).ap()

        self.ARENA_BYTES = 172544
        self.arena = self.sb("arena", [128, self.ARENA_BYTES // 2], BF16)
        self.WSTR = self.sb("wstr", [128, 4 * 1024], BF16)
        self.WX = self.sb("wxr", [128, DC * 96], BF16)
        self.WDT = self.sb("wdtr", [DTR, DI], BF16)
        self.UTS = self.sb("uts", [128, FC * 128], BF16)
        self.VSB = self.sb("vsb", [128, D], BF16)
        self.GATET = self.sb("gatetr", [128, FC * 8], BF16)
        self.SELB = self.sb("selbr", [128, NST * 128], BF16)
        self.SELC = self.sb("selcr", [128, NST * 128], BF16)
        self.REP8 = self.sb("rep8r", [8, 128], BF16)
        self.ONES1 = self.sb("ones1r", [128, 1], BF16)
        self.IDEN = self.sb("idenr", [128, 128], BF16)
        self.CONVW = self.sb("convw", [128, 64], F32)
        self.CONVB = self.sb("convb", [128, 16], F32)
        self.BDT = self.sb("bdt", [128, 16], F32)
        self.DCOL = self.sb("dcol", [128, 16], F32)
        self.ANEG = self.sb("aneg", [128, 256], F32)
        self.MUR = self.sb("mur", [128, 8], F32)
        self.MUK = self.sb("muk", [128, 8], F32)
        self.HST = self.sb("hst", [128, 512], F32)
        self.DV = self.sb("dv", [128, 1024], BF16)
        self.DA = self.sb("da", [128, 1024], BF16)
        def stats_set(tag):
            t = lambda nm, c: self.sb(tag + nm, [128, c], F32)
            return {"pf": tag, "s1": t("s1", 16), "s2": t("s2", 16),
                    "negm": t("nm", 16), "rstd": t("rs", 16),
                    "sa": t("sa", 16), "sb": t("sb", 16),
                    "sv": t("sv", 16), "sw": t("sw", 16),
                    "aux": t("ax", 16), "aux2": t("a2", 16),
                    "dv": self.DV, "da": self.DA}
        self.STA = stats_set("LA")
        self.STB = stats_set("LB")
        self.RINV = self.sb("rinv", [128, 4], F32)
        self.NM = self.sb("nm", [128, 2], F32)
        self.GE = self.sb("ge", [128, 16], F32)
        self.GS = self.sb("gs", [128, 4], F32)
        self.GSUB = self.sb("gsub", [128, 16], BF16)
        self.GFM = self.sb("gfm", [8, 2 * LC], BF16)
        self.FCONST = self.sb("fconst", [128, 2], F32)
        self.nc.const_aps.register(F32, EPS, self.FCONST[:, 0:1])
        self.nc.const_aps.register(F32, 1.0, self.FCONST[:, 1:2])
        self.PS = [self.ctx.enter_context(self.nc.psum_tensor(f"ps{i}", [128, 512], F32))
                   for i in range(8)]

        self.prep()
        self.pl.barrier()
        self.reset_arena()
        self.phaseA_merged()
        if not _A_ONLY:
            self.pl.barrier()
            self.reset_arena()
            self.phaseB_merged()
            for s in range(NSEQ):
                self.pl.barrier()
                self.reset_arena()
                self.phaseC(s)
                self.pl.barrier()
                self.reset_arena()
                self.phaseD(s)

        self.pl.add("sync", lambda e: e.nop(),
                    r=[("out", s, g) for s in range(NSEQ) for g in range(TS)])

        sched, cum = self.pl.finalize()
        sems = {}
        names = ["e_tensor", "e_vector", "e_scalar", "e_gpsimd", "e_sync", "bar"]
        names += ["g_" + g for g in self.pl.groups]
        for name in names:
            sems[name] = self.ctx.enter_context(nc.semaphore(name))

        def body(eng_obj, items):
            for fn, wl, inc in items:
                for sname, v in wl:
                    eng_obj.wait_ge(sems[sname], v)
                inst = fn(eng_obj)
                if inc is not None:
                    inst.then_inc(sems[inc[0]], inc[1])

        with nc.Block() as blk:
            blk.tensor(lambda e: body(e, sched["tensor"]))
            blk.vector(lambda e: body(e, sched["vector"]))
            blk.scalar(lambda e: body(e, sched["scalar"]))
            blk.gpsimd(lambda e: body(e, sched["gpsimd"]))
            blk.sync(lambda e: body(e, sched["sync"]))
        self.ctx.close()
        return nc

    def prep(self):
        self.vmemset(self.FCONST[:, 0:1], EPS, w=[("fconst",)])
        self.vmemset(self.FCONST[:, 1:2], 1.0, w=[("fconst",)])
        for dst, src, key in [(self.win_p, self.w_in, "win_p"),
                              (self.wout_p, self.w_out, "wout_p"),
                              (self.w0t_p, self.w0t, "w0t_p"),
                              (self.wrt_p, self.wrt, "wrt_p"),
                              (self.wkt_p, self.wkt, "wkt_p"),
                              (self.wvt_p, self.wvt, "wvt_p")]:
            rows = src.shape[0]
            q = rows // 4
            for j in range(4):
                self.dma("gpsimd", dst[j * q:(j + 1) * q], src[j * q:(j + 1) * q],
                         "p_" + key, w=[(key, j)])
        self.dma("gpsimd", self.WX[:], self.w_x[:], "pres", w=[("wx",)])
        self.dma("gpsimd", self.WDT[:], self.w_dt[:], "pres", w=[("wdt",)])
        self.dma("gpsimd", self.UTS[:], self.ut[:], "pres", w=[("uts",)])
        self.dma("gpsimd", self.VSB[:], self.v_r[:], "pres", w=[("vsb",)])
        self.dma("gpsimd", self.GATET[:], self.gatet[:], "pres", w=[("gatet",)])
        for dst, src, key in [(self.CONVW, self.conv_w_r, "convw"),
                              (self.CONVB, self.conv_b_r, "convb"),
                              (self.BDT, self.b_dt_r, "bdt"),
                              (self.DCOL, self.d_r, "dcol"),
                              (self.MUR, self.mur_r, "mur"),
                              (self.MUK, self.muk_r, "muk")]:
            self.dma("sync", dst[:], src[:], "psml", w=[(key,)])
        self.dma("sync", self.SELB[64:96, :], self.selb[:], "pcon", w=[("selb",)])
        self.dma("sync", self.SELC[64:96, :], self.selc[:], "pcon", w=[("selc",)])
        self.dma("sync", self.REP8[:], self.rep8[:], "pcon", w=[("rep8",)])
        self.dma("sync", self.ONES1[:], self.ones1[:], "pcon", w=[("ones1",)])
        self.dma("sync", self.IDEN[:], self.iden[:], "pcon", w=[("iden",)])
        self.dma("sync", self.ANEG[:], self.a_log_r[:], "pan", w=[("aneg",)])
        self.act(self.ANEG[:], self.ANEG[:], AF.Exp, r=[("aneg",)], w=[("aneg",)])
        self.ts(self.ANEG[:], self.ANEG[:], -1.0, None, ALU.mult, None,
                r=[("aneg",)], w=[("aneg",)])

    def ln_stats(self, st, src_ap, col, key_r):
        pf = st["pf"]
        self.ts(st["dv"][:, 0:src_ap.shape[1]], src_ap, 1.0, 0.0, ALU.mult, ALU.add,
                accum=st["s1"][:, col:col + 1],
                r=key_r, w=[(pf, "s1", col), (pf, "dumpv")])
        self.act(st["da"][:, 0:src_ap.shape[1]], src_ap, AF.Square,
                 accum=st["s2"][:, col:col + 1],
                 r=key_r, w=[(pf, "s2", col), (pf, "dumpa")])

    def ln_math(self, st, lo, n, want_r=False):
        pf = st["pf"]
        sl = slice(lo, lo + n)
        rs1 = [(pf, "s1", c) for c in range(lo, lo + n)]
        rs2 = [(pf, "s2", c) for c in range(lo, lo + n)]
        self.ts(st["negm"][:, sl], st["s1"][:, sl], -1.0 / D, None, ALU.mult, None,
                r=rs1, w=[(pf, "negm", lo)])
        self.ts(st["sa"][:, sl], st["s1"][:, sl], 1.0 / D, None, ALU.mult, None,
                r=rs1, w=[(pf, "sa", lo)])
        self.tt(st["sb"][:, sl], st["sa"][:, sl], st["sa"][:, sl], ALU.mult,
                r=[(pf, "sa", lo)], w=[(pf, "sb", lo)])
        self.stt(st["sv"][:, sl], st["s2"][:, sl], 1.0 / D, st["sb"][:, sl],
                 ALU.mult, ALU.subtract, r=rs2 + [(pf, "sb", lo)], w=[(pf, "sv", lo)])
        self.act(st["sw"][:, sl], st["sv"][:, sl], AF.Ln, bias=EPS,
                 r=[(pf, "sv", lo)], w=[(pf, "sw", lo)])
        self.act(st["rstd"][:, sl], st["sw"][:, sl], AF.Exp, scale=-0.5,
                 r=[(pf, "sw", lo)], w=[(pf, "rstd", lo)])
        if want_r:
            self.tt(st["aux"][:, sl], st["rstd"][:, sl], st["rstd"][:, sl], ALU.mult,
                    r=[(pf, "rstd", lo)], w=[(pf, "aux", lo)])
            self.stt(st["aux"][:, sl], st["sv"][:, sl], float(D), st["aux"][:, sl],
                     ALU.mult, ALU.mult,
                     r=[(pf, "sv", lo), (pf, "aux", lo)], w=[(pf, "aux", lo)])
            self.ts(st["aux"][:, sl], st["aux"][:, sl], 1e-20, None, ALU.max, None,
                    r=[(pf, "aux", lo)], w=[(pf, "aux", lo)])
            self.act(st["aux"][:, sl], st["aux"][:, sl], AF.Ln,
                     r=[(pf, "aux", lo)], w=[(pf, "aux", lo)])
            self.act(st["aux"][:, sl], st["aux"][:, sl], AF.Exp, scale=-0.5,
                     r=[(pf, "aux", lo)], w=[(pf, "aux", lo)])
            self.tt(st["aux2"][:, sl], st["rstd"][:, sl], st["aux"][:, sl], ALU.mult,
                    r=[(pf, "rstd", lo), (pf, "aux", lo)], w=[(pf, "aux2", lo)])

    def phaseA_merged(self):
        v = self.view
        self.AU = [v(BF16, DC * UW) for _ in range(2)]
        self.ADEL = v(BF16, DC * 512)
        self.AW = v(BF16, DC * 512)
        self.AY = v(FP16, DC * 512)
        self.AXFA = [v(BF16, FC * 512) for _ in range(2)]
        self.AX0 = [v(F32, 1024) for _ in range(4)]
        self.AXH = [v(BF16, 1024) for _ in range(4)]
        self.AXO = [v(F32, 1024) for _ in range(4)]
        self.ADR = v(BF16, 512)
        self.AXDB = [v(BF16, 512) for _ in range(2)]
        self.AE1 = v(F32, 512)
        self.ABBC = [v(BF16, 512) for _ in range(2)]
        self.ACBC = [v(BF16, 512) for _ in range(2)]
        self.AA2 = [v(FP16, 512) for _ in range(4)]
        self.AB2 = [v(BF16, 512) for _ in range(2)]
        self.AH2 = [v(BF16, 512) for _ in range(2)]
        self.AYC = [v(BF16, 512) for _ in range(2)]
        self.AACC = v(BF16, 512)
        self.AGR = [v(BF16, 512) for _ in range(8)]
        self.ATMP = [v(F32, 512) for _ in range(2)]
        self.x0cnt = 0
        self.xhcnt = 0
        self.a2c = 0
        self.rx0 = {}
        NSTEP = 2 * NCH

        self.vmemset(self.HST[:], 0.0,
                     w=[("hst", s, n, dc) for s in range(2)
                        for n in range(NST) for dc in range(DC)])
        for q in range(2):
            u3 = self.AU[q].rearrange("p (dc r) -> p dc r", dc=DC)
            self.vmemset(u3[:, :, UOF - 3:UOF], 0.0,
                         w=[("tl", q, dc) for dc in range(DC)])

        for j0 in (0, 1):
            self.frontA_LN(j0)
            self.frontA_dmats(j0, (0, 1))
            self.frontA_dmats(j0, (2, 3))
            self.frontA_win(j0)
            self.frontA_conv(j0)
        self.frontB_xdbl(0)
        self.frontB_delta(0, 0, 16)
        for j in range(NSTEP):
            nxt = j + 1 < NSTEP
            self.sweepA(j, 0, nxt)
            self.gateA(j, 0)
            if nxt and j + 1 >= 2:
                self.frontA_conv(j + 1)
            self.sweepA(j, 1, nxt)
            self.gateA(j, 8)
            if nxt:
                self.frontB_delta(j + 1, 8, 16)
            self.woutA_close(j)
            self.outLN_store(j)

    def autile(self, q, dc):
        return self.AU[q][:, dc * UW: dc * UW + UW]

    def adtile(self, dc):
        return self.ADEL[:, dc * 512:(dc + 1) * 512]

    def awtile(self, dc):
        return self.AW[:, dc * 512:(dc + 1) * 512]

    def aytile(self, dc):
        return self.AY[:, dc * 512:(dc + 1) * 512]

    def axfa(self, q, fc):
        return self.AXFA[q][:, fc * 512:(fc + 1) * 512]

    def frontA_LN(self, j):
        s, c = j % 2, j // 2
        q = j % 2
        lo = 4 * q
        st = self.STA
        pf = st["pf"]
        for sub in range(NSUB):
            g = c * NSUB + sub
            b = self.x0cnt % 4
            self.x0cnt += 1
            self.dma("sync", self.AX0[b][:], self.x_in[s, g * 128:(g + 1) * 128, :],
                     f"x0{b}", w=[("x0", b)])
            self.ln_stats(st, self.AX0[b][:], lo + sub, [("x0", b)])
        self.ln_math(st, lo, NSUB)
        for sub in range(NSUB):
            g = c * NSUB + sub
            b = self.x0cnt % 4
            self.x0cnt += 1
            rx = sub
            self.dma("sync", self.AX0[b][:], self.x_in[s, g * 128:(g + 1) * 128, :],
                     f"x0{b}", w=[("x0", b)])
            self.ts(self.AXH[rx], self.AX0[b][:],
                    st["negm"][:, lo + sub:lo + sub + 1],
                    st["rstd"][:, lo + sub:lo + sub + 1], ALU.add, ALU.mult,
                    r=[("x0", b), (pf, "negm", lo), (pf, "rstd", lo)],
                    w=[("xh", rx)])

    def frontA_dmats(self, j, subs):
        q = j % 2
        for sub in subs:
            rx = sub
            for fc in range(FC):
                self.dmat(self.axfa(q, fc)[:, sub * 128:(sub + 1) * 128],
                          self.AXH[rx][:, fc * 128:(fc + 1) * 128],
                          r=[("xh", rx)], w=[("xfa", q, fc, sub)])

    def frontA_win(self, j):
        q = j % 2
        xfar = [("xfa", q, fc, sub) for fc in range(FC) for sub in range(NSUB)]
        for oc in range(DC):
            p = oc % 2
            wb = self.WSTR[:, p * 1024:p * 1024 + 1024]
            self.dma("sync", wb, self.win_p[oc], f"wb{p}",
                     r=[("win_p", jj) for jj in range(4)], w=[("wbd", p)])
            pb = self.PS[p]
            for fc in range(FC):
                self.mm(pb[:], wb[:, fc * 128:(fc + 1) * 128], self.axfa(q, fc),
                        start=(fc == 0), stop=(fc == FC - 1),
                        r=[("wbd", p)] + xfar, w=[("ps", p)])
            self.act(self.autile(q, oc)[:, UOF:UOF + 512], pb[:], AF.Copy,
                     r=[("ps", p)], w=[("u", q, oc), ("ps", p)])

    def frontA_conv(self, j):
        q = j % 2
        for dc in range(DC):
            ut = self.autile(q, dc)
            self.ts(self.AACC, ut[:, UOF - 3:UOF + 509],
                    self.CONVW[:, dc * 4:dc * 4 + 1], None, ALU.mult, None,
                    r=[("u", q, dc), ("tl", q, dc), ("convw",)], w=[("acc",)])
            for k in range(1, 4):
                self.stt(self.AACC, ut[:, UOF - 3 + k:UOF + 509 + k],
                         self.CONVW[:, dc * 4 + k:dc * 4 + k + 1], self.AACC,
                         ALU.mult, ALU.add,
                         r=[("u", q, dc), ("tl", q, dc), ("acc",), ("convw",)],
                         w=[("acc",)])
            self.gcopy(ut[:, UOF - 3:UOF], ut[:, UOF + 509:UOF + 512],
                       r=[("u", q, dc)], w=[("tl", q, dc)])
            self.act(ut[:, UOF:UOF + 512], self.AACC, AF.Silu,
                     bias=self.CONVB[:, dc:dc + 1],
                     r=[("acc",), ("convb",)], w=[("u", q, dc)])

    def frontB_xdbl(self, j):
        q = j % 2
        px = self.PS[2]
        for kc in range(DC):
            self.mm(px[0:96, :], self.WX[:, kc * 96:(kc + 1) * 96],
                    self.autile(q, kc)[:, UOF:UOF + 512],
                    start=(kc == 0), stop=(kc == DC - 1),
                    r=[("u", q, kc), ("wx",)], w=[("ps", 2)])
        self.act(self.ADR[0:64, :], px[0:64, :], AF.Copy,
                 r=[("ps", 2)], w=[("dr",)])
        self.act(self.AXDB[q][64:96, :], px[64:96, :], AF.Copy,
                 r=[("ps", 2)], w=[("xdb", q), ("ps", 2)])

    def frontB_delta(self, j, d0, d1):
        q = j % 2
        for dc in range(d0, d1):
            pi = 3 if dc % 2 == 0 else 2
            pd = self.PS[pi]
            self.mm(pd[:], self.WDT[:, dc * 128:(dc + 1) * 128], self.ADR[0:64, :],
                    start=True, stop=True,
                    r=[("dr",), ("wdt",)], w=[("ps", pi)])
            self.act(self.AE1, pd[:], AF.Exp, bias=self.BDT[:, dc:dc + 1],
                     r=[("ps", pi), ("bdt",)], w=[("e1",), ("ps", pi)])
            self.act(self.adtile(dc), self.AE1, AF.Ln, bias=1.0,
                     r=[("e1",)], w=[("del", dc)])
            self.tt(self.awtile(dc), self.adtile(dc),
                    self.autile(q, dc)[:, UOF:UOF + 512], ALU.mult,
                    r=[("del", dc), ("u", q, dc)], w=[("w", dc)])

    def resA(self, j, base):
        q = j % 2
        xfar = [("xfa", q, fc, sub) for fc in range(FC) for sub in range(NSUB)]
        for k in range(8):
            oc = DC + base + k
            p = oc % 2
            wb = self.WSTR[:, p * 1024:p * 1024 + 1024]
            self.dma("sync", wb, self.win_p[oc], f"wb{p}",
                     r=[("win_p", jj) for jj in range(4)], w=[("wbd", p)])
            pb = self.PS[p]
            for fc in range(FC):
                self.mm(pb[:], wb[:, fc * 128:(fc + 1) * 128], self.axfa(q, fc),
                        start=(fc == 0), stop=(fc == FC - 1),
                        r=[("wbd", p)] + xfar, w=[("ps", p)])
            self.act(self.AGR[k], pb[:], AF.Silu,
                     r=[("ps", p)], w=[("gr", k), ("ps", p)])

    def gateA(self, j, base):
        q = j % 2
        for k in range(8):
            dc = base + k
            p = dc % 2
            self.stt(self.ATMP[p], self.autile(q, dc)[:, UOF:UOF + 512],
                     self.DCOL[:, dc:dc + 1], self.aytile(dc), ALU.mult, ALU.add,
                     r=[("u", q, dc), ("dcol",), ("y", dc)], w=[("tmp", p)])
            self.tt(self.autile(q, dc)[:, UOF:UOF + 512], self.ATMP[p],
                    self.AGR[k], ALU.mult,
                    r=[("tmp", p), ("gr", k), ("u", q, dc)], w=[("u", q, dc)])

    def prefX0(self, j):
        s, c = j % 2, j // 2
        bl = []
        for sub in range(NSUB):
            g = c * NSUB + sub
            b = self.x0cnt % 4
            self.x0cnt += 1
            self.dma("sync", self.AX0[b][:], self.x_in[s, g * 128:(g + 1) * 128, :],
                     f"x0{b}", w=[("x0", b)])
            bl.append(b)
        self.rx0[j] = bl

    WOBANK = (0, 1, 6, 7)

    def woutA_chain(self, j, half, d0, d1, stop_end):
        q = j % 2
        for dc in range(d0, d1):
            wsl = dc % 2
            wb = self.WSTR[:, 2048 + wsl * 512:2048 + wsl * 512 + 512]
            self.dma("sync", wb,
                     self.wout_p[dc * 128:(dc + 1) * 128,
                                 half * 512:(half + 1) * 512],
                     f"wo{wsl}",
                     r=[("wout_p", dc // 4)], w=[("wod", wsl)])
            for sub in range(NSUB):
                bk = self.WOBANK[sub]
                self.mm(self.PS[bk][:],
                        self.autile(q, dc)[:, UOF + sub * 128:UOF + (sub + 1) * 128],
                        wb[:, 0:512],
                        start=(dc == 0), stop=(stop_end and dc == d1 - 1),
                        r=[("wod", wsl), ("u", q, dc)], w=[("ps", bk)])

    def woutA_evac(self, j, half):
        for sub in range(NSUB):
            bk = self.WOBANK[sub]
            self.act(self.AXO[sub][:, half * 512:(half + 1) * 512],
                     self.PS[bk][:], AF.Copy,
                     r=[("ps", bk)], w=[("xo", sub), ("ps", bk)])

    def woutA_close(self, j):
        self.woutA_chain(j, 0, 8, 16, True)
        self.woutA_evac(j, 0)
        self.woutA_chain(j, 1, 0, 16, True)
        self.woutA_evac(j, 1)

    def outLN_store(self, j):
        s, c = j % 2, j // 2
        q = j % 2
        st = self.STA
        pf = st["pf"]
        lo = 8 + 4 * q
        if _A_ONLY:
            for sub in range(NSUB):
                g = c * NSUB + sub
                self.dma("gpsimd", self.out[s, g * 128:(g + 1) * 128, :],
                         self.AXO[sub][:], "out", r=[("xo", sub)],
                         w=[("odbg", s, g)])
        for sub in range(NSUB):
            self.ln_stats(st, self.AXO[sub][:], lo + sub, [("xo", sub)])
        self.ln_math(st, lo, NSUB)
        for sub in range(NSUB):
            g = c * NSUB + sub
            b = self.rx0[j][sub]
            for half in range(2):
                hp = half
                self.ts(self.ATMP[hp],
                        self.AXO[sub][:, half * 512:(half + 1) * 512],
                        st["negm"][:, lo + sub:lo + sub + 1],
                        st["rstd"][:, lo + sub:lo + sub + 1], ALU.add, ALU.mult,
                        r=[("xo", sub), (pf, "negm", lo), (pf, "rstd", lo)],
                        w=[("tmp", hp)])
                self.tt(self.AXO[sub][:, half * 512:(half + 1) * 512],
                        self.AX0[b][:, half * 512:(half + 1) * 512],
                        self.ATMP[hp], ALU.add,
                        r=[("x0", b), ("tmp", hp)], w=[("xo", sub)])
            self.dma("gpsimd", self.xres_h[s, g * 128:(g + 1) * 128, :],
                     self.AXO[sub][:], "xw",
                     r=[("xo", sub)], w=[("xrh", s, g)])

    def sweepA(self, j, sh, nxt):
        s = j % 2
        q = j % 2
        d0 = 8 * sh
        for n in range(NST):
            if sh == 0:
                if nxt and j + 1 >= 2:
                    if n == 4:
                        self.frontA_LN(j + 1)
                        self.frontA_dmats(j + 1, (0, 1))
                    elif n == 6:
                        self.frontA_dmats(j + 1, (2, 3))
                    elif n == 8:
                        self.frontA_win(j + 1)
                if n == 11:
                    self.resA(j, 0)
            else:
                if n == 2:
                    self.resA(j, 8)
                if nxt:
                    if n == 4:
                        self.frontB_xdbl(j + 1)
                    elif n == 6:
                        self.frontB_delta(j + 1, 0, 8)
                if n == 5:
                    self.woutA_chain(j, 0, 0, 8, False)
                if n == 8:
                    self.prefX0(j)
            nb = n % 2
            self.mm(self.PS[4][:], self.SELB[64:96, n * 128:(n + 1) * 128],
                    self.AXDB[q][64:96, :], start=True, stop=True,
                    r=[("xdb", q), ("selb",)], w=[("ps", 4)])
            self.act(self.ABBC[nb], self.PS[4][:], AF.Copy,
                     r=[("ps", 4)], w=[("bbc", nb), ("ps", 4)])
            self.mm(self.PS[5][:], self.SELC[64:96, n * 128:(n + 1) * 128],
                    self.AXDB[q][64:96, :], start=True, stop=True,
                    r=[("xdb", q), ("selc",)], w=[("ps", 5)])
            self.vcopy(self.ACBC[nb], self.PS[5][:],
                       r=[("ps", 5)], w=[("cbc", nb), ("ps", 5)])
            for dc in range(d0, d0 + 8):
                p = dc % 2
                a2 = self.a2c % 4
                self.a2c += 1
                col = s * 256 + dc * 16 + n
                acol = dc * 16 + n
                self.act(self.AA2[a2], self.adtile(dc), AF.Exp,
                         scale=self.ANEG[:, acol:acol + 1],
                         r=[("del", dc), ("aneg",)], w=[("a2", a2)])
                self.tt(self.AB2[p], self.awtile(dc), self.ABBC[nb], ALU.mult,
                        r=[("w", dc), ("bbc", nb)], w=[("b2", p)])
                self.scan(self.AH2[p], self.AA2[a2], self.AB2[p],
                          self.HST[:, col:col + 1],
                          r=[("a2", a2), ("b2", p), ("hst", s, n, dc)],
                          w=[("h2", p)])
                self.gcopy(self.HST[:, col:col + 1], self.AH2[p][:, 511:512],
                           r=[("h2", p)], w=[("hst", s, n, dc)])
                if n == 0:
                    self.tt(self.aytile(dc), self.AH2[p], self.ACBC[nb], ALU.mult,
                            r=[("h2", p), ("cbc", nb)], w=[("y", dc)])
                else:
                    self.tt(self.AYC[p], self.AH2[p], self.ACBC[nb], ALU.mult,
                            r=[("h2", p), ("cbc", nb)], w=[("yc", p)])
                    self.tt(self.aytile(dc), self.aytile(dc), self.AYC[p], ALU.add,
                            r=[("yc", p), ("y", dc)], w=[("y", dc)])

    def phaseB_merged(self):
        v = self.view
        BXIN = [v(F32, 1024) for _ in range(6)]
        BXFB = [v(BF16, FC * 512) for _ in range(2)]
        BZ = [v(F32, NSUB * 1024) for _ in range(2)]
        BXH = [v(BF16, 1024) for _ in range(2)]
        BT = [v(BF16, 512) for _ in range(2)]
        BG = [v(BF16, 512) for _ in range(2)]
        BTMP = [v(F32, 512) for _ in range(2)]
        st = self.STB
        pf = st["pf"]
        xfb = lambda qb, fc: BXFB[qb][:, fc * 512:(fc + 1) * 512]

        for i in range(2 * NCH):
            s, c = i % 2, i // 2
            qb = i % 2
            lo = 4 * qb
            loz = 8 + 4 * qb
            tb = [(i * NSUB + sub) % 6 for sub in range(NSUB)]
            for sub in range(NSUB):
                g = c * NSUB + sub
                b = tb[sub]
                self.dma("sync", BXIN[b][:], self.xres_h[s, g * 128:(g + 1) * 128, :],
                         f"bx{b % 3}", r=[("xrh", s, g)], w=[("bxin", b)])
                self.ln_stats(st, BXIN[b][:], lo + sub, [("bxin", b)])
            self.ln_math(st, lo, NSUB)
            for sub in range(NSUB):
                b = tb[sub]
                rx = sub % 2
                self.ts(BXH[rx], BXIN[b][:],
                        st["negm"][:, lo + sub:lo + sub + 1],
                        st["rstd"][:, lo + sub:lo + sub + 1], ALU.add, ALU.mult,
                        r=[("bxin", b), (pf, "negm", lo), (pf, "rstd", lo)],
                        w=[("bxh", rx)])
                for fc in range(FC):
                    self.dmat(xfb(qb, fc)[:, sub * 128:(sub + 1) * 128],
                              BXH[rx][:, fc * 128:(fc + 1) * 128],
                              r=[("bxh", rx)], w=[("xfb", qb, fc, sub)])
            xfbr = [("xfb", qb, fc, sub) for fc in range(FC) for sub in range(NSUB)]

            pt = qb
            for fc in range(FC):
                self.mm(self.PS[pt][:], self.UTS[:, fc * 128:(fc + 1) * 128],
                        xfb(qb, fc), start=(fc == 0), stop=(fc == FC - 1),
                        r=[("uts",)] + xfbr, w=[("ps", pt)])
            self.vcopy(BT[qb], self.PS[pt][:],
                       r=[("ps", pt)], w=[("texp", qb), ("ps", pt)])
            pg = 2 + qb
            for sub in range(NSUB):
                for fc in range(FC):
                    self.mm(self.PS[pg][:, 0:8],
                            xfb(qb, fc)[:, sub * 128:(sub + 1) * 128],
                            self.GATET[:, fc * 8:(fc + 1) * 8],
                            start=(fc == 0), stop=(fc == FC - 1),
                            r=[("gatet",)] + xfbr, w=[("ps", pg)])
                self.rmax(self.NM[:, qb:qb + 1], self.PS[pg][:, 0:8], True,
                          r=[("ps", pg)], w=[("nm", qb)])
                self.act(self.GE[:, 8 * qb:8 * qb + 8], self.PS[pg][:, 0:8], AF.Exp,
                         bias=self.NM[:, qb:qb + 1],
                         r=[("ps", pg), ("nm", qb)], w=[("ge", qb), ("ps", pg)])
                self.rsumf(self.GS[:, 2 * qb:2 * qb + 1], self.GE[:, 8 * qb:8 * qb + 8],
                           r=[("ge", qb)], w=[("gs", qb)])
                self.ts(self.GS[:, 2 * qb + 1:2 * qb + 2], self.GS[:, 2 * qb:2 * qb + 1],
                        16.0, None, ALU.mult, None,
                        r=[("gs", qb)], w=[("gs16", qb)])
                self.recip(self.GS[:, 2 * qb:2 * qb + 1], self.GS[:, 2 * qb + 1:2 * qb + 2],
                           r=[("gs16", qb)], w=[("gs", qb)])
                self.ts(self.GSUB[:, 8 * qb:8 * qb + 8], self.GE[:, 8 * qb:8 * qb + 8],
                        self.GS[:, 2 * qb:2 * qb + 1], None, ALU.mult,
                        None, r=[("ge", qb), ("gs", qb)], w=[("gsub", qb)])
                pgt = self.PS[pg].ap().bitcast(BF16)[0:8, 0:128]
                self.tpose(pgt, self.GSUB[:, 8 * qb:8 * qb + 8],
                           r=[("gsub", qb), ("iden",)], w=[("ps", pg)])
                self.act(self.GFM[:, qb * LC + sub * 128:qb * LC + (sub + 1) * 128],
                         pgt, AF.Copy,
                         r=[("ps", pg)], w=[("gfm", qb, sub), ("ps", pg)])
            self.mm(self.PS[pt][:], self.REP8[:], self.GFM[:, qb * LC:(qb + 1) * LC],
                    start=True, stop=True,
                    r=[("rep8",)] + [("gfm", qb, ii) for ii in range(NSUB)],
                    w=[("ps", pt)])
            self.vcopy(BG[qb], self.PS[pt][:], r=[("ps", pt)], w=[("bg", qb), ("ps", pt)])
            self.tt(BT[qb], BT[qb], BG[qb], ALU.mult,
                    r=[("texp", qb), ("bg", qb)], w=[("texp", qb)])

            for half in range(2):
                for fc in range(FC):
                    p = fc % 2
                    wb = self.WSTR[:, 3072 + p * 512:3072 + p * 512 + 512]
                    self.dma("sync", wb,
                             self.w0t_p[fc * 128:(fc + 1) * 128,
                                        half * 512:(half + 1) * 512],
                             f"wB{p}",
                             r=[("w0t_p", fc // 2)], w=[("wbB", p)])
                    for sub in range(NSUB):
                        self.mm(self.PS[4 + sub][:],
                                xfb(qb, fc)[:, sub * 128:(sub + 1) * 128],
                                wb[:, 0:512],
                                start=(fc == 0), stop=False,
                                r=[("wbB", p)] + xfbr, w=[("ps", 4 + sub)])
                for sub in range(NSUB):
                    self.mm(self.PS[4 + sub][:],
                            BT[qb][:, sub * 128:(sub + 1) * 128],
                            self.VSB[:, half * 512:(half + 1) * 512],
                            start=False, stop=True,
                            r=[("texp", qb), ("vsb",)], w=[("ps", 4 + sub)])
                for sub in range(NSUB):
                    self.act(BZ[qb][:, sub * 1024 + half * 512:
                                    sub * 1024 + (half + 1) * 512],
                             self.PS[4 + sub][:], AF.Copy,
                             r=[("ps", 4 + sub)],
                             w=[("bz", qb, sub), ("ps", 4 + sub)])
            for sub in range(NSUB):
                self.ln_stats(st, BZ[qb][:, sub * 1024:(sub + 1) * 1024], loz + sub,
                              [("bz", qb, sub)])
            self.ln_math(st, loz, NSUB)
            for sub in range(NSUB):
                g = c * NSUB + sub
                b = tb[sub]
                for half in range(2):
                    hp = half
                    self.ts(BTMP[hp], BZ[qb][:, sub * 1024 + half * 512:
                                              sub * 1024 + (half + 1) * 512],
                            st["negm"][:, loz + sub:loz + sub + 1],
                            st["rstd"][:, loz + sub:loz + sub + 1],
                            ALU.add, ALU.mult,
                            r=[("bz", qb, sub), (pf, "negm", loz), (pf, "rstd", loz)],
                            w=[("btmp", hp)])
                    self.tt(BXIN[b][:, half * 512:(half + 1) * 512],
                            BXIN[b][:, half * 512:(half + 1) * 512],
                            BTMP[hp], ALU.add,
                            r=[("btmp", hp)], w=[("bxin", b)])
                self.dma("gpsimd", self.xres_h[s, g * 128:(g + 1) * 128, :],
                         BXIN[b][:], "xw",
                         r=[("bxin", b)], w=[("xrh", s, g)])

    def phaseC(self, s):
        pf = ("C", s)
        st = self.STA
        spf = st["pf"]
        v = self.view
        XHT = v(BF16, TS * D)
        XFC = v(BF16, FC * SEQ)
        ES = v(BF16, TS * 512)
        AO = [v(F32, D) for _ in range(2)]
        XT = [v(BF16, D) for _ in range(2)]
        TMP = [v(F32, 512) for _ in range(2)]
        CXI = [v(F32, 1024) for _ in range(3)]
        CXR = [v(F32, 1024) for _ in range(2)]
        xht = lambda g: XHT[:, g * D:(g + 1) * D]
        xfc = lambda fc: XFC[:, fc * SEQ:(fc + 1) * SEQ]
        es = lambda kc: ES[:, kc * 512:(kc + 1) * 512]
        cx = [0]

        for g in range(TS):
            b = cx[0] % 3
            cx[0] += 1
            self.dma("sync", CXI[b][:], self.xres_h[s, g * 128:(g + 1) * 128, :],
                     f"cx{b}", r=[("xrh", s, g)], w=[("cxi", b)])
            self.ln_stats(st, CXI[b][:], g, [("cxi", b)])
        self.ln_math(st, 0, TS, want_r=True)
        for g in range(TS):
            b = cx[0] % 3
            cx[0] += 1
            p = g % 2
            self.dma("sync", CXI[b][:], self.xres_h[s, g * 128:(g + 1) * 128, :],
                     f"cx{b}", r=[("xrh", s, g)], w=[("cxi", b)])
            self.ts(xht(g), CXI[b][:],
                    st["negm"][:, g:g + 1], st["rstd"][:, g:g + 1],
                    ALU.add, ALU.mult,
                    r=[("cxi", b), (spf, "negm", 0), (spf, "rstd", 0)],
                    w=[(pf, "xht", g)])
            self.ts(XT[p], CXI[b][:],
                    st["negm"][:, g:g + 1], st["aux2"][:, g:g + 1],
                    ALU.add, ALU.mult,
                    r=[("cxi", b), (spf, "negm", 0), (spf, "aux2", 0)],
                    w=[(pf, "xt", p)])
            for fc in range(FC):
                self.dmat(xfc(fc)[:, g * 128:(g + 1) * 128],
                          XT[p][:, fc * 128:(fc + 1) * 128],
                          r=[(pf, "xt", p)], w=[(pf, "xfc", fc, g)])

        for qb in range(NCH):
            q0 = qb * 512
            xfcq = [(pf, "xfc", fc, qb * 4 + i) for fc in range(FC) for i in range(4)]
            for kc in range(TS):
                pq = kc % 2
                for fc in range(FC):
                    self.mm(self.PS[pq][:], xfc(fc)[:, kc * 128:(kc + 1) * 128],
                            xfc(fc)[:, q0:q0 + 512],
                            start=(fc == 0), stop=(fc == FC - 1),
                            r=[(pf, "xfc", fc, kc)] + xfcq, w=[("ps", pq)])
                self.act(es(kc), self.PS[pq][:], AF.Exp,
                         r=[("ps", pq)], w=[(pf, "es", kc), ("ps", pq)])
            for qs in range(NSUB):
                gq = qb * NSUB + qs
                i0, i1 = 2 + 2 * (qs % 2), 3 + 2 * (qs % 2)
                for kc in range(TS):
                    l = es(kc)[:, qs * 128:(qs + 1) * 128]
                    self.mm(self.PS[i0][:], l, xht(kc)[:, 0:512],
                            start=(kc == 0), stop=(kc == TS - 1),
                            r=[(pf, "es", kc), (pf, "xht", kc)], w=[("ps", i0)])
                    self.mm(self.PS[i1][:], l, xht(kc)[:, 512:1024],
                            start=(kc == 0), stop=(kc == TS - 1),
                            r=[(pf, "es", kc), (pf, "xht", kc)], w=[("ps", i1)])
                    self.mm(self.PS[6][:, 0:1], l, self.ONES1[:],
                            start=(kc == 0), stop=(kc == TS - 1),
                            r=[(pf, "es", kc), ("ones1",)], w=[("ps", 6)])
                self.recip(self.RINV[:, qs:qs + 1], self.PS[6][:, 0:1],
                           r=[("ps", 6)], w=[(pf, "rinv", qs), ("ps", 6)])
                pa = qs % 2
                for half, pi in ((0, i0), (1, i1)):
                    self.stt(AO[pa][:, half * 512:(half + 1) * 512],
                             self.PS[pi][:], self.RINV[:, qs:qs + 1],
                             xht(gq)[:, half * 512:(half + 1) * 512],
                             ALU.mult, ALU.add,
                             r=[("ps", pi), (pf, "rinv", qs), (pf, "xht", gq)],
                             w=[(pf, "ao", pa), ("ps", pi)])
                self.ln_stats(st, AO[pa][:], qs, [(pf, "ao", pa)])
                self.ln_math(st, qs, 1)
                b2 = cx[0] % 2
                cx[0] += 1
                self.dma("sync", CXR[b2][:],
                         self.xres_h[s, gq * 128:(gq + 1) * 128, :],
                         f"cr{b2}", r=[("xrh", s, gq)], w=[("cxr", b2)])
                for half in range(2):
                    self.ts(TMP[half], AO[pa][:, half * 512:(half + 1) * 512],
                            st["negm"][:, qs:qs + 1], st["rstd"][:, qs:qs + 1],
                            ALU.add, ALU.mult,
                            r=[(pf, "ao", pa), (spf, "negm", qs), (spf, "rstd", qs)],
                            w=[(pf, "tmp", half)])
                    self.tt(CXR[b2][:, half * 512:(half + 1) * 512],
                            CXR[b2][:, half * 512:(half + 1) * 512],
                            TMP[half], ALU.add,
                            r=[(pf, "tmp", half)], w=[("cxr", b2)])
                self.dma("gpsimd", self.xres_h[s, gq * 128:(gq + 1) * 128, :],
                         CXR[b2][:], "xw",
                         r=[("cxr", b2)], w=[("xrh", s, gq)])

    def phaseD(self, s):
        pf = ("D", s)
        st = self.STA
        spf = st["pf"]
        v = self.view
        XFD = v(BF16, FC * 544)
        XR = v(BF16, FC * 512)
        XK = v(BF16, FC * 512)
        KACT = v(BF16, FC * 512)
        RT = v(BF16, NSUB * D)
        VR = v(F32, NSUB * D)
        WS2 = v(BF16, FC * 512)
        XH = [v(BF16, D) for _ in range(2)]
        DD = v(BF16, 512)
        KR = v(BF16, 512)
        XO = [v(F32, D) for _ in range(2)]
        TMP = [v(F32, 512) for _ in range(2)]
        DXI = [v(F32, 1024) for _ in range(6)]
        xfd3 = XFD.rearrange("p (fc r) -> p fc r", fc=FC)
        xfd = lambda fc: XFD[:, fc * 544:(fc + 1) * 544]
        xr = lambda fc: XR[:, fc * 512:(fc + 1) * 512]
        xk = lambda fc: XK[:, fc * 512:(fc + 1) * 512]
        ka = lambda dcb: KACT[:, dcb * 512:(dcb + 1) * 512]
        dx = [0]

        self.vmemset(xfd3[:, :, 15:16], 0.0, w=[(pf, "bnd")])

        for c in range(NCH):
            xfdr = [(pf, "xfd", fc, sub) for fc in range(FC) for sub in range(NSUB)]
            if c > 0:
                self.vcopy(xfd3[:, :, 15:16], xfd3[:, :, 527:528],
                           r=xfdr, w=[(pf, "bnd")])
            tb = [(c * NSUB + sub) % 6 for sub in range(NSUB)]
            for sub in range(NSUB):
                g = c * NSUB + sub
                b = tb[sub]
                self.dma("sync", DXI[b][:], self.xres_h[s, g * 128:(g + 1) * 128, :],
                         f"dx{b % 3}", r=[("xrh", s, g)], w=[("dxi", b)])
                self.ln_stats(st, DXI[b][:], sub, [("dxi", b)])
            self.ln_math(st, 0, NSUB)
            for sub in range(NSUB):
                g = c * NSUB + sub
                b = tb[sub]
                p = sub % 2
                self.ts(XH[p], DXI[b][:],
                        st["negm"][:, sub:sub + 1], st["rstd"][:, sub:sub + 1],
                        ALU.add, ALU.mult,
                        r=[("dxi", b), (spf, "negm", 0), (spf, "rstd", 0)],
                        w=[(pf, "xh", p)])
                for fc in range(FC):
                    self.dmat(xfd(fc)[:, 16 + sub * 128:16 + (sub + 1) * 128],
                              XH[p][:, fc * 128:(fc + 1) * 128],
                              r=[(pf, "xh", p), (pf, "bnd")],
                              w=[(pf, "xfd", fc, sub)])
            for fc in range(FC):
                cur = xfd(fc)[:, 16:528]
                prv = xfd(fc)[:, 15:527]
                self.tt(DD, cur, prv, ALU.subtract,
                        r=xfdr + [(pf, "bnd")], w=[(pf, "dd")])
                self.stt(xr(fc), DD, self.MUR[:, fc:fc + 1], prv, ALU.mult, ALU.add,
                         r=[(pf, "dd"), ("mur",)] + xfdr + [(pf, "bnd")],
                         w=[(pf, "xr", fc)])
                self.stt(xk(fc), DD, self.MUK[:, fc:fc + 1], prv, ALU.mult, ALU.add,
                         r=[(pf, "dd"), ("muk",)] + xfdr + [(pf, "bnd")],
                         w=[(pf, "xk", fc)])
            xrr = [(pf, "xr", fc) for fc in range(FC)]
            xkr = [(pf, "xk", fc) for fc in range(FC)]
            for dcb in range(FC):
                p = dcb % 2
                wb = self.WSTR[:, p * 1024:p * 1024 + 1024]
                self.dma("sync", wb, self.wkt_p[dcb], f"wb{p}",
                         r=[("wkt_p", j) for j in range(4)], w=[("wbd", p)])
                for fc in range(FC):
                    self.mm(self.PS[p][:], wb[:, fc * 128:(fc + 1) * 128], xk(fc),
                            start=(fc == 0), stop=(fc == FC - 1),
                            r=[("wbd", p)] + xkr, w=[("ps", p)])
                self.act(KR, self.PS[p][:], AF.Relu,
                         r=[("ps", p)], w=[(pf, "kr"), ("ps", p)])
                self.tt(ka(dcb), KR, KR, ALU.mult, r=[(pf, "kr")], w=[(pf, "ka", dcb)])
            for half in range(2):
                for fc in range(FC):
                    self.dma("sync", WS2[:, fc * 512:(fc + 1) * 512],
                             self.wrt_p[fc * 128:(fc + 1) * 128,
                                        half * 512:(half + 1) * 512],
                             "ws2",
                             r=[("wrt_p", fc // 2)], w=[(pf, "ws2d", fc)])
                for sub in range(NSUB):
                    for fc in range(FC):
                        self.mm(self.PS[2 + sub][:],
                                xr(fc)[:, sub * 128:(sub + 1) * 128],
                                WS2[:, fc * 512:(fc + 1) * 512],
                                start=(fc == 0), stop=(fc == FC - 1),
                                r=[(pf, "ws2d", fc)] + xrr, w=[("ps", 2 + sub)])
                    self.act(RT[:, sub * D + half * 512:sub * D + (half + 1) * 512],
                             self.PS[2 + sub][:], AF.Sigmoid,
                             r=[("ps", 2 + sub)],
                             w=[(pf, "rt", sub), ("ps", 2 + sub)])
            for half in range(2):
                for dcb in range(FC):
                    self.dma("sync", WS2[:, dcb * 512:(dcb + 1) * 512],
                             self.wvt_p[dcb * 128:(dcb + 1) * 128,
                                        half * 512:(half + 1) * 512],
                             "ws2",
                             r=[("wvt_p", dcb // 2)], w=[(pf, "ws2d", dcb)])
                for sub in range(NSUB):
                    for dcb in range(FC):
                        self.mm(self.PS[2 + sub][:],
                                ka(dcb)[:, sub * 128:(sub + 1) * 128],
                                WS2[:, dcb * 512:(dcb + 1) * 512],
                                start=(dcb == 0), stop=(dcb == FC - 1),
                                r=[(pf, "ws2d", dcb), (pf, "ka", dcb)],
                                w=[("ps", 2 + sub)])
                    self.tt(VR[:, sub * D + half * 512:sub * D + (half + 1) * 512],
                            self.PS[2 + sub][:],
                            RT[:, sub * D + half * 512:sub * D + (half + 1) * 512],
                            ALU.mult,
                            r=[("ps", 2 + sub), (pf, "rt", sub)],
                            w=[(pf, "vr", sub), ("ps", 2 + sub)])
            for sub in range(NSUB):
                self.ln_stats(st, VR[:, sub * D:(sub + 1) * D], sub,
                              [(pf, "vr", sub)])
            self.ln_math(st, 0, NSUB)
            for sub in range(NSUB):
                g = c * NSUB + sub
                b = tb[sub]
                p = sub % 2
                for half in range(2):
                    self.ts(TMP[half],
                            VR[:, sub * D + half * 512:sub * D + (half + 1) * 512],
                            st["negm"][:, sub:sub + 1], st["rstd"][:, sub:sub + 1],
                            ALU.add, ALU.mult,
                            r=[(pf, "vr", sub), (spf, "negm", 0), (spf, "rstd", 0)],
                            w=[(pf, "tmp", half)])
                    self.tt(XO[p][:, half * 512:(half + 1) * 512],
                            DXI[b][:, half * 512:(half + 1) * 512],
                            TMP[half], ALU.add,
                            r=[(pf, "tmp", half), ("dxi", b)], w=[(pf, "xo", p)])
                self.dma("sync", self.out[s, g * 128:(g + 1) * 128, :], XO[p][:],
                         "out", r=[(pf, "xo", p)], w=[("out", s, g)])


def _layouts(i):
    def col16(v):
        return np.ascontiguousarray(v.reshape(16, 128).T)
    out = {}
    out["conv_w_r"] = np.ascontiguousarray(
        i["L0_attn_conv_w"].reshape(16, 128, 4).transpose(1, 0, 2).reshape(128, 64))
    out["conv_b_r"] = col16(i["L0_attn_conv_b"])
    out["b_dt_r"] = col16(i["L0_attn_b_dt"])
    out["d_r"] = col16(i["L0_attn_D"])
    out["a_log_r"] = np.ascontiguousarray(
        i["L0_attn_A_log"].reshape(16, 128, 16).transpose(1, 0, 2).reshape(128, 256))
    out["w_in"] = np.ascontiguousarray(
        i["L0_attn_W_in"].reshape(8, 128, 32, 128).transpose(2, 1, 0, 3)
        .reshape(32, 128, 1024))
    out["w_x"] = np.ascontiguousarray(
        i["L0_attn_W_x"].reshape(16, 128, 96).transpose(1, 0, 2).reshape(128, 16 * 96))
    out["w_dt"] = np.ascontiguousarray(i["L0_attn_W_dt"])
    out["w_out"] = np.ascontiguousarray(i["L0_attn_W_out"])
    out["w0t"] = np.ascontiguousarray(i["L0_ffn_weight"].T)
    out["gatet"] = np.ascontiguousarray(
        i["L0_ffn_gate_w"].T.reshape(8, 128, 8).transpose(1, 0, 2).reshape(128, 64))
    out["ut"] = np.ascontiguousarray(
        i["L0_ffn_lora_U"].reshape(128, 1024).T.reshape(8, 128, 128)
        .transpose(1, 0, 2).reshape(128, 8 * 128))
    out["v_r"] = np.ascontiguousarray(
        i["L0_ffn_lora_V"].transpose(0, 2, 1).reshape(128, 1024))
    out["wrt"] = np.ascontiguousarray(i["L1_ffn_W_r"].T)
    out["wkt"] = np.ascontiguousarray(
        i["L1_ffn_W_k"].T.reshape(8, 128, 8, 128).transpose(2, 1, 0, 3)
        .reshape(8, 128, 1024))
    out["wvt"] = np.ascontiguousarray(i["L1_ffn_W_v"].T)
    out["mur_r"] = np.ascontiguousarray(i["L1_ffn_mu_r"].reshape(8, 128).T)
    out["muk_r"] = np.ascontiguousarray(i["L1_ffn_mu_k"].reshape(8, 128).T)
    bf = ml_dtypes.bfloat16
    selb = np.zeros((32, NST * 128), np.float32)
    selc = np.zeros((32, NST * 128), np.float32)
    for n in range(NST):
        selb[n, n * 128:(n + 1) * 128] = 1.0
        selc[16 + n, n * 128:(n + 1) * 128] = 1.0
    out["selb"] = selb.astype(bf)
    out["selc"] = selc.astype(bf)
    rep = np.zeros((8, 128), np.float32)
    for e in range(8):
        rep[e, e * 16:(e + 1) * 16] = 1.0
    out["rep8"] = rep.astype(bf)
    out["ones1"] = np.ones((128, 1), np.float32).astype(bf)
    out["iden"] = np.eye(128, dtype=np.float32).astype(bf)
    return out


_NC_CACHE = None


_BUILDER = None


def _get_nc():
    global _NC_CACHE, _BUILDER
    if _NC_CACHE is None:
        _BUILDER = Builder()
        _NC_CACHE = _BUILDER.build()
    return _NC_CACHE


def kernel(**inputs):
    x = np.ascontiguousarray(np.asarray(inputs["L0_attn_x"], dtype=np.float32))
    shared = _layouts({k: np.asarray(v, dtype=np.float32)
                       for k, v in inputs.items() if k != "L0_attn_x"})
    nc = _get_nc()
    in_maps = []
    for i in range(8):
        m = dict(shared)
        m["x"] = np.ascontiguousarray(x[2 * i:2 * i + 2])
        in_maps.append(m)
    res = run_bass_kernel_spmd(nc, in_maps, core_ids=list(range(8)))
    out = np.stack([r["out"] for r in res.results], axis=0)
    return out.reshape(16, SEQ, D).astype(np.float32)
